# version 06
# baseline (speedup 1.0000x reference)
import numpy as np
import ml_dtypes
from contextlib import ExitStack

import concourse.bass as bass
import concourse.mybir as mybir
from concourse.bass_utils import run_bass_kernel_spmd

F32 = mybir.dt.float32
BF16 = mybir.dt.bfloat16
AF = mybir.ActivationFunctionType
ALU = mybir.AluOpType
AX = mybir.AxisListType

T = 4096
D = 1024
NT = T // 128
ND = D // 128
DCD = 4096
NEXP = 4096
PEER_H = 4
F1 = 4096
E1 = 8
NEG = -1.0e30


def _rope_tables():
    inv_freq = (1.0 / (500000.0 ** (np.arange(0, 64, 2, dtype=np.float32) / 64))).astype(np.float32)
    t = np.arange(T, dtype=np.float32)
    freqs = (t[:, None] * inv_freq[None, :]).astype(np.float32)
    return np.cos(freqs).astype(np.float32), np.sin(freqs).astype(np.float32)


def _diag_masks():
    kk = np.arange(128)[:, None]
    qq = np.arange(512)[None, :]
    ms = [np.where(qq >= o * 128 + kk, 0.0, NEG).astype(np.float32) for o in range(4)]
    ms.append(np.full((128, 512), NEG, np.float32))
    return np.stack(ms)


class Ctrs:
    def __init__(self, ctx, nc, names):
        self.sem = {}
        self.v = {}
        for n in names:
            self.sem[n] = ctx.enter_context(nc.semaphore(n))
            self.v[n] = 0

    def inc(self, instr, name, n=1):
        if name in ("dve", "act"):
            instr._wait_ge(self.sem[name], self.v[name])
        instr.then_inc(self.sem[name], n)
        self.v[name] += n
        return instr

    def wait(self, eng, name, val=None):
        eng.wait_ge(self.sem[name], self.v[name] if val is None else val)

    def snap(self):
        return dict(self.v)


def build_program(stop_after=None, dbg=None, single_core_debug=False, sparse=False):
    nc = bass.Bass("TRN2", target_bir_lowering=False, debug=False, dynamic_dma_scratch_size=16384)

    inp = {}
    def ein(name, shape):
        inp[name] = nc.dram_tensor(name, list(shape), F32, kind="ExternalInput").ap()
        return inp[name]

    x0 = ein("x", (T, D))
    Wq0 = ein("L0_attn_q_heads", (D, D))
    Wk0 = ein("L0_attn_k_heads", (D, D))
    Wv0 = ein("L0_attn_v_heads", (D, D))
    Wo0 = ein("L0_attn_output_proj", (D, D))
    Wg0 = ein("L0_ffn_Wg", (D, DCD))
    bg0 = ein("L0_ffn_bg", (DCD,))
    Wu0 = ein("L0_ffn_Wu", (D, DCD))
    bu0 = ein("L0_ffn_bu", (DCD,))
    Wd0 = ein("L0_ffn_Wd", (DCD, D))
    bd0 = ein("L0_ffn_bd", (D,))
    Wqp = ein("L0_ffn_Wq", (D, 512))
    keys0 = ein("L0_ffn_keys", (4, 64, 2, 64))
    dem0 = ein("L0_ffn_down_embed", (NEXP, D))
    uem0 = ein("L0_ffn_up_embed", (NEXP, D))
    Wqkv1 = ein("L1_attn_W_qkv", (D, 3 * D))
    Wout1 = ein("L1_attn_W_out", (D, D))
    Wg1 = ein("L1_ffn_Wg", (E1, D))
    W11 = ein("L1_ffn_W1", (E1, D, F1))
    W21 = ein("L1_ffn_W2", (E1, F1, D))

    out = nc.dram_tensor("out", [T, D], F32, kind="ExternalOutput").ap()

    def scratch(name, shape, dtype):
        return nc.dram_tensor(name, list(shape), dtype, kind="Internal").ap()

    xn_d = scratch("xn_d", (T, D), BF16)
    wcast_d = scratch("wcast_d", (DCD, D), BF16)
    WTq_d = scratch("WTq_d", (D, D), BF16)
    WTk_d = scratch("WTk_d", (D, D), BF16)
    WTv_d = scratch("WTv_d", (D, D), BF16)
    WTo_d = scratch("WTo_d", (D, D), BF16)
    deT_d = scratch("deT_d", (D, NEXP), BF16)
    wgb_d = scratch("wgb_d", (D, DCD), BF16)
    wub_d = scratch("wub_d", (D, DCD), BF16)
    wdb_d = scratch("wdb_d", (DCD, D), BF16)
    wall_d = scratch("wall_d", (T, NEXP), BF16)
    acc_d = scratch("acc_d", (T, D), F32)
    x1_d = scratch("x1_d", (T, D), F32)
    aoT_d = scratch("aoT_d", (D, T), BF16)
    qfT_d = scratch("qfT_d", (D, T), BF16)
    w2b_d = scratch("w2b_d", (E1, F1, D), BF16)
    CAP = 1536
    cx_d = scratch("cx_d", (E1 * CAP, D), BF16)
    cmeta_d = scratch("cmeta_d", (E1 * CAP, 8), F32)
    pos_d = scratch("pos_d", (8, T), F32)
    ysc_d = scratch("ysc_d", (2 * T, D), F32)

    dbg_map = {"xn_d": xn_d, "acc_d": acc_d, "x1_d": x1_d, "aoT_d": aoT_d,
               "wall_d": wall_d, "qfT_d": qfT_d, "WTq_d": WTq_d, "deT_d": deT_d,
               "WTo_d": WTo_d, "ysc_d": ysc_d}
    dbg_out = None
    if dbg is not None:
        dbg_out = nc.dram_tensor("dbg", list(dbg_map[dbg].shape), F32,
                                 kind="ExternalOutput").ap()

    cos_np, sin_np = _rope_tables()
    cosb = nc.inline_tensor(cos_np.astype(ml_dtypes.bfloat16), "cosb").ap()
    sinb = nc.inline_tensor(sin_np.astype(ml_dtypes.bfloat16), "sinb").ap()
    msinb = nc.inline_tensor((-sin_np).astype(ml_dtypes.bfloat16), "msinb").ap()
    idb_c = nc.inline_tensor(np.eye(128, dtype=np.float32).astype(ml_dtypes.bfloat16), "idb").ap()
    idf_c = nc.inline_tensor(np.eye(128, dtype=np.float32), "idf").ap()
    onesb_c = nc.inline_tensor(np.ones((128, 2), np.float32).astype(ml_dtypes.bfloat16), "onesb").ap()
    onesrb_c = nc.inline_tensor(np.ones((1, 128), np.float32).astype(ml_dtypes.bfloat16), "onesrb").ap()
    onesrf_c = nc.inline_tensor(np.ones((1, 128), np.float32), "onesrf").ap()
    dmask_c = nc.inline_tensor(_diag_masks(), "dmask").ap()
    epsc_c = nc.inline_tensor(np.tile(np.array([[1e-5, 1e-6]], np.float32), (128, 1)), "epsc").ap()
    tid_np = (np.arange(NT)[None, :] * 128 + np.arange(128)[:, None]).astype(np.float32)
    tid_c = nc.inline_tensor(tid_np, "tid_c").ap()

    ctx = ExitStack()
    with ctx:
        c = Ctrs(ctx, nc, ["ld", "ldA", "ldB", "ldC", "ldD", "ldE", "ldF", "g", "gA", "gB", "gC", "gD", "gE", "gF", "gG", "gH", "gI", "gJ", "gK", "gL", "gM", "gN", "g1", "g2", "pe", "dve", "act", "st", "stA", "stB", "stC", "stD"])
        sem = c.sem

        def sb(name, shape, dtype):
            return ctx.enter_context(nc.sbuf_tensor(name, list(shape), dtype))

        _uid = [0]
        def usb(pc, name, shape, dtype):
            _uid[0] += 1
            return pc.enter_context(nc.sbuf_tensor(f"{name}_{_uid[0]}", list(shape), dtype))

        keysT_sb = sb("keysT_sb", (128, 8, 64), BF16)
        wgT_sb = sb("wgT_sb", (128, ND, 8), BF16)
        id_sb = sb("id_sb", (128, 128), BF16)
        idf_sb = sb("idf_sb", (128, 128), F32)
        ones_sb = sb("ones_sb", (128, 2), BF16)
        onesr_sb = sb("onesr_sb", (1, 128), BF16)
        onesrf_sb = sb("onesrf_sb", (1, 128), F32)
        cos_sb = sb("cos_sb", (128, NT, 32), BF16)
        sin_sb = sb("sin_sb", (128, NT, 32), BF16)
        msin_sb = sb("msin_sb", (128, NT, 32), BF16)
        dm_sb = sb("dm_sb", (128, 5, 512), F32)
        bdb_sb = sb("bdb_sb", (128, ND, 128), F32)
        bg_sb = sb("bg_sb", (128, 32), F32)
        eps_sb = sb("eps_sb", (128, 2), F32)
        tid_sb = sb("tid_sb", (128, NT), F32)
        bu_sb = sb("bu_sb", (128, 32), F32)

        pq = [ctx.enter_context(nc.psum_tensor(f"pq{i}", [128, 1024], F32)) for i in range(2)]
        pb45 = [ctx.enter_context(nc.psum_tensor(f"pb{i}", [128, 512], F32)) for i in range(2)]
        pbt = [ctx.enter_context(nc.psum_tensor(f"pbt{i}", [128, 1024], BF16)) for i in range(2)]
        pb = [pq[0].ap()[:, 0:512], pq[0].ap()[:, 512:1024],
              pq[1].ap()[:, 0:512], pq[1].ap()[:, 512:1024],
              pb45[0].ap(), pb45[1].ap()]

        block = ctx.enter_context(nc.Block())

        _last_bar = [None]

        def barrier():
            cur = c.snap()
            prev = _last_bar[0]
            changed = [n for n in cur if prev is None or cur[n] != prev[n]]
            def mk(e, changed=changed, cur=cur):
                for n in changed:
                    e.wait_ge(sem[n], cur[n])
            block.tensor(mk)
            block.vector(mk)
            block.scalar(mk)
            block.sync(mk)
            block.gpsimd(mk)
            _last_bar[0] = cur

        with ExitStack() as s0:
            tstage = s0.enter_context(nc.sbuf_tensor("tstage", [128, DCD], BF16))
            kload = s0.enter_context(nc.sbuf_tensor("kload", [64, 8, 64], BF16))
            wgld = s0.enter_context(nc.sbuf_tensor("wgld", [8, D], BF16))
            bsm = s0.enter_context(nc.sbuf_tensor("bsm", [32, 128], F32))
            brow = s0.enter_context(nc.sbuf_tensor("brow", [1, D], F32))

            @block.sync
            def _(sy):
                for dst, src in [(id_sb[:], idb_c), (idf_sb[:], idf_c),
                                 (ones_sb[:], onesb_c), (onesr_sb[:], onesrb_c),
                                 (onesrf_sb[:], onesrf_c)]:
                    c.inc(sy.dma_start(out=dst, in_=src), "ld", 16)
                c.inc(sy.dma_start(out=cos_sb[:], in_=cosb.rearrange("(a p) b -> p a b", p=128)), "ld", 16)
                c.inc(sy.dma_start(out=sin_sb[:], in_=sinb.rearrange("(a p) b -> p a b", p=128)), "ld", 16)
                c.inc(sy.dma_start(out=msin_sb[:], in_=msinb.rearrange("(a p) b -> p a b", p=128)), "ld", 16)
                c.inc(sy.dma_start(out=dm_sb[:], in_=dmask_c.rearrange("a p b -> p a b")), "ld", 16)
                c.inc(sy.dma_start(out=bsm[:], in_=bg0.rearrange("(a b) -> a b", b=128)), "ld", 16)
                c.inc(sy.dma_start(out=brow[:], in_=bd0.rearrange("(a b) -> a b", a=1)), "ld", 16)
                c.inc(sy.dma_start(out=eps_sb[:], in_=epsc_c), "ld", 16)
                c.inc(sy.dma_start(out=tid_sb[:], in_=tid_c), "ld", 16)

            for wi, (w_ap, wt_ap, R) in enumerate(
                    [(Wq0, WTq_d, D), (Wk0, WTk_d, D), (Wv0, WTv_d, D),
                     (Wo0, WTo_d, D), (dem0, deT_d, NEXP)]):
                @block.gpsimd
                def _(g, w_ap=w_ap, R=R):
                    c.wait(g, "ld")
                    c.inc(g.dma_start(out=wcast_d[0:R, :].rearrange("(a p) b -> p a b", p=128),
                                      in_=w_ap.rearrange("(a p) b -> p a b", p=128)), "g", 16)
                @block.sync
                def _(sy, wt_ap=wt_ap, R=R):
                    c.wait(sy, "g")
                    for ch in range(ND):
                        c.inc(sy.dma_start(out=tstage[:, 0:R], in_=wcast_d[0:R, ch * 128:(ch + 1) * 128],
                                           transpose=True), "ld", 16)
                        c.wait(sy, "ld")
                        c.inc(sy.dma_start(out=wt_ap[ch * 128:(ch + 1) * 128, :], in_=tstage[:, 0:R]), "st", 16)
                        c.wait(sy, "st")

            @block.gpsimd
            def _(g):
                for h in range(4):
                    for p in range(2):
                        c.inc(g.dma_start(out=kload[:, h * 2 + p, :], in_=keys0[h, :, p, :]), "g2", 16)
                c.inc(g.dma_start(out=wgld[:], in_=Wg1), "g2", 16)
                for e in range(E1):
                    c.inc(g.dma_start(out=w2b_d[e].rearrange("(a p) b -> p a b", p=128),
                                      in_=W21[e].rearrange("(a p) b -> p a b", p=128)), "g1", 16)
                for dst, src in [(wgb_d, Wg0), (wub_d, Wu0), (wdb_d, Wd0)]:
                    c.inc(g.dma_start(out=dst.rearrange("(a p) b -> p a b", p=128),
                                      in_=src.rearrange("(a p) b -> p a b", p=128)), "g1", 16)

            b = c.snap()
            @block.tensor
            def _(t, b=b):
                c.wait(t, "g2")
                c.wait(t, "ld")
                for i in range(8):
                    c.inc(t.transpose(pbt[0][0:64, i * 64:(i + 1) * 64], kload[:, i, :],
                                      id_sb[0:64, 0:64]), "pe")
                for dc in range(ND):
                    c.inc(t.transpose(pbt[1][0:128, dc * 8:(dc + 1) * 8],
                                      wgld[0:8, dc * 128:(dc + 1) * 128], id_sb[0:8, 0:8]), "pe")
                c.inc(t.transpose(pb[2][0:128, 0:32], bsm[:], idf_sb[0:32, 0:32]), "pe")
                for half in range(2):
                    if half == 1:
                        t.wait_ge(sem["dve"], b["dve"] + 29)
                    c.inc(t.matmul(pb[3][:, 0:512], onesrf_sb[:],
                                   brow[:, half * 512:(half + 1) * 512], start=True, stop=True), "pe")

            @block.vector
            def _(v, b=b):
                v.wait_ge(sem["pe"], b["pe"] + 8)
                for i in range(8):
                    c.inc(v.tensor_copy(keysT_sb[0:64, i, :], pbt[0][0:64, i * 64:(i + 1) * 64]), "dve")
                    c.inc(v.tensor_copy(keysT_sb[64:128, i, :], pbt[0][0:64, i * 64:(i + 1) * 64]), "dve")
                v.wait_ge(sem["pe"], b["pe"] + 16)
                for dc in range(ND):
                    c.inc(v.tensor_copy(wgT_sb[:, dc, :], pbt[1][0:128, dc * 8:(dc + 1) * 8]), "dve")
                v.wait_ge(sem["pe"], b["pe"] + 17)
                c.inc(v.tensor_copy(bg_sb[:], pb[2][0:128, 0:32]), "dve")
                v.wait_ge(sem["pe"], b["pe"] + 18)
                for j in range(4):
                    c.inc(v.tensor_copy(bdb_sb[:, j, :], pb[3][:, j * 128:(j + 1) * 128]), "dve")
                v.wait_ge(sem["pe"], b["pe"] + 19)
                for j in range(4):
                    c.inc(v.tensor_copy(bdb_sb[:, 4 + j, :], pb[3][:, j * 128:(j + 1) * 128]), "dve")
            barrier()

            b = c.snap()
            @block.sync
            def _(sy, b=b):
                c.inc(sy.dma_start(out=bsm[:], in_=bu0.rearrange("(a b) -> a b", b=128)), "ld", 16)
            @block.tensor
            def _(t, b=b):
                t.wait_ge(sem["ld"], b["ld"] + 16)
                c.inc(t.transpose(pb[2][0:128, 0:32], bsm[:], idf_sb[0:32, 0:32]), "pe")
            @block.vector
            def _(v, b=b):
                v.wait_ge(sem["pe"], b["pe"] + 1)
                c.inc(v.tensor_copy(bu_sb[:], pb[2][0:128, 0:32]), "dve")
            barrier()

        if stop_after == "S0":
            _finish(nc, block, c, dbg, dbg_map, dbg_out)
            return nc

        def emit_ln(x_ap, xn_dst):
            with ExitStack() as pc:
                xt = [usb(pc, f"ln_xt{i}", [128, D], F32) for i in range(2)]
                xc = [usb(pc, f"ln_xc{i}", [128, D], F32) for i in range(2)]
                xnb = [usb(pc, f"ln_xn{i}", [128, D], BF16) for i in range(2)]
                sq = usb(pc, "ln_sq", [128, D], F32)
                stt = usb(pc, "ln_st", [128, 8], F32)
                b = c.snap()
                NDV = 5

                @block.sync
                def _(sy, b=b):
                    for u in range(NT):
                        ldp = "ldA" if u % 2 == 0 else "ldB"
                        if u >= 2:
                            sy.wait_ge(sem["dve"], b["dve"] + NDV * (u - 2) + 3)
                        c.inc(sy.dma_start(out=xt[u % 2][:], in_=x_ap[u * 128:(u + 1) * 128, :]), ldp, 16)
                        if u >= 1:
                            stp = "stA" if (u - 1) % 2 == 0 else "stB"
                            sy.wait_ge(sem["dve"], b["dve"] + NDV * u)
                            c.inc(sy.dma_start(out=xn_dst[(u - 1) * 128:u * 128, :], in_=xnb[(u - 1) % 2][:]), stp, 16)
                    stp = "stA" if (NT - 1) % 2 == 0 else "stB"
                    sy.wait_ge(sem["dve"], b["dve"] + NDV * NT)
                    c.inc(sy.dma_start(out=xn_dst[(NT - 1) * 128:NT * 128, :], in_=xnb[(NT - 1) % 2][:]), stp, 16)
                    c.wait(sy, "stA")
                    c.wait(sy, "stB")

                @block.vector
                def _(v, b=b):
                    for u in range(NT):
                        ldp = "ldA" if u % 2 == 0 else "ldB"
                        v.wait_ge(sem[ldp], b[ldp] + 16 * (u // 2 + 1))
                        c.inc(v.reduce_sum(stt[:, 0:1], xt[u % 2][:], axis=AX.X), "dve")
                        c.inc(v.tensor_scalar(stt[:, 1:2], stt[:, 0:1], -1.0 / D, None, ALU.mult), "dve")
                        c.inc(v.tensor_scalar(xc[u % 2][:], xt[u % 2][:], stt[:, 1:2], None, ALU.add), "dve")
                        v.wait_ge(sem["act"], b["act"] + 2 * (u + 1))
                        c.inc(v.reciprocal(stt[:, 4:5], stt[:, 3:4]), "dve")
                        if u >= 2:
                            stp = "stA" if u % 2 == 0 else "stB"
                            v.wait_ge(sem[stp], b[stp] + 16 * (u // 2))
                        c.inc(v.tensor_scalar(xnb[u % 2][:], xc[u % 2][:], stt[:, 4:5], None, ALU.mult), "dve")

                @block.scalar
                def _(s, b=b):
                    for u in range(NT):
                        s.wait_ge(sem["dve"], b["dve"] + NDV * u + 3)
                        c.inc(s.activation(out=sq[:], in_=xc[u % 2][:], func=AF.Square,
                                           accum_out=stt[:, 2:3]), "act")
                        c.inc(s.activation(out=stt[:, 3:4], in_=stt[:, 2:3], func=AF.Sqrt,
                                           bias=eps_sb[:, 0:1], scale=1.0 / D), "act")
                barrier()

        def emit_xnT(xn_src, xnT_sb):
            b = c.snap()
            @block.sync
            def _(sy, b=b):
                for ch in range(ND):
                    c.inc(sy.dma_start(out=xnT_sb[:, ch, :], in_=xn_src[:, ch * 128:(ch + 1) * 128],
                                       transpose=True), "ld", 16)
                c.wait(sy, "ld")
            barrier()

        def emit_outproj(w_src_ap, w_pretransposed, dst_ap):
            with ExitStack() as pc:
                wsl = usb(pc, "op_w", [128, ND, D], BF16)
                at = [usb(pc, f"op_a{i}", [128, ND, 128], BF16) for i in range(2)]
                ot = [usb(pc, f"op_o{i}", [128, 512], F32) for i in range(4)]
                b = c.snap()

                if w_pretransposed:
                    @block.sync
                    def _(sy, b=b):
                        c.inc(sy.dma_start(out=wsl[:], in_=w_src_ap.rearrange("(a p) b -> p a b", p=128)), "ld", 16)
                else:
                    @block.gpsimd
                    def _(g, b=b):
                        c.inc(g.dma_start(out=wsl[:], in_=w_src_ap.rearrange("(a p) b -> p a b", p=128)), "g", 16)

                wofs = 16 if w_pretransposed else 0

                @block.sync
                def _(sy, b=b):
                    for u in range(NT):
                        ldp = "ldA" if u % 2 == 0 else "ldB"
                        if u >= 2:
                            sy.wait_ge(sem["pe"], b["pe"] + 16 * (u - 1))
                        c.inc(sy.dma_start(
                            out=at[u % 2][:],
                            in_=aoT_d[:, u * 128:(u + 1) * 128].rearrange("(a p) t -> p a t", p=128)),
                            ldp, 16)

                @block.tensor
                def _(t, b=b):
                    if not w_pretransposed:
                        t.wait_ge(sem["g"], b["g"] + 16)
                    else:
                        t.wait_ge(sem["ld"], b["ld"] + 16)
                    for u in range(NT):
                        ldp = "ldA" if u % 2 == 0 else "ldB"
                        t.wait_ge(sem[ldp], b[ldp] + 16 * (u // 2 + 1))
                        for fc in range(2):
                            k = 2 * u + fc
                            if k >= 4:
                                t.wait_ge(sem["dve"], b["dve"] + k - 3)
                            for vc in range(ND):
                                c.inc(t.matmul(pb[k % 4][:, 0:512], at[u % 2][:, vc, :],
                                               wsl[:, vc, fc * 512:(fc + 1) * 512],
                                               start=(vc == 0), stop=(vc == ND - 1)), "pe")

                STP = ["gA", "gB", "gC", "gD"]

                @block.vector
                def _(v, b=b):
                    for u in range(NT):
                        for fc in range(2):
                            k = 2 * u + fc
                            v.wait_ge(sem["pe"], b["pe"] + 8 * (k + 1))
                            if k >= 4:
                                v.wait_ge(sem[STP[k % 4]], b[STP[k % 4]] + 16 * (k // 4))
                            c.inc(v.tensor_copy(ot[k % 4][:], pb[k % 4][:, 0:512]), "dve")

                @block.gpsimd
                def _(g, b=b):
                    for u in range(NT):
                        for fc in range(2):
                            k = 2 * u + fc
                            g.wait_ge(sem["dve"], b["dve"] + k + 1)
                            c.inc(g.dma_start(out=dst_ap[u * 128:(u + 1) * 128, fc * 512:(fc + 1) * 512],
                                              in_=ot[k % 4][:]), STP[k % 4], 16)
                    for n in STP:
                        c.wait(g, n)
                barrier()

        def emit_l0_attn(xnT_sb):
            with ExitStack() as pc:
                wt_sb = usb(pc, "a0_wt", [128, ND, 128], BF16)
                qtA = usb(pc, "a0_qA", [65, T], BF16)
                qtB = usb(pc, "a0_qB", [65, T], BF16)
                ktA = usb(pc, "a0_kA", [65, T], BF16)
                ktB = usb(pc, "a0_kB", [65, T], BF16)
                vtA = usb(pc, "a0_vA", [128, NT, 65], BF16)
                vtB = usb(pc, "a0_vB", [128, NT, 65], BF16)
                ptil = [usb(pc, f"a0_p{i}", [128, 1024], BF16) for i in range(2)]
                sqs = usb(pc, "a0_sq", [64, 512], BF16)
                kmax = usb(pc, "a0_km", [1, 4], F32)
                qnr = usb(pc, "a0_qn", [1, 512], F32)
                rr = usb(pc, "a0_rr", [1, 512], F32)
                rrb = usb(pc, "a0_rrb", [1, 512], BF16)
                rsb = usb(pc, "a0_rsb", [64, 512], F32)
                aop = usb(pc, "a0_ao", [128, T], BF16)

                for pair in range(8):
                    b = c.snap()
                    @block.sync
                    def _(sy, pair=pair, b=b):
                        for wi, wt_d in enumerate([WTq_d, WTk_d, WTv_d]):
                            if wi > 0:
                                sy.wait_ge(sem["pe"], b["pe"] + 64 * wi)
                            c.inc(sy.dma_start(
                                out=wt_sb[:],
                                in_=wt_d[:, pair * 128:(pair + 1) * 128].rearrange("(a p) f -> p a f", p=128)),
                                "ld", 16)

                    @block.tensor
                    def _(t, pair=pair, b=b):
                        for u in range(16):
                            wi, sp = divmod(u, 8)
                            t.wait_ge(sem["ld"], b["ld"] + 16 * (wi + 1))
                            if u >= 2:
                                t.wait_ge(sem["dve"], b["dve"] + 2 * (u - 1))
                            for dc in range(ND):
                                c.inc(t.matmul(pb[u % 2][:, 0:512], wt_sb[:, dc, :],
                                               xnT_sb[:, dc, sp * 512:(sp + 1) * 512],
                                               start=(dc == 0), stop=(dc == ND - 1)), "pe")
                        for tt in range(NT):
                            u = 16 + tt
                            t.wait_ge(sem["ld"], b["ld"] + 48)
                            t.wait_ge(sem["dve"], b["dve"] + 2 * (u - 1))
                            for dc in range(ND):
                                c.inc(t.matmul(pb[u % 2][:, 0:128], xnT_sb[:, dc, tt * 128:(tt + 1) * 128],
                                               wt_sb[:, dc, :],
                                               start=(dc == 0), stop=(dc == ND - 1)), "pe")

                    @block.vector
                    def _(v, pair=pair, b=b):
                        for u in range(16):
                            wi, sp = divmod(u, 8)
                            dA = qtA if wi == 0 else ktA
                            dB = qtB if wi == 0 else ktB
                            v.wait_ge(sem["pe"], b["pe"] + 8 * (u + 1))
                            c.inc(v.tensor_copy(dA[0:64, sp * 512:(sp + 1) * 512], pb[u % 2][0:64, 0:512]), "dve")
                            c.inc(v.tensor_copy(dB[0:64, sp * 512:(sp + 1) * 512], pb[u % 2][64:128, 0:512]), "dve")
                        for tt in range(NT):
                            u = 16 + tt
                            v.wait_ge(sem["pe"], b["pe"] + 8 * (u + 1))
                            c.inc(v.tensor_copy(vtA[:, tt, 0:64], pb[u % 2][:, 0:64]), "dve")
                            c.inc(v.tensor_copy(vtB[:, tt, 0:64], pb[u % 2][:, 64:128]), "dve")
                        c.inc(v.memset(vtA[:, :, 64:65], 1.0), "dve")
                        c.inc(v.memset(vtB[:, :, 64:65], 1.0), "dve")
                        c.inc(v.memset(ktA[64:65, :], 1.0), "dve")
                        c.inc(v.memset(ktB[64:65, :], 1.0), "dve")
                    barrier()

                    for hh in range(2):
                        head = pair * 2 + hh
                        QT = qtA if hh == 0 else qtB
                        KT = ktA if hh == 0 else ktB
                        VT = vtA if hh == 0 else vtB

                        b = c.snap()
                        @block.scalar
                        def _(s, KT=KT, b=b):
                            for sp in range(8):
                                if sp >= 1:
                                    s.wait_ge(sem["pe"], b["pe"] + sp)
                                c.inc(s.activation(out=sqs[:], in_=KT[0:64, sp * 512:(sp + 1) * 512],
                                                   func=AF.Square), "act")
                        @block.tensor
                        def _(t, b=b):
                            for sp in range(8):
                                t.wait_ge(sem["act"], b["act"] + sp + 1)
                                if sp >= 1:
                                    t.wait_ge(sem["dve"], b["dve"] + 3 * (sp - 1) + 1)
                                c.inc(t.matmul(pb[5][0:1, 0:512], ones_sb[0:64, 0:1], sqs[:],
                                               start=True, stop=True), "pe")
                        @block.vector
                        def _(v, b=b):
                            for sp in range(8):
                                v.wait_ge(sem["pe"], b["pe"] + sp + 1)
                                c.inc(v.tensor_copy(qnr[:], pb[5][0:1, 0:512]), "dve")
                                c.inc(v.reduce_max(kmax[:, 2:3], qnr[:], axis=AX.X), "dve")
                                if sp == 0:
                                    c.inc(v.tensor_copy(kmax[:, 0:1], kmax[:, 2:3]), "dve")
                                else:
                                    c.inc(v.tensor_tensor(kmax[:, 0:1], kmax[:, 0:1], kmax[:, 2:3],
                                                          ALU.max), "dve")
                        @block.scalar
                        def _(s, b=b):
                            s.wait_ge(sem["dve"], b["dve"] + 24)
                            c.inc(s.activation(out=kmax[:, 1:2], in_=kmax[:, 0:1], func=AF.Sqrt), "act")
                        barrier()

                        b = c.snap()
                        @block.scalar
                        def _(s, QT=QT, b=b):
                            for sp in range(8):
                                if sp >= 1:
                                    s.wait_ge(sem["pe"], b["pe"] + sp)
                                c.inc(s.activation(out=sqs[:], in_=QT[0:64, sp * 512:(sp + 1) * 512],
                                                   func=AF.Square), "act")
                                s.wait_ge(sem["pe"], b["pe"] + sp + 1)
                                if sp >= 1:
                                    s.wait_ge(sem["dve"], b["dve"] + sp)
                                c.inc(s.activation(out=qnr[:], in_=pb[5][0:1, 0:512], func=AF.Sqrt), "act")
                        @block.tensor
                        def _(t, b=b):
                            for sp in range(8):
                                t.wait_ge(sem["act"], b["act"] + 2 * sp + 1)
                                c.inc(t.matmul(pb[5][0:1, 0:512], ones_sb[0:64, 0:1], sqs[:],
                                               start=True, stop=True), "pe")
                        @block.vector
                        def _(v, QT=QT, b=b):
                            for sp in range(8):
                                v.wait_ge(sem["act"], b["act"] + 2 * (sp + 1))
                                c.inc(v.tensor_scalar(QT[64:65, sp * 512:(sp + 1) * 512], qnr[:],
                                                      kmax[:, 1:2], -0.5, ALU.mult, ALU.mult), "dve")
                        barrier()

                        b = c.snap()
                        units = []
                        for qs2 in range(4):
                            for kb in range(8 * qs2 + 8):
                                units.append((qs2, kb))
                        U = len(units)
                        is_last = [kb == 8 * qs2 + 7 for (qs2, kb) in units]
                        odiag = [kb - 8 * qs2 for (qs2, kb) in units]
                        span_last_idx = {}
                        for i, (qs2, kb) in enumerate(units):
                            if is_last[i]:
                                span_last_idx[qs2] = i
                        pe_ops = []
                        for i in range(U):
                            pe_ops.append(("za", i))
                            pe_ops.append(("zb", i))
                            if i >= 1:
                                pe_ops.append(("ava", i - 1))
                                pe_ops.append(("avb", i - 1))
                                if is_last[i - 1]:
                                    pe_ops.append(("r0", units[i - 1][0]))
                                    pe_ops.append(("r1", units[i - 1][0]))
                        pe_ops += [("ava", U - 1), ("avb", U - 1),
                                   ("r0", units[U - 1][0]), ("r1", units[U - 1][0])]
                        pe_pos = {op: idx + 1 for idx, op in enumerate(pe_ops)}
                        dve_mask_end = {}
                        dve_evac_base = {}
                        d = 0
                        for i, (qs2, kb) in enumerate(units):
                            o = odiag[i]
                            nm = 0 if o < 0 else (1 if o <= 3 else 2)
                            d += nm
                            dve_mask_end[i] = d
                            if is_last[i]:
                                dve_evac_base[qs2] = d
                                d += 8

                        @block.tensor
                        def _(t, QT=QT, KT=KT, VT=VT, b=b, units=units, pe_pos=pe_pos,
                              dve_mask_end=dve_mask_end, dve_evac_base=dve_evac_base,
                              is_last=is_last, span_last_idx=span_last_idx):
                            rev_last = {span_last_idx[s_]: s_ for s_ in span_last_idx}
                            for op in pe_ops:
                                kind, j = op
                                if kind in ("za", "zb"):
                                    i = j
                                    qs2, kb = units[i]
                                    if kind == "za":
                                        if i >= 2:
                                            t.wait_ge(sem["act"], b["act"] + (i - 1))
                                            if (i - 2) in rev_last:
                                                t.wait_ge(sem["dve"], b["dve"] +
                                                          dve_evac_base[rev_last[i - 2]] + 7)
                                        half = pq[i % 2].ap()[:, 0:512]
                                        qsl = QT[:, qs2 * 1024:qs2 * 1024 + 512]
                                    else:
                                        half = pq[i % 2].ap()[:, 512:1024]
                                        qsl = QT[:, qs2 * 1024 + 512:qs2 * 1024 + 1024]
                                    c.inc(t.matmul(half, KT[:, kb * 128:(kb + 1) * 128], qsl,
                                                   start=True, stop=True), "pe")
                                elif kind in ("ava", "avb"):
                                    i = j
                                    qs2, kb = units[i]
                                    t.wait_ge(sem["act"], b["act"] + i + 1)
                                    if kb == 0 and qs2 >= 1 and kind == "ava":
                                        t.wait_ge(sem["dve"], b["dve"] + dve_evac_base[qs2 - 1] + 8)
                                    dst = pb45[0] if kind == "ava" else pb45[1]
                                    rhs = ptil[i % 2][:, 0:512] if kind == "ava" else ptil[i % 2][:, 512:1024]
                                    c.inc(t.matmul(dst.ap()[0:65, 0:512], VT[:, kb, :], rhs,
                                                   start=(kb == 0), stop=is_last[i]), "pe")
                                else:
                                    qs2 = j
                                    il = span_last_idx[qs2]
                                    off = 2 if kind == "r0" else 6
                                    t.wait_ge(sem["dve"], b["dve"] + dve_evac_base[qs2] + off)
                                    c.inc(t.matmul(pq[il % 2].ap()[0:64, 0:512],
                                                   onesr_sb[0:1, 0:64], rrb[:],
                                                   start=True, stop=True), "pe")

                        @block.vector
                        def _(v, b=b, units=units, pe_pos=pe_pos, dve_evac_base=dve_evac_base,
                              is_last=is_last, span_last_idx=span_last_idx, head=head):
                            for i, (qs2, kb) in enumerate(units):
                                o = odiag[i]
                                if o >= 0:
                                    v.wait_ge(sem["pe"], b["pe"] + pe_pos[("zb", i)])
                                    if o <= 3:
                                        c.inc(v.tensor_tensor(pq[i % 2].ap()[:, 0:512],
                                                              pq[i % 2].ap()[:, 0:512],
                                                              dm_sb[:, o, :], ALU.add), "dve")
                                    else:
                                        c.inc(v.tensor_tensor(pq[i % 2].ap()[:, 0:512],
                                                              pq[i % 2].ap()[:, 0:512],
                                                              dm_sb[:, 4, :], ALU.add), "dve")
                                        c.inc(v.tensor_tensor(pq[i % 2].ap()[:, 512:1024],
                                                              pq[i % 2].ap()[:, 512:1024],
                                                              dm_sb[:, o - 4, :], ALU.add), "dve")
                                if is_last[i]:
                                    il = i
                                    r0 = (head % 2) * 64
                                    v.wait_ge(sem["pe"], b["pe"] + pe_pos[("ava", i)])
                                    c.inc(v.reciprocal(rr[:], pb45[0].ap()[64:65, 0:512]), "dve")
                                    c.inc(v.tensor_copy(rrb[:], rr[:]), "dve")
                                    v.wait_ge(sem["pe"], b["pe"] + pe_pos[("r0", qs2)])
                                    c.inc(v.tensor_copy(rsb[:], pq[il % 2].ap()[0:64, 0:512]), "dve")
                                    c.inc(v.tensor_tensor(aop[r0:r0 + 64, qs2 * 1024:qs2 * 1024 + 512],
                                                          pb45[0].ap()[0:64, 0:512], rsb[:], ALU.mult), "dve")
                                    v.wait_ge(sem["pe"], b["pe"] + pe_pos[("avb", i)])
                                    c.inc(v.reciprocal(rr[:], pb45[1].ap()[64:65, 0:512]), "dve")
                                    c.inc(v.tensor_copy(rrb[:], rr[:]), "dve")
                                    v.wait_ge(sem["pe"], b["pe"] + pe_pos[("r1", qs2)])
                                    c.inc(v.tensor_copy(rsb[:], pq[il % 2].ap()[0:64, 0:512]), "dve")
                                    c.inc(v.tensor_tensor(aop[r0:r0 + 64,
                                                              qs2 * 1024 + 512:qs2 * 1024 + 1024],
                                                          pb45[1].ap()[0:64, 0:512], rsb[:], ALU.mult), "dve")

                        @block.scalar
                        def _(s, b=b, units=units, pe_pos=pe_pos, dve_mask_end=dve_mask_end):
                            for i, (qs2, kb) in enumerate(units):
                                s.wait_ge(sem["pe"], b["pe"] + pe_pos[("zb", i)])
                                if odiag[i] >= 0:
                                    s.wait_ge(sem["dve"], b["dve"] + dve_mask_end[i])
                                c.inc(s.activation(out=ptil[i % 2][:], in_=pq[i % 2].ap()[:],
                                                   func=AF.Exp, scale=0.125), "act")
                        barrier()

                    b = c.snap()
                    @block.sync
                    def _(sy, pair=pair, b=b):
                        c.inc(sy.dma_start(out=aoT_d[pair * 128:(pair + 1) * 128, :], in_=aop[:]), "st", 16)
                        c.wait(sy, "st")
                    barrier()

        def emit_l0_swiglu(xnT_sb):
            with ExitStack() as pc:
                wg2 = [usb(pc, f"s2_wg{i}", [128, ND, 256], BF16) for i in range(2)]
                wu2 = [usb(pc, f"s2_wu{i}", [128, ND, 256], BF16) for i in range(2)]
                wd2 = [usb(pc, f"s2_wd{i}", [128, 2, 512], BF16) for i in range(2)]
                gsil = usb(pc, "s2_g", [128, 512], BF16)
                upb = usb(pc, "s2_u", [128, 512], F32)
                prodT = usb(pc, "s2_p", [128, 32, 512], BF16)
                ot = [usb(pc, f"s2_o{i}", [128, 512], F32) for i in range(2)]

                peB_end_prev = [0]
                prev_evac_dve = [0]
                prevA = [None]
                wg_consumed = [0, 0]
                wu_consumed = [0, 0]
                wd_consumed = [0, 0]

                for sp in range(8):
                    bA = c.snap()
                    @block.sync
                    def _(sy, sp=sp, bA=bA):
                        for fp in range(16):
                            k = fp % 2
                            gp = "gA" if k == 0 else "gB"
                            up = "gC" if k == 0 else "gD"
                            if fp < 2:
                                sy.wait_ge(sem["pe"], wg_consumed[k])
                            else:
                                sy.wait_ge(sem["pe"], bA["pe"] + 32 * (fp - 1))
                            c.inc(sy.dma_start(out=wg2[k][:],
                                               in_=wgb_d[:, fp * 256:(fp + 1) * 256].rearrange("(a p) f -> p a f", p=128)), gp, 16)
                            c.inc(sy.dma_start(out=wu2[k][:],
                                               in_=wub_d[:, fp * 256:(fp + 1) * 256].rearrange("(a p) f -> p a f", p=128)), up, 16)

                    @block.tensor
                    def _(t, sp=sp, bA=bA):
                        for fc in range(32):
                            fp = fc // 2
                            half = fc % 2
                            k = fp % 2
                            gp = "gA" if k == 0 else "gB"
                            up = "gC" if k == 0 else "gD"
                            if half == 0:
                                t.wait_ge(sem[gp], bA[gp] + 16 * (fp // 2 + 1))
                                t.wait_ge(sem[up], bA[up] + 16 * (fp // 2 + 1))
                            if fc == 0:
                                if prevA[0] is not None:
                                    t.wait_ge(sem["act"], prevA[0][0] + 32)
                                    t.wait_ge(sem["dve"], prevA[0][1] + 64)
                            else:
                                t.wait_ge(sem["act"], bA["act"] + fc)
                                t.wait_ge(sem["dve"], bA["dve"] + 2 * fc - 1)
                            for dc in range(ND):
                                c.inc(t.matmul(pb[0][:, 0:512], wg2[k][:, dc, half * 128:(half + 1) * 128],
                                               xnT_sb[:, dc, sp * 512:(sp + 1) * 512],
                                               start=(dc == 0), stop=(dc == ND - 1)), "pe")
                            for dc in range(ND):
                                c.inc(t.matmul(pb[1][:, 0:512], wu2[k][:, dc, half * 128:(half + 1) * 128],
                                               xnT_sb[:, dc, sp * 512:(sp + 1) * 512],
                                               start=(dc == 0), stop=(dc == ND - 1)), "pe")

                    wg_consumed[0] = bA["pe"] + 16 * (2 * 14 + 2)
                    wg_consumed[1] = bA["pe"] + 16 * (2 * 15 + 2)
                    wu_consumed[0] = wg_consumed[0]
                    wu_consumed[1] = wg_consumed[1]

                    @block.scalar
                    def _(s, sp=sp, bA=bA):
                        for fc in range(32):
                            if fc == 0:
                                s.wait_ge(sem["pe"], peB_end_prev[0])
                            s.wait_ge(sem["pe"], bA["pe"] + 16 * fc + 8)
                            if fc >= 1:
                                s.wait_ge(sem["dve"], bA["dve"] + 2 * fc)
                            c.inc(s.activation(out=gsil[:], in_=pb[0][:, 0:512], func=AF.Silu,
                                               bias=bg_sb[:, fc:fc + 1]), "act")

                    @block.vector
                    def _(v, sp=sp, bA=bA):
                        for fc in range(32):
                            if fc == 0:
                                v.wait_ge(sem["pe"], peB_end_prev[0])
                            v.wait_ge(sem["pe"], bA["pe"] + 16 * (fc + 1))
                            v.wait_ge(sem["act"], bA["act"] + fc + 1)
                            c.inc(v.tensor_scalar(upb[:], pb[1][:, 0:512], bu_sb[:, fc:fc + 1],
                                                  None, ALU.add), "dve")
                            c.inc(v.tensor_tensor(prodT[:, fc, :], upb[:], gsil[:], ALU.mult), "dve")
                    prevA[0] = (bA["act"], bA["dve"])

                    for dc in range(2):
                        bB = c.snap()
                        @block.sync
                        def _(sy, sp=sp, dc=dc, bB=bB):
                            for fp in range(16):
                                k = fp % 2
                                gp = "gE" if k == 0 else "gF"
                                if fp < 2:
                                    sy.wait_ge(sem["pe"], wd_consumed[k])
                                else:
                                    sy.wait_ge(sem["pe"], bB["pe"] + 8 * (fp - 1))
                                c.inc(sy.dma_start(out=wd2[k][:],
                                                   in_=wdb_d[fp * 256:(fp + 1) * 256, dc * 512:(dc + 1) * 512]
                                                       .rearrange("(a p) d -> p a d", p=128)), gp, 16)

                        @block.tensor
                        def _(t, sp=sp, dc=dc, bB=bB, bA=bA):
                            for fp in range(16):
                                k = fp % 2
                                gp = "gE" if k == 0 else "gF"
                                t.wait_ge(sem[gp], bB[gp] + 16 * (fp // 2 + 1))
                                if fp == 0:
                                    t.wait_ge(sem["dve"], prev_evac_dve[0])
                                    t.wait_ge(sem["dve"], bA["dve"] + 64)
                                for half in range(2):
                                    fc = 2 * fp + half
                                    for tt4 in range(4):
                                        c.inc(t.matmul(pb[2 + tt4][:, 0:512],
                                                       prodT[:, fc, tt4 * 128:(tt4 + 1) * 128],
                                                       wd2[k][:, half, :],
                                                       start=(fc == 0), stop=(fc == 31)), "pe")
                        wd_consumed[0] = bB["pe"] + 8 * 15
                        wd_consumed[1] = bB["pe"] + 8 * 16
                        peB_end_prev[0] = c.v["pe"]

                        @block.vector
                        def _(v, sp=sp, dc=dc, bB=bB):
                            v.wait_ge(sem["pe"], bB["pe"] + 128)
                            for tt4 in range(4):
                                stp = "gG" if tt4 % 2 == 0 else "gH"
                                v.wait_ge(sem[stp], bB[stp] + 16 * (tt4 // 2))
                                c.inc(v.tensor_tensor(ot[tt4 % 2][:], pb[2 + tt4][:, 0:512],
                                                      bdb_sb[:, dc * 4:(dc + 1) * 4, :].rearrange("p a x -> p (a x)"),
                                                      ALU.add), "dve")
                        prev_evac_dve[0] = c.v["dve"]

                        @block.gpsimd
                        def _(g, sp=sp, dc=dc, bB=bB):
                            for tt4 in range(4):
                                stp = "gG" if tt4 % 2 == 0 else "gH"
                                g.wait_ge(sem["dve"], bB["dve"] + tt4 + 1)
                                tt = sp * 4 + tt4
                                c.inc(g.dma_start(out=acc_d[tt * 128:(tt + 1) * 128, dc * 512:(dc + 1) * 512],
                                                  in_=ot[tt4 % 2][:], accum_op=ALU.add), stp, 16)
                @block.gpsimd
                def _(g):
                    c.wait(g, "gG")
                    c.wait(g, "gH")
                barrier()

        def emit_l0_peer_a(xnT_sb, qT_sb):
            with ExitStack() as pc:
                det = [usb(pc, f"pa_de{i}", [128, ND, 512], BF16) for i in range(2)]
                wqt = usb(pc, "pa_wq", [128, ND, 512], BF16)
                wst = [usb(pc, f"pa_ws{i}", [128, 512], BF16) for i in range(2)]

                b = c.snap()
                @block.gpsimd
                def _(g, b=b):
                    c.inc(g.dma_start(out=wqt[:], in_=Wqp.rearrange("(a p) f -> p a f", p=128)), "g", 16)
                @block.tensor
                def _(t, b=b):
                    t.wait_ge(sem["g"], b["g"] + 16)
                    for u in range(32):
                        fc, sp = divmod(u, 8)
                        if u >= 2:
                            t.wait_ge(sem["dve"], b["dve"] + u - 1)
                        for dc in range(ND):
                            c.inc(t.matmul(pb[u % 2][:, 0:512], wqt[:, dc, fc * 128:(fc + 1) * 128],
                                           xnT_sb[:, dc, sp * 512:(sp + 1) * 512],
                                           start=(dc == 0), stop=(dc == ND - 1)), "pe")
                @block.vector
                def _(v, b=b):
                    for u in range(32):
                        fc, sp = divmod(u, 8)
                        v.wait_ge(sem["pe"], b["pe"] + 8 * (u + 1))
                        c.inc(v.tensor_copy(qT_sb[:, fc, sp * 512:(sp + 1) * 512], pb[u % 2][:, 0:512]), "dve")
                barrier()

                for ic in range(8):
                    b = c.snap()
                    @block.sync
                    def _(sy, ic=ic, b=b):
                        c.inc(sy.dma_start(out=det[ic % 2][:],
                                           in_=deT_d[:, ic * 512:(ic + 1) * 512].rearrange("(a p) i -> p a i", p=128)),
                              "ld", 16)
                        for tt in range(NT):
                            stp = "stA" if tt % 2 == 0 else "stB"
                            sy.wait_ge(sem["dve"], b["dve"] + tt + 1)
                            c.inc(sy.dma_start(out=wall_d[tt * 128:(tt + 1) * 128, ic * 512:(ic + 1) * 512],
                                               in_=wst[tt % 2][:]), stp, 16)
                        c.wait(sy, "stA")
                        c.wait(sy, "stB")

                    @block.tensor
                    def _(t, ic=ic, b=b):
                        t.wait_ge(sem["ld"], b["ld"] + 16)
                        for tt in range(NT):
                            if tt >= 2:
                                t.wait_ge(sem["dve"], b["dve"] + tt - 1)
                            for dc in range(ND):
                                c.inc(t.matmul(pb[tt % 2][:, 0:512], xnT_sb[:, dc, tt * 128:(tt + 1) * 128],
                                               det[ic % 2][:, dc, :],
                                               start=(dc == 0), stop=(dc == ND - 1)), "pe")

                    @block.vector
                    def _(v, ic=ic, b=b):
                        for tt in range(NT):
                            stp = "stA" if tt % 2 == 0 else "stB"
                            v.wait_ge(sem["pe"], b["pe"] + 8 * (tt + 1))
                            if tt >= 2:
                                v.wait_ge(sem[stp], b[stp] + 16 * (tt // 2)), 
                            c.inc(v.tensor_copy(wst[tt % 2][:], pb[tt % 2][:, 0:512]), "dve")
                    barrier()

        def emit_l0_peer_b(qT_sb):
            with ExitStack() as pc:
                ue = usb(pc, "pb_ue", [128, 32, D], BF16)
                wallt = [usb(pc, f"pb_w{i}", [128, NEXP], BF16) for i in range(2)]
                silw = usb(pc, "pb_sw", [128, NEXP], BF16)
                sfull = usb(pc, "pb_sf", [128, NEXP], BF16)
                fsc = usb(pc, "pb_fs", [128, NEXP], BF16)
                pexp = usb(pc, "pb_pe", [128, NEXP], BF16)
                mskt = usb(pc, "pb_mk", [128, NEXP], BF16)
                sT = usb(pc, "pb_sT", [128, 32, 128], BF16)
                simb = usb(pc, "pb_sb", [128, 2, 64], BF16)
                simf = usb(pc, "pb_sfl", [128, 2, 64], F32)
                sx8 = usb(pc, "pb_sx", [128, 8], F32)
                sy8 = usb(pc, "pb_sy", [128, 8], F32)
                cnd = usb(pc, "pb_cn", [128, 64], BF16)
                cndf = usb(pc, "pb_cf", [128, 64], F32)
                c8 = usb(pc, "pb_c8", [128, 8], F32)
                e8 = usb(pc, "pb_e8", [128, 8], F32)
                cls = usb(pc, "pb_cl", [128, 8], F32)
                est = [usb(pc, f"pb_es{i}", [128, 512], F32) for i in range(2)]

                b0 = c.snap()
                @block.gpsimd
                def _(g):
                    c.inc(g.dma_start(out=ue[:], in_=uem0.rearrange("(a p) d -> p a d", p=128)), "g", 16)

                PE_T = 8 + 32 + 64
                DVE_T = 4 * 13 + 1 + 32 + 2
                ACT_T = 1 + 4 * 3

                for tt in range(NT):
                    b = c.snap()
                    ldp = "ldA" if tt % 2 == 0 else "ldB"
                    @block.sync
                    def _(sy, tt=tt, b=b, ldp=ldp):
                        if tt >= 2:
                            sy.wait_ge(sem["act"], b["act"] - (ACT_T - 1))
                        c.inc(sy.dma_start(out=wallt[tt % 2][:], in_=wall_d[tt * 128:(tt + 1) * 128, :]), ldp, 16)

                    @block.scalar
                    def _(s, tt=tt, b=b, ldp=ldp):
                        s.wait_ge(sem[ldp], b[ldp] + 16)
                        if tt >= 1:
                            s.wait_ge(sem["dve"], b["dve"] - DVE_T + 53)
                        c.inc(s.activation(out=silw[:], in_=wallt[tt % 2][:], func=AF.Silu), "act")
                        for h in range(PEER_H):
                            hb_dve = b["dve"] + 13 * h
                            hb_act = b["act"] + 1 + 3 * h
                            s.wait_ge(sem["dve"], hb_dve + 8)
                            c.inc(s.activation(out=e8[:], in_=c8[:], func=AF.Exp,
                                               bias=cls[:, 0:1], accum_out=cls[:, 1:2]), "act")
                            c.inc(s.activation(out=cls[:, 2:3], in_=cls[:, 1:2], func=AF.Ln), "act")
                            s.wait_ge(sem["dve"], hb_dve + 10)
                            c.inc(s.activation(out=pexp[:], in_=fsc[:], func=AF.Exp,
                                               bias=cls[:, 3:4]), "act")

                    @block.tensor
                    def _(t, tt=tt, b=b):
                        if tt == 0:
                            t.wait_ge(sem["g"], b0["g"] + 16)
                        for h in range(PEER_H):
                            if h >= 1:
                                t.wait_ge(sem["dve"], b["dve"] + 13 * (h - 1) + 2)
                            elif tt >= 1:
                                t.wait_ge(sem["dve"], b["dve"] - DVE_T + 13 * 3 + 2)
                            for p in range(2):
                                fc = 2 * p + h // 2
                                off = (h % 2) * 64
                                c.inc(t.matmul(pb[5][0:128, p * 64:(p + 1) * 64],
                                               qT_sb[off:off + 64, fc, tt * 128:(tt + 1) * 128],
                                               keysT_sb[off:off + 64, h * 2 + p, :],
                                               start=True, stop=True), "pe")
                        t.wait_ge(sem["dve"], b["dve"] + 53)
                        for icn in range(32):
                            if icn >= 2:
                                t.wait_ge(sem["dve"], b["dve"] + 53 + icn - 1)
                            c.inc(t.transpose(pbt[icn % 2][0:128, 0:128],
                                              fsc[:, icn * 128:(icn + 1) * 128], id_sb[:]), "pe")
                        t.wait_ge(sem["dve"], b["dve"] + 85)
                        for dc in range(2):
                            if tt >= 1 and dc == 0:
                                t.wait_ge(sem["dve"], b["dve"] + 85)
                            if tt >= 1:
                                t.wait_ge(sem["st"], b["st"] - 16)
                            for icn in range(32):
                                c.inc(t.matmul(pb[2 + dc][:, 0:512], sT[:, icn, :],
                                               ue[:, icn, dc * 512:(dc + 1) * 512],
                                               start=(icn == 0), stop=(icn == 31)), "pe")

                    @block.vector
                    def _(v, tt=tt, b=b):
                        for h in range(PEER_H):
                            hb_act = b["act"] + 1 + 3 * h
                            v.wait_ge(sem["pe"], b["pe"] + 2 * (h + 1))
                            c.inc(v.tensor_copy(simb[:], pb[5][0:128, 0:128].rearrange("p (a x) -> p a x", a=2)), "dve")
                            c.inc(v.tensor_copy(simf[:], pb[5][0:128, 0:128].rearrange("p (a x) -> p a x", a=2)), "dve")
                            c.inc(v.max(out=sx8[:], in_=simf[:, 0, :]), "dve")
                            c.inc(v.max(out=sy8[:], in_=simf[:, 1, :]), "dve")
                            c.inc(v.tensor_tensor(
                                cnd[:].rearrange("p (a x) -> p a x", a=8),
                                sx8[:].rearrange("p (a o) -> p a o", o=1).to_broadcast([128, 8, 8]),
                                sy8[:].rearrange("p (o x) -> p o x", o=1).to_broadcast([128, 8, 8]),
                                ALU.add), "dve")
                            c.inc(v.tensor_copy(cndf[:], cnd[:]), "dve")
                            c.inc(v.max(out=c8[:], in_=cndf[:]), "dve")
                            c.inc(v.tensor_scalar(cls[:, 0:1], c8[:, 0:1], -1.0, None, ALU.mult), "dve")
                            v.wait_ge(sem["act"], hb_act + 2)
                            c.inc(v.tensor_scalar(cls[:, 3:4], cls[:, 0:1], cls[:, 2:3], None,
                                                  ALU.subtract), "dve")
                            c.inc(v.tensor_tensor(
                                fsc[:].rearrange("p (a x) -> p a x", a=64),
                                simb[:, 0, :].rearrange("p (a o) -> p a o", o=1).to_broadcast([128, 64, 64]),
                                simb[:, 1, :].rearrange("p (o x) -> p o x", o=1).to_broadcast([128, 64, 64]),
                                ALU.add), "dve")
                            c.inc(v.tensor_scalar(mskt[:], fsc[:], c8[:, 7:8], None, ALU.is_ge), "dve")
                            v.wait_ge(sem["act"], hb_act + 3)
                            c.inc(v.tensor_tensor(pexp[:], pexp[:], mskt[:], ALU.mult), "dve")
                            if h == 0:
                                c.inc(v.tensor_copy(sfull[:], pexp[:]), "dve")
                            else:
                                c.inc(v.tensor_tensor(sfull[:], sfull[:], pexp[:], ALU.add), "dve")
                        v.wait_ge(sem["act"], b["act"] + 1)
                        c.inc(v.tensor_tensor(fsc[:], sfull[:], silw[:], ALU.mult), "dve")
                        for icn in range(32):
                            v.wait_ge(sem["pe"], b["pe"] + 8 + icn + 1)
                            c.inc(v.tensor_copy(sT[:, icn, :], pbt[icn % 2][0:128, 0:128]), "dve")
                        for dc in range(2):
                            gp = "gA" if dc == 0 else "gB"
                            v.wait_ge(sem["pe"], b["pe"] + 8 + 32 + 32 * (dc + 1))
                            if tt >= 1:
                                v.wait_ge(sem[gp], b[gp])
                            c.inc(v.tensor_copy(est[dc][:], pb[2 + dc][:, 0:512]), "dve")

                    @block.gpsimd
                    def _(g, tt=tt, b=b):
                        for dc in range(2):
                            gp = "gA" if dc == 0 else "gB"
                            g.wait_ge(sem["dve"], b["dve"] + 85 + dc + 1)
                            c.inc(g.dma_start(out=acc_d[tt * 128:(tt + 1) * 128, dc * 512:(dc + 1) * 512],
                                              in_=est[dc][:], accum_op=ALU.add), gp, 16)
                @block.gpsimd
                def _(g):
                    c.wait(g, "gA")
                    c.wait(g, "gB")
                barrier()

        def emit_residual(x_src, acc_src, dst_ap):
            with ExitStack() as pc:
                xt = [usb(pc, f"r_x{i}", [128, D], F32) for i in range(2)]
                at = [usb(pc, f"r_a{i}", [128, D], F32) for i in range(2)]
                so = [usb(pc, f"r_s{i}", [128, D], F32) for i in range(2)]
                b = c.snap()

                @block.sync
                def _(sy, b=b):
                    for u in range(NT):
                        xp = "ldA" if u % 2 == 0 else "ldB"
                        ap_ = "ldC" if u % 2 == 0 else "ldD"
                        if u >= 2:
                            sy.wait_ge(sem["dve"], b["dve"] + (u - 1))
                        c.inc(sy.dma_start(out=xt[u % 2][:], in_=x_src[u * 128:(u + 1) * 128, :]), xp, 16)
                        c.inc(sy.dma_start(out=at[u % 2][:], in_=acc_src[u * 128:(u + 1) * 128, :]), ap_, 16)
                        if u >= 1:
                            stp = "stA" if (u - 1) % 2 == 0 else "stB"
                            sy.wait_ge(sem["dve"], b["dve"] + u)
                            c.inc(sy.dma_start(out=dst_ap[(u - 1) * 128:u * 128, :], in_=so[(u - 1) % 2][:]), stp, 16)
                    stp = "stA" if (NT - 1) % 2 == 0 else "stB"
                    sy.wait_ge(sem["dve"], b["dve"] + NT)
                    c.inc(sy.dma_start(out=dst_ap[(NT - 1) * 128:NT * 128, :], in_=so[(NT - 1) % 2][:]), stp, 16)
                    c.wait(sy, "stA")
                    c.wait(sy, "stB")

                @block.vector
                def _(v, b=b):
                    for u in range(NT):
                        xp = "ldA" if u % 2 == 0 else "ldB"
                        ap_ = "ldC" if u % 2 == 0 else "ldD"
                        v.wait_ge(sem[xp], b[xp] + 16 * (u // 2 + 1))
                        v.wait_ge(sem[ap_], b[ap_] + 16 * (u // 2 + 1))
                        if u >= 2:
                            stp = "stA" if u % 2 == 0 else "stB"
                            v.wait_ge(sem[stp], b[stp] + 16 * (u // 2))
                        c.inc(v.tensor_tensor(so[u % 2][:], xt[u % 2][:], at[u % 2][:], ALU.add), "dve")
                barrier()

        def emit_l1_attn_a(xnT_sb):
            with ExitStack() as pc:
                wqkv = usb(pc, "l1_wqkv", [128, ND, 3 * D], BF16)
                qkvb = [usb(pc, f"l1_qkv{i}", [128, 3 * D], BF16) for i in range(2)]
                ropc = usb(pc, "l1_rc", [128, 2 * D], BF16)
                ropt = usb(pc, "l1_rt", [128, 2 * D], BF16)
                roped = usb(pc, "l1_rp", [128, 2 * D], BF16)
                tmin = usb(pc, "l1_tm", [128, 2 * D], BF16)
                texp = usb(pc, "l1_te", [128, 2 * D], BF16)
                feat = [usb(pc, f"l1_f{i}", [128, 2 * D], BF16) for i in range(2)]
                onec = usb(pc, "l1_on", [128, 1], BF16)
                qfs = [usb(pc, f"l1_qt{i}", [128, ND, 128], BF16) for i in range(2)]

                b0 = c.snap()
                @block.gpsimd
                def _(g):
                    c.inc(g.dma_start(out=wqkv[:], in_=Wqkv1.rearrange("(a p) f -> p a f", p=128)), "g", 16)
                @block.vector
                def _(v):
                    c.inc(v.memset(onec[:], 1.0), "dve")
                barrier()

                DV = 23
                PE_T2 = 48 + 32 + 8
                b = c.snap()

                def kv_bank(h):
                    return 3 + h // 7, (h % 7) * 65

                @block.tensor
                def _(t, b=b):
                    for tt in range(NT):
                        pbase = b["pe"] + PE_T2 * tt
                        dbase = b["dve"] + DV * tt
                        for fcq in range(6):
                            if tt > 0 or fcq >= 2:
                                if fcq >= 2:
                                    t.wait_ge(sem["dve"], dbase + fcq - 1)
                                else:
                                    t.wait_ge(sem["dve"], dbase - DV + fcq + 5)
                            for dc in range(ND):
                                c.inc(t.matmul(pb[fcq % 2][:, 0:512], xnT_sb[:, dc, tt * 128:(tt + 1) * 128],
                                               wqkv[:, dc, fcq * 512:(fcq + 1) * 512],
                                               start=(dc == 0), stop=(dc == ND - 1)), "pe")
                        t.wait_ge(sem["dve"], dbase + 15)
                        for h in range(16):
                            bk, col = kv_bank(h)
                            first_h = 0 if bk == 3 else (7 if bk == 4 else 14)
                            last_h = 6 if bk == 3 else (13 if bk == 4 else 15)
                            c.inc(t.matmul(pb[bk][0:64, col:col + 64],
                                           feat[tt % 2][:, D + h * 64:D + h * 64 + 64],
                                           qkvb[tt % 2][:, 2 * D + h * 64:2 * D + h * 64 + 64],
                                           start=(tt == 0 and h == first_h), stop=False), "pe")
                            c.inc(t.matmul(pb[bk][0:64, col + 64:col + 65],
                                           feat[tt % 2][:, D + h * 64:D + h * 64 + 64],
                                           onec[:], start=False,
                                           stop=(tt == NT - 1 and h == last_h)), "pe")
                        for dc in range(ND):
                            if dc >= 2:
                                t.wait_ge(sem["dve"], dbase + 15 + dc - 1)
                            c.inc(t.transpose(pbt[dc % 2][0:128, 0:128],
                                              feat[tt % 2][:, dc * 128:(dc + 1) * 128], id_sb[:]), "pe")

                @block.vector
                def _(v, b=b):
                    for tt in range(NT):
                        pbase = b["pe"] + PE_T2 * tt
                        dbase = b["dve"] + DV * tt
                        abase = b["act"] + 2 * tt
                        for fcq in range(6):
                            v.wait_ge(sem["pe"], pbase + 8 * (fcq + 1))
                            c.inc(v.tensor_copy(qkvb[tt % 2][:, fcq * 512:(fcq + 1) * 512],
                                                pb[fcq % 2][:, 0:512]), "dve")
                        qk = qkvb[tt % 2][:, 0:2 * D]
                        qk4 = qk.rearrange("p (h x n) -> p h x n", h=32, x=2)
                        cosv = cos_sb[:, tt, :].rearrange("p (a x n) -> p a x n", a=1, x=1).to_broadcast([128, 32, 2, 32])
                        sinv = sin_sb[:, tt, :].rearrange("p (a n) -> p a n", a=1).to_broadcast([128, 32, 32])
                        msinv = msin_sb[:, tt, :].rearrange("p (a n) -> p a n", a=1).to_broadcast([128, 32, 32])
                        rc4 = ropc[:].rearrange("p (h x n) -> p h x n", h=32, x=2)
                        rt4 = ropt[:].rearrange("p (h x n) -> p h x n", h=32, x=2)
                        c.inc(v.tensor_tensor(rc4, qk4, cosv, ALU.mult), "dve")
                        c.inc(v.tensor_tensor(rt4[:, :, 0, :], qk4[:, :, 1, :], msinv, ALU.mult), "dve")
                        c.inc(v.tensor_tensor(rt4[:, :, 1, :], qk4[:, :, 0, :], sinv, ALU.mult), "dve")
                        c.inc(v.tensor_tensor(roped[:], ropc[:], ropt[:], ALU.add), "dve")
                        c.inc(v.tensor_scalar(tmin[:, 0:D], roped[:, 0:D], 0.125, 0.0, ALU.mult, ALU.min), "dve")
                        c.inc(v.tensor_scalar(tmin[:, D:2 * D], roped[:, D:2 * D], 1.0, 0.0, ALU.mult, ALU.min), "dve")
                        if tt >= 2:
                            v.wait_ge(sem["pe"], b["pe"] + PE_T2 * (tt - 1))
                        c.inc(v.tensor_scalar(feat[tt % 2][:, 0:D], roped[:, 0:D], 0.125, 0.0,
                                              ALU.mult, ALU.max), "dve")
                        c.inc(v.tensor_scalar(feat[tt % 2][:, D:2 * D], roped[:, D:2 * D], 1.0, 0.0,
                                              ALU.mult, ALU.max), "dve")
                        v.wait_ge(sem["act"], abase + 2)
                        c.inc(v.tensor_tensor(feat[tt % 2][:], feat[tt % 2][:], texp[:], ALU.add), "dve")
                        for dc in range(ND):
                            v.wait_ge(sem["pe"], pbase + 48 + 32 + dc + 1)
                            if tt >= 2 and dc == 0:
                                stp = "stA" if tt % 2 == 0 else "stB"
                                v.wait_ge(sem[stp], b[stp] + 16 * (tt // 2))
                            c.inc(v.tensor_copy(qfs[tt % 2][:, dc, :], pbt[dc % 2][0:128, 0:128]), "dve")

                @block.scalar
                def _(s, b=b):
                    for tt in range(NT):
                        dbase = b["dve"] + DV * tt
                        s.wait_ge(sem["dve"], dbase + 12)
                        c.inc(s.activation(out=texp[:, 0:D], in_=tmin[:, 0:D], func=AF.Exp), "act")
                        c.inc(s.activation(out=texp[:, D:2 * D], in_=tmin[:, D:2 * D], func=AF.Exp), "act")

                @block.sync
                def _(sy, b=b):
                    for tt in range(NT):
                        stp = "stA" if tt % 2 == 0 else "stB"
                        sy.wait_ge(sem["dve"], b["dve"] + DV * (tt + 1))
                        c.inc(sy.dma_start(out=qfT_d[:, tt * 128:(tt + 1) * 128].rearrange("(a p) t -> p a t", p=128),
                                           in_=qfs[tt % 2][:]), stp, 16)
                    c.wait(sy, "stA")
                    c.wait(sy, "stB")
                barrier()

        def emit_l1_attn_b():
            with ExitStack() as pc:
                kvb = usb(pc, "l1b_kv", [128, 8, 65], BF16)
                qfp = [usb(pc, f"l1b_q{i}", [128, T], BF16) for i in range(2)]
                dmx = usb(pc, "l1b_dm", [1, 512], F32)
                rr = usb(pc, "l1b_rr", [1, 512], F32)
                rrb = usb(pc, "l1b_rb", [1, 512], BF16)
                rsb = usb(pc, "l1b_rs", [64, 512], F32)
                aop = usb(pc, "l1b_ao", [128, T], BF16)

                b = c.snap()
                @block.vector
                def _(v, b=b):
                    kvf = kvb[:].rearrange("p a x -> p (a x)")
                    for h in range(16):
                        bk, col = (3 + h // 7, (h % 7) * 65)
                        dst_col = (h // 2) * 65
                        r0 = (h % 2) * 64
                        c.inc(v.tensor_copy(kvf[r0:r0 + 64, dst_col:dst_col + 65],
                                            pb[bk][0:64, col:col + 65]), "dve")
                barrier()

                for pair in range(8):
                    b = c.snap()
                    @block.sync
                    def _(sy, pair=pair, b=b):
                        if pair >= 2:
                            sy.wait_ge(sem["pe"], b["pe"] - 32)
                        c.inc(sy.dma_start(out=qfp[pair % 2][:], in_=qfT_d[pair * 128:(pair + 1) * 128, :]), "ld", 16)

                    @block.tensor
                    def _(t, pair=pair, b=b):
                        t.wait_ge(sem["ld"], b["ld"] + 16)
                        rowA = kvb[0:64, pair, :]
                        rowB = kvb[64:128, pair, :]
                        for tc in range(8):
                            for hh in range(2):
                                u = tc * 2 + hh
                                row = rowA if hh == 0 else rowB
                                qap = qfp[pair % 2][(hh * 64):(hh * 64) + 64, tc * 512:(tc + 1) * 512]
                                if u >= 2 or pair >= 1:
                                    t.wait_ge(sem["dve"], b["dve"] + 5 * (u - 2) + 5)
                                c.inc(t.matmul(pb[hh][0:65, 0:512], row, qap, start=True, stop=True), "pe")
                                t.wait_ge(sem["dve"], b["dve"] + 5 * u + 3)
                                c.inc(t.matmul(pb[2][0:64, 0:512], onesr_sb[0:1, 0:64], rrb[:],
                                               start=True, stop=True), "pe")

                    @block.vector
                    def _(v, pair=pair, b=b):
                        for tc in range(8):
                            for hh in range(2):
                                u = tc * 2 + hh
                                if u == 0 and pair >= 1:
                                    v.wait_ge(sem["st"], b["st"])
                                v.wait_ge(sem["pe"], b["pe"] + 2 * u + 1)
                                c.inc(v.tensor_scalar(dmx[:], pb[hh][64:65, 0:512], eps_sb[0:1, 1:2], None, ALU.max), "dve")
                                c.inc(v.reciprocal(rr[:], dmx[:]), "dve")
                                c.inc(v.tensor_copy(rrb[:], rr[:]), "dve")
                                v.wait_ge(sem["pe"], b["pe"] + 2 * u + 2)
                                c.inc(v.tensor_copy(rsb[:], pb[2][0:64, 0:512]), "dve")
                                c.inc(v.tensor_tensor(aop[hh * 64:hh * 64 + 64, tc * 512:(tc + 1) * 512],
                                                      pb[hh][0:64, 0:512], rsb[:], ALU.mult), "dve")

                    @block.sync
                    def _(sy, pair=pair, b=b):
                        sy.wait_ge(sem["dve"], b["dve"] + 80)
                        c.inc(sy.dma_start(out=aoT_d[pair * 128:(pair + 1) * 128, :], in_=aop[:]), "st", 16)
                barrier()

        def emit_l1_router(xnT_sb, g_sb, s_sb):
            with ExitStack() as pc:
                lg = usb(pc, "rt_lg", [128, 8], F32)
                t2 = usb(pc, "rt_t2", [128, 8], F32)
                w01 = usb(pc, "rt_w", [128, 4], F32)
                eqs = usb(pc, "rt_e", [128, 2], F32)
                b = c.snap()
                DVR = 4 + 7 * E1

                @block.tensor
                def _(t, b=b):
                    for tt in range(NT):
                        if tt >= 1:
                            t.wait_ge(sem["dve"], b["dve"] + DVR * (tt - 1) + 1)
                        for dc in range(ND):
                            c.inc(t.matmul(pb[5][0:128, 0:8], xnT_sb[:, dc, tt * 128:(tt + 1) * 128],
                                           wgT_sb[:, dc, :], start=(dc == 0), stop=(dc == ND - 1)), "pe")

                @block.vector
                def _(v, b=b):
                    for tt in range(NT):
                        v.wait_ge(sem["pe"], b["pe"] + 8 * (tt + 1))
                        c.inc(v.tensor_copy(lg[:], pb[5][0:128, 0:8]), "dve")
                        c.inc(v.max(out=t2[:], in_=lg[:]), "dve")
                        c.inc(v.tensor_scalar(w01[:, 0:1], t2[:, 0:1], t2[:, 1:2], None, ALU.subtract), "dve")
                        v.wait_ge(sem["act"], b["act"] + tt + 1)
                        c.inc(v.tensor_scalar(w01[:, 2:3], w01[:, 1:2], -1.0, 1.0, ALU.mult, ALU.add), "dve")
                        for e in range(E1):
                            c.inc(v.tensor_scalar(eqs[:, 0:1], lg[:, e:e + 1], t2[:, 0:1], None, ALU.is_equal), "dve")
                            c.inc(v.tensor_scalar(eqs[:, 1:2], lg[:, e:e + 1], t2[:, 1:2], None, ALU.is_equal), "dve")
                            c.inc(v.tensor_scalar(s_sb[:, tt, e:e + 1], eqs[:, 0:1], -1.0, 1.0, ALU.mult, ALU.add), "dve")
                            c.inc(v.tensor_tensor(eqs[:, 1:2], eqs[:, 1:2], s_sb[:, tt, e:e + 1], ALU.mult), "dve")
                            c.inc(v.tensor_tensor(eqs[:, 0:1], eqs[:, 0:1], w01[:, 1:2], ALU.mult), "dve")
                            c.inc(v.tensor_tensor(eqs[:, 1:2], eqs[:, 1:2], w01[:, 2:3], ALU.mult), "dve")
                            c.inc(v.tensor_tensor(g_sb[:, tt, e:e + 1], eqs[:, 0:1], eqs[:, 1:2], ALU.add), "dve")

                @block.scalar
                def _(s, b=b):
                    for tt in range(NT):
                        s.wait_ge(sem["dve"], b["dve"] + DVR * tt + 3)
                        c.inc(s.activation(out=w01[:, 1:2], in_=w01[:, 0:1], func=AF.Sigmoid), "act")
                barrier()

        def emit_l1_moe(xnT_sb, g_sb):
            with ExitStack() as pc:
                w1s = usb(pc, "moe_w1", [128, ND, F1], BF16)
                w2t = [usb(pc, f"moe_w2{i}", [128, 2, 512], BF16) for i in range(4)]
                actT = usb(pc, "moe_a", [128, 32, 512], BF16)
                ot = [usb(pc, f"moe_o{i}", [128, 512], F32) for i in range(2)]

                LDP = ["ldA", "ldB", "ldC", "ldD"]
                w2_consumed_pe = [0, 0, 0, 0]
                peB_end_prev = [0]
                prev_evac_dve = [0]
                prevA = [None]
                g_after_w1 = [0]

                @block.gpsimd
                def _(g):
                    c.inc(g.dma_start(out=w1s[:], in_=W11[0].rearrange("(a p) f -> p a f", p=128)), "g", 16)
                g_after_w1[0] = c.v["g"]

                for e in range(E1):
                    for sp in range(8):
                        bA = c.snap()
                        @block.tensor
                        def _(t, e=e, sp=sp, bA=bA):
                            if sp == 0:
                                t.wait_ge(sem["g"], g_after_w1[0])
                            for fc in range(32):
                                if fc < 2:
                                    if prevA[0] is not None:
                                        t.wait_ge(sem["act"], prevA[0][0] + 32)
                                else:
                                    t.wait_ge(sem["act"], bA["act"] + fc - 1)
                                for dc in range(ND):
                                    c.inc(t.matmul(pb[fc % 2][:, 0:512], w1s[:, dc, fc * 128:(fc + 1) * 128],
                                                   xnT_sb[:, dc, sp * 512:(sp + 1) * 512],
                                                   start=(dc == 0), stop=(dc == ND - 1)), "pe")

                        @block.scalar
                        def _(s, e=e, sp=sp, bA=bA):
                            for fc in range(32):
                                if fc == 0:
                                    s.wait_ge(sem["pe"], peB_end_prev[0])
                                s.wait_ge(sem["pe"], bA["pe"] + 8 * (fc + 1))
                                c.inc(s.activation(out=actT[:, fc, :], in_=pb[fc % 2][:, 0:512],
                                                   func=AF.Silu), "act")
                        prevA[0] = (bA["act"], bA["dve"])

                        if sp == 7 and e < E1 - 1:
                            gate_pe = c.v["pe"]
                            @block.gpsimd
                            def _(g, e=e, gate_pe=gate_pe):
                                g.wait_ge(sem["pe"], gate_pe)
                                c.inc(g.dma_start(out=w1s[:],
                                                  in_=W11[e + 1].rearrange("(a p) f -> p a f", p=128)), "g", 16)
                            g_after_w1[0] = c.v["g"]

                        for dc in range(2):
                            bB = c.snap()
                            @block.sync
                            def _(sy, e=e, sp=sp, dc=dc, bB=bB):
                                for fp in range(16):
                                    k = fp % 4
                                    if fp < 4:
                                        sy.wait_ge(sem["pe"], w2_consumed_pe[k])
                                    else:
                                        sy.wait_ge(sem["pe"], bB["pe"] + 8 * (fp - 3))
                                    c.inc(sy.dma_start(
                                        out=w2t[k][:],
                                        in_=w2b_d[e][fp * 256:(fp + 1) * 256, dc * 512:(dc + 1) * 512]
                                            .rearrange("(a p) d -> p a d", p=128)), LDP[k], 16)

                            @block.tensor
                            def _(t, e=e, sp=sp, dc=dc, bB=bB, bA=bA):
                                for fp in range(16):
                                    k = fp % 4
                                    t.wait_ge(sem[LDP[k]], bB[LDP[k]] + 16 * (fp // 4 + 1))
                                    if fp == 0:
                                        t.wait_ge(sem["dve"], prev_evac_dve[0])
                                        t.wait_ge(sem["act"], bA["act"] + 32)
                                    for half in range(2):
                                        fc = 2 * fp + half
                                        for tt4 in range(4):
                                            c.inc(t.matmul(pb[2 + tt4][:, 0:512],
                                                           actT[:, fc, tt4 * 128:(tt4 + 1) * 128],
                                                           w2t[k][:, half, :],
                                                           start=(fc == 0), stop=(fc == 31)), "pe")
                                    w2_consumed_pe[k] = c.v["pe"]
                            peB_end_prev[0] = c.v["pe"]

                            @block.vector
                            def _(v, e=e, sp=sp, dc=dc, bB=bB):
                                v.wait_ge(sem["pe"], bB["pe"] + 128)
                                for tt4 in range(4):
                                    gp = "gA" if tt4 % 2 == 0 else "gB"
                                    v.wait_ge(sem[gp], bB[gp] + 16 * (tt4 // 2))
                                    tt = sp * 4 + tt4
                                    c.inc(v.tensor_scalar(ot[tt4 % 2][:], pb[2 + tt4][:, 0:512],
                                                          g_sb[:, tt, e:e + 1], None, ALU.mult), "dve")
                            prev_evac_dve[0] = c.v["dve"]

                            @block.gpsimd
                            def _(g, e=e, sp=sp, dc=dc, bB=bB):
                                for tt4 in range(4):
                                    gp = "gA" if tt4 % 2 == 0 else "gB"
                                    g.wait_ge(sem["dve"], bB["dve"] + tt4 + 1)
                                    tt = sp * 4 + tt4
                                    c.inc(g.dma_start(out=acc_d[tt * 128:(tt + 1) * 128,
                                                                dc * 512:(dc + 1) * 512],
                                                      in_=ot[tt4 % 2][:], accum_op=ALU.add), gp, 16)
                @block.gpsimd
                def _(g):
                    c.wait(g, "gA")
                    c.wait(g, "gB")
                barrier()

        def emit_l1_moe_sparse(g_sb, s_sb):
            from concourse.bass_core_rust import IndirectOffsetOnAxis
            I32 = mybir.dt.int32

            with ExitStack() as ps_:
                gT = usb(ps_, "ms_gT", [8, T], F32)
                rA = usb(ps_, "ms_rA", [8, T], F32)
                rB = usb(ps_, "ms_rB", [8, T], F32)
                zst = usb(ps_, "ms_z", [128, 4096], F32)
                mst = usb(ps_, "ms_m", [128, CAP // 128 * 8], F32)
                b = c.snap()

                @block.tensor
                def _(t, b=b):
                    for tt in range(NT):
                        if tt >= 1:
                            t.wait_ge(sem["dve"], b["dve"] + tt)
                        c.inc(t.transpose(pb45[0].ap()[0:8, 0:128], g_sb[:, tt, :],
                                          idf_sb[:]), "pe")

                @block.vector
                def _(v, b=b):
                    for tt in range(NT):
                        v.wait_ge(sem["pe"], b["pe"] + tt + 1)
                        c.inc(v.tensor_copy(gT[0:8, tt * 128:(tt + 1) * 128],
                                            pb45[0].ap()[0:8, 0:128]), "dve")
                    c.inc(v.tensor_scalar(rA[:], gT[:], 0.0, None, ALU.is_gt), "dve")
                    c.inc(v.tensor_tensor_scan(rB[:], rA[:], rA[:], 0.0, ALU.add, ALU.bypass), "dve")
                    c.inc(v.tensor_tensor(rB[:], rB[:], rA[:], ALU.subtract), "dve")
                    c.inc(v.tensor_tensor(rB[:], rB[:], rA[:], ALU.mult), "dve")
                    c.inc(v.tensor_scalar(rA[:], rA[:], -4000000.0, 4000000.0, ALU.mult, ALU.add), "dve")
                    c.inc(v.tensor_tensor(rB[:], rB[:], rA[:], ALU.add), "dve")
                    c.inc(v.memset(zst[:], 0.0), "dve")
                    c.inc(v.memset(mst[:], 0.0), "dve")
                    c.inc(v.memset(mst[:].rearrange("p (a x) -> p a x", x=8)[:, :, 1:2], 5000000.0), "dve")

                @block.sync
                def _(sy, b=b):
                    c.wait(sy, "dve")
                    c.inc(sy.dma_start(out=pos_d, in_=rB[:]), "ld", 16)
                    for k in range(16):
                        c.inc(sy.dma_start(
                            out=ysc_d[k * 512:(k + 1) * 512, :].rearrange("(a p) x -> p a x", p=128),
                            in_=zst[:].rearrange("p (a x) -> p a x", a=4)), "st", 16)
                    for e in range(E1):
                        c.inc(sy.dma_start(
                            out=cmeta_d[e * CAP:(e + 1) * CAP, :].rearrange("(a p) x -> p a x", p=128),
                            in_=mst[:].rearrange("p (a x) -> p a x", x=8)), "st", 16)
                    c.wait(sy, "st")
                    c.wait(sy, "ld")
                barrier()

            with ExitStack() as pc:
                w1s = usb(pc, "ms_w1", [128, ND, F1], BF16)
                w2t = [usb(pc, f"ms_w2{i}", [128, 2, 512], BF16) for i in range(4)]
                cxT = [usb(pc, f"ms_cxT{i}", [128, ND, CAP], BF16) for i in range(2)]
                actT = usb(pc, "ms_a", [128, 32, 512], BF16)
                ot = [usb(pc, f"ms_o{i}", [128, 512], F32) for i in range(2)]
                xnb = [usb(pc, f"ms_x{i}", [128, D], BF16) for i in range(2)]
                posf = [usb(pc, f"ms_p{i}", [128, 1], F32) for i in range(2)]
                posi = [usb(pc, f"ms_pi{i}", [128, 1], I32) for i in range(2)]
                meta = [usb(pc, f"ms_mt{i}", [128, 8], F32) for i in range(2)]
                cme = [usb(pc, f"ms_cm{i}", [128, CAP // 128, 8], F32) for i in range(2)]
                tids = [usb(pc, f"ms_ti{i}", [128, CAP // 128], I32) for i in range(2)]

                LDP = ["ldA", "ldB", "ldC", "ldD"]
                w2_consumed_pe = [0, 0, 0, 0]
                prev_evac_dve = [0]
                prevA = [None]
                g_after_w1 = [0]
                cxT_free_pe = [0, 0]
                scat_done = [None] * E1
                ysc_done = [None] * E1
                NSP = CAP // 512

                @block.vector
                def _(v):
                    c.inc(v.memset(meta[0][:], 0.0), "dve")
                    c.inc(v.memset(meta[1][:], 0.0), "dve")

                bc_regs = []
                @block.gpsimd
                def _(g):
                    bc_regs.append(g.snap(CAP - 1))
                    bc_regs.append(g.snap(2 * T - 1))
                    c.inc(g.dma_start(out=w1s[:], in_=W11[0].rearrange("(a p) f -> p a f", p=128)), "g", 16)
                g_after_w1[0] = c.v["g"]

                for e in range(E1):
                    bS = c.snap()
                    @block.sync
                    def _(sy, e=e, bS=bS):
                        for tt in range(NT):
                            xp = "ldE" if tt % 2 == 0 else "ldF"
                            pp = "stC" if tt % 2 == 0 else "stD"
                            if tt >= 2:
                                gx = "gI" if tt % 2 == 0 else "gJ"
                                sy.wait_ge(sem[gx], bS[gx] + 16 * (tt // 2))
                                sy.wait_ge(sem["dve"], bS["dve"] + 4 * (tt - 1))
                            c.inc(sy.dma_start(out=xnb[tt % 2][:],
                                               in_=xn_d[tt * 128:(tt + 1) * 128, :]), xp, 16)
                            c.inc(sy.dma_start(
                                out=posf[tt % 2][:],
                                in_=pos_d[e:e + 1, tt * 128:(tt + 1) * 128]
                                    .rearrange("o (p x) -> (o p) x", x=1)), pp, 16)

                    @block.vector
                    def _(v, e=e, bS=bS):
                        for tt in range(NT):
                            pp = "stC" if tt % 2 == 0 else "stD"
                            gm = "gK" if tt % 2 == 0 else "gL"
                            v.wait_ge(sem[pp], bS[pp] + 16 * (tt // 2 + 1))
                            if tt >= 2:
                                gx2 = "gI" if tt % 2 == 0 else "gJ"
                                v.wait_ge(sem[gm], bS[gm] + 16 * (tt // 2))
                                v.wait_ge(sem[gx2], bS[gx2] + 16 * (tt // 2))
                            c.inc(v.tensor_copy(posi[tt % 2][:], posf[tt % 2][:]), "dve")
                            c.inc(v.tensor_copy(meta[tt % 2][:, 0:1], g_sb[:, tt, e:e + 1]), "dve")
                            c.inc(v.tensor_scalar(meta[tt % 2][:, 1:2], s_sb[:, tt, e:e + 1],
                                                  4096.0, None, ALU.mult), "dve")
                            c.inc(v.tensor_tensor(meta[tt % 2][:, 1:2], meta[tt % 2][:, 1:2],
                                                  tid_sb[:, tt:tt + 1], ALU.add), "dve")

                    @block.gpsimd
                    def _(g, e=e, bS=bS):
                        if e >= 1:
                            for n in ("gI", "gJ", "gK", "gL"):
                                g.wait_ge(sem[n], scat_done[e - 1][n])
                            for n in ("gM", "gN"):
                                g.wait_ge(sem[n], ysc_done[e - 1][n])
                            g.dma_reset()
                        for tt in range(NT):
                            xp = "ldE" if tt % 2 == 0 else "ldF"
                            gx = "gI" if tt % 2 == 0 else "gJ"
                            gm = "gK" if tt % 2 == 0 else "gL"
                            if tt >= 4:
                                g.wait_ge(sem[gx], bS[gx] + 16 * ((tt - 4) // 2 + 1))
                                g.wait_ge(sem[gm], bS[gm] + 16 * ((tt - 4) // 2 + 1))
                            g.wait_ge(sem[xp], bS[xp] + 16 * (tt // 2 + 1))
                            g.wait_ge(sem["dve"], bS["dve"] + 4 * (tt + 1))
                            c.inc(g.indirect_dma_start(
                                out=cx_d, out_offset=IndirectOffsetOnAxis(posi[tt % 2][:], 0),
                                in_=xnb[tt % 2][:], in_offset=None, element_offset=e * CAP * D,
                                bounds_check=bc_regs[0], oob_is_err=False), gx, 16)
                            c.inc(g.indirect_dma_start(
                                out=cmeta_d, out_offset=IndirectOffsetOnAxis(posi[tt % 2][:], 0),
                                in_=meta[tt % 2][:], in_offset=None, element_offset=e * CAP * 8,
                                bounds_check=bc_regs[0], oob_is_err=False), gm, 16)
                    scat_done[e] = {n: c.v[n] for n in ("gI", "gJ", "gK", "gL")}

                    bT = c.snap()
                    @block.sync
                    def _(sy, e=e, bT=bT):
                        for n in ("gI", "gJ"):
                            sy.wait_ge(sem[n], scat_done[e][n])
                        sy.wait_ge(sem["pe"], cxT_free_pe[e % 2])
                        for ch in range(ND):
                            c.inc(sy.dma_start(out=cxT[e % 2][:, ch, :],
                                               in_=cx_d[e * CAP:(e + 1) * CAP, ch * 128:(ch + 1) * 128],
                                               transpose=True), "ld", 16)
                        for n in ("gK", "gL"):
                            sy.wait_ge(sem[n], scat_done[e][n])
                        c.inc(sy.dma_start(out=cme[e % 2][:],
                                           in_=cmeta_d[e * CAP:(e + 1) * CAP, :].rearrange("(a p) x -> p a x", p=128)), "ld", 16)
                    ld_after_T = c.v["ld"]

                    @block.vector
                    def _(v, e=e, bT=bT, ld_after_T=ld_after_T):
                        v.wait_ge(sem["ld"], ld_after_T)
                        for j in range(CAP // 128):
                            c.inc(v.tensor_copy(tids[e % 2][:, j:j + 1], cme[e % 2][:, j, 1:2]), "dve")
                    tids_dve = c.v["dve"]

                    for sp in range(NSP):
                        bA = c.snap()
                        @block.tensor
                        def _(t, e=e, sp=sp, bA=bA, ld_after_T=ld_after_T):
                            if sp == 0:
                                t.wait_ge(sem["g"], g_after_w1[0])
                                t.wait_ge(sem["ld"], ld_after_T)
                            for fc in range(32):
                                if fc < 2:
                                    if prevA[0] is not None:
                                        t.wait_ge(sem["act"], prevA[0][0] + 32)
                                else:
                                    t.wait_ge(sem["act"], bA["act"] + fc - 1)
                                for dc in range(ND):
                                    c.inc(t.matmul(pb[fc % 2][:, 0:512], w1s[:, dc, fc * 128:(fc + 1) * 128],
                                                   cxT[e % 2][:, dc, sp * 512:(sp + 1) * 512],
                                                   start=(dc == 0), stop=(dc == ND - 1)), "pe")

                        @block.scalar
                        def _(s, e=e, sp=sp, bA=bA):
                            for fc in range(32):
                                if fc == 0:
                                    s.wait_ge(sem["pe"], prev_evac_dve[0] * 0 + peB_end_prev2[0])
                                s.wait_ge(sem["pe"], bA["pe"] + 8 * (fc + 1))
                                c.inc(s.activation(out=actT[:, fc, :], in_=pb[fc % 2][:, 0:512],
                                                   func=AF.Silu), "act")
                        prevA[0] = (bA["act"], bA["dve"])

                        if sp == NSP - 1:
                            cxT_free_pe[e % 2] = c.v["pe"]
                            if e < E1 - 1:
                                gate_pe = c.v["pe"]
                                @block.gpsimd
                                def _(g, e=e, gate_pe=gate_pe):
                                    g.wait_ge(sem["pe"], gate_pe)
                                    c.inc(g.dma_start(out=w1s[:],
                                                      in_=W11[e + 1].rearrange("(a p) f -> p a f", p=128)), "g", 16)
                                g_after_w1[0] = c.v["g"]

                        for dc in range(2):
                            bB = c.snap()
                            @block.sync
                            def _(sy, e=e, sp=sp, dc=dc, bB=bB):
                                for fp in range(16):
                                    k = fp % 4
                                    if fp < 4:
                                        sy.wait_ge(sem["pe"], w2_consumed_pe[k])
                                    else:
                                        sy.wait_ge(sem["pe"], bB["pe"] + 8 * (fp - 3))
                                    c.inc(sy.dma_start(
                                        out=w2t[k][:],
                                        in_=w2b_d[e][fp * 256:(fp + 1) * 256, dc * 512:(dc + 1) * 512]
                                            .rearrange("(a p) d -> p a d", p=128)), LDP[k], 16)

                            @block.tensor
                            def _(t, e=e, sp=sp, dc=dc, bB=bB, bA=bA):
                                for fp in range(16):
                                    k = fp % 4
                                    t.wait_ge(sem[LDP[k]], bB[LDP[k]] + 16 * (fp // 4 + 1))
                                    if fp == 0:
                                        t.wait_ge(sem["dve"], prev_evac_dve[0])
                                        t.wait_ge(sem["act"], bA["act"] + 32)
                                    for half in range(2):
                                        fc = 2 * fp + half
                                        for tt4 in range(4):
                                            c.inc(t.matmul(pb[2 + tt4][:, 0:512],
                                                           actT[:, fc, tt4 * 128:(tt4 + 1) * 128],
                                                           w2t[k][:, half, :],
                                                           start=(fc == 0), stop=(fc == 31)), "pe")
                                    w2_consumed_pe[k] = c.v["pe"]
                            peB_end_prev2[0] = c.v["pe"]

                            @block.vector
                            def _(v, e=e, sp=sp, dc=dc, bB=bB, tids_dve=tids_dve):
                                v.wait_ge(sem["pe"], bB["pe"] + 128)
                                v.wait_ge(sem["dve"], tids_dve)
                                for tt4 in range(4):
                                    gp = "gM" if tt4 % 2 == 0 else "gN"
                                    v.wait_ge(sem[gp], bB[gp] + 16 * (tt4 // 2))
                                    j = sp * 4 + tt4
                                    c.inc(v.tensor_scalar(ot[tt4 % 2][:], pb[2 + tt4][:, 0:512],
                                                          cme[e % 2][:, j, 0:1], None, ALU.mult), "dve")
                            prev_evac_dve[0] = c.v["dve"]

                            @block.gpsimd
                            def _(g, e=e, sp=sp, dc=dc, bB=bB):
                                g.wait_ge(sem["gM"], bB["gM"])
                                g.wait_ge(sem["gN"], bB["gN"])
                                for tt4 in range(4):
                                    gp = "gM" if tt4 % 2 == 0 else "gN"
                                    g.wait_ge(sem["dve"], bB["dve"] + (CAP // 128) * 0 + tt4 + 1 +
                                              (c.v["dve"] - bB["dve"]) * 0)
                                    j = sp * 4 + tt4
                                    c.inc(g.indirect_dma_start(
                                        out=ysc_d,
                                        out_offset=IndirectOffsetOnAxis(tids[e % 2][:, j:j + 1], 0),
                                        in_=ot[tt4 % 2][:], in_offset=None,
                                        element_offset=dc * 512,
                                        bounds_check=bc_regs[1], oob_is_err=False), gp, 16)
                    ysc_done[e] = {n: c.v[n] for n in ("gM", "gN")}

                @block.gpsimd
                def _(g):
                    c.wait(g, "gM")
                    c.wait(g, "gN")
                barrier()

        peB_end_prev2 = [0]

        def emit_residual4(x_src, acc_src, ysc_src, dst_ap):
            with ExitStack() as pc:
                xt = [usb(pc, f"r4_x{i}", [128, D], F32) for i in range(2)]
                at = [usb(pc, f"r4_a{i}", [128, D], F32) for i in range(2)]
                y0t = [usb(pc, f"r4_y0{i}", [128, D], F32) for i in range(2)]
                y1t = [usb(pc, f"r4_y1{i}", [128, D], F32) for i in range(2)]
                so = [usb(pc, f"r4_s{i}", [128, D], F32) for i in range(2)]
                b = c.snap()
                NDV4 = 3

                @block.sync
                def _(sy, b=b):
                    for u in range(NT):
                        xp = "ldA" if u % 2 == 0 else "ldB"
                        ap_ = "ldC" if u % 2 == 0 else "ldD"
                        y0p = "stC" if u % 2 == 0 else "stD"
                        y1p = "ldE" if u % 2 == 0 else "ldF"
                        if u >= 2:
                            sy.wait_ge(sem["dve"], b["dve"] + NDV4 * (u - 1))
                        c.inc(sy.dma_start(out=xt[u % 2][:], in_=x_src[u * 128:(u + 1) * 128, :]), xp, 16)
                        c.inc(sy.dma_start(out=at[u % 2][:], in_=acc_src[u * 128:(u + 1) * 128, :]), ap_, 16)
                        c.inc(sy.dma_start(out=y0t[u % 2][:], in_=ysc_src[u * 128:(u + 1) * 128, :]), y0p, 16)
                        c.inc(sy.dma_start(out=y1t[u % 2][:], in_=ysc_src[T + u * 128:T + (u + 1) * 128, :]), y1p, 16)
                        if u >= 1:
                            stp = "stA" if (u - 1) % 2 == 0 else "stB"
                            sy.wait_ge(sem["dve"], b["dve"] + NDV4 * u)
                            c.inc(sy.dma_start(out=dst_ap[(u - 1) * 128:u * 128, :], in_=so[(u - 1) % 2][:]), stp, 16)
                    stp = "stA" if (NT - 1) % 2 == 0 else "stB"
                    sy.wait_ge(sem["dve"], b["dve"] + NDV4 * NT)
                    c.inc(sy.dma_start(out=dst_ap[(NT - 1) * 128:NT * 128, :], in_=so[(NT - 1) % 2][:]), stp, 16)
                    c.wait(sy, "stA")
                    c.wait(sy, "stB")

                @block.vector
                def _(v, b=b):
                    for u in range(NT):
                        xp = "ldA" if u % 2 == 0 else "ldB"
                        ap_ = "ldC" if u % 2 == 0 else "ldD"
                        y0p = "stC" if u % 2 == 0 else "stD"
                        y1p = "ldE" if u % 2 == 0 else "ldF"
                        for n in (xp, ap_, y0p, y1p):
                            v.wait_ge(sem[n], b[n] + 16 * (u // 2 + 1))
                        if u >= 2:
                            stp = "stA" if u % 2 == 0 else "stB"
                            v.wait_ge(sem[stp], b[stp] + 16 * (u // 2))
                        c.inc(v.tensor_tensor(so[u % 2][:], xt[u % 2][:], at[u % 2][:], ALU.add), "dve")
                        c.inc(v.tensor_tensor(so[u % 2][:], so[u % 2][:], y0t[u % 2][:], ALU.add), "dve")
                        c.inc(v.tensor_tensor(so[u % 2][:], so[u % 2][:], y1t[u % 2][:], ALU.add), "dve")
                barrier()

        done = [False]
        def chk(name):
            if stop_after is not None and name == stop_after:
                done[0] = True
            return done[0]

        emit_ln(x0, xn_d)
        if not chk("ln0"):
            with ExitStack() as l0c:
                qT_sb = usb(l0c, "qT_sb", [128, 4, T], BF16)
                with ExitStack() as l0i:
                    xnT_sb = usb(l0i, "xnT0", [128, ND, T], BF16)
                    emit_xnT(xn_d, xnT_sb)
                    if not chk("xnT0"):
                        emit_l0_attn(xnT_sb)
                    if not chk("attn0"):
                        emit_outproj(WTo_d, True, acc_d)
                    if not chk("proj0"):
                        emit_l0_swiglu(xnT_sb)
                    if not chk("swiglu0"):
                        emit_l0_peer_a(xnT_sb, qT_sb)
                    chk("peera")
                if not done[0]:
                    emit_l0_peer_b(qT_sb)
                chk("peerb")
            if not done[0]:
                emit_residual(x0, acc_d, x1_d)
            chk("res0")

        if not done[0]:
            emit_ln(x1_d, xn_d)
        if not chk("ln1"):
            with ExitStack() as l1c:
                g_sb = usb(l1c, "g_sb", [128, NT, 8], F32)
                s_sb = usb(l1c, "s_sb", [128, NT, 8], F32)
                with ExitStack() as l1i:
                    xnT_sb = usb(l1i, "xnT1", [128, ND, T], BF16)
                    emit_xnT(xn_d, xnT_sb)
                    if not chk("xnT1"):
                        emit_l1_attn_a(xnT_sb)
                    if not chk("attn1a"):
                        emit_l1_attn_b()
                    if not chk("attn1b"):
                        emit_outproj(Wout1, False, acc_d)
                    if not chk("proj1"):
                        emit_l1_router(xnT_sb, g_sb, s_sb)
                    if not chk("router1") and not sparse:
                        emit_l1_moe(xnT_sb, g_sb)
                if not done[0] and sparse:
                    emit_l1_moe_sparse(g_sb, s_sb)
                chk("moe1")
            if not done[0]:
                if sparse:
                    emit_residual4(x1_d, acc_d, ysc_d, out)
                else:
                    emit_residual(x1_d, acc_d, out)

        _finish(nc, block, c, dbg, dbg_map, dbg_out)

    return nc


def _finish(nc, block, c, dbg, dbg_map, dbg_out):
    sem = c.sem
    if dbg is not None:
        src_ap = dbg_map[dbg]
        @block.gpsimd
        def _(g):
            c.wait(g, "st")
            c.inc(g.dma_start(out=dbg_out.rearrange("(a p) b -> p a b", p=128),
                              in_=src_ap.rearrange("(a p) b -> p a b", p=128)), "g", 16)

    @block.sync
    def _(sy):
        c.wait(sy, "st")
        c.wait(sy, "ld")

    @block.gpsimd
    def _(g):
        c.wait(g, "g")
        c.wait(g, "st")


def kernel(**inputs):
    x_full = np.ascontiguousarray(inputs["L0_attn_x"])
    weights = {k: np.ascontiguousarray(v) for k, v in inputs.items() if k != "L0_attn_x"}
    nc = build_program()
    in_maps = []
    for core in range(8):
        m = {"x": np.ascontiguousarray(x_full[core])}
        m.update(weights)
        in_maps.append(m)
    res = run_bass_kernel_spmd(nc, in_maps, core_ids=list(range(8)))
    return np.stack([r["out"] for r in res.results], axis=0)


# version 07
# speedup vs baseline: 1.0028x; 1.0028x over previous
import numpy as np
import ml_dtypes
from contextlib import ExitStack

import concourse.bass as bass
import concourse.mybir as mybir
from concourse.bass_utils import run_bass_kernel_spmd

F32 = mybir.dt.float32
BF16 = mybir.dt.bfloat16
AF = mybir.ActivationFunctionType
ALU = mybir.AluOpType
AX = mybir.AxisListType

T = 4096
D = 1024
NT = T // 128
ND = D // 128
DCD = 4096
NEXP = 4096
PEER_H = 4
F1 = 4096
E1 = 8
NEG = -1.0e30


def _rope_tables():
    inv_freq = (1.0 / (500000.0 ** (np.arange(0, 64, 2, dtype=np.float32) / 64))).astype(np.float32)
    t = np.arange(T, dtype=np.float32)
    freqs = (t[:, None] * inv_freq[None, :]).astype(np.float32)
    return np.cos(freqs).astype(np.float32), np.sin(freqs).astype(np.float32)


def _diag_masks():
    kk = np.arange(128)[:, None]
    qq = np.arange(512)[None, :]
    ms = [np.where(qq >= o * 128 + kk, 0.0, NEG).astype(np.float32) for o in range(4)]
    ms.append(np.full((128, 512), NEG, np.float32))
    return np.stack(ms)


class Ctrs:
    def __init__(self, ctx, nc, names):
        self.sem = {}
        self.v = {}
        for n in names:
            self.sem[n] = ctx.enter_context(nc.semaphore(n))
            self.v[n] = 0

    def inc(self, instr, name, n=1):
        if name in ("dve", "act"):
            instr._wait_ge(self.sem[name], self.v[name])
        instr.then_inc(self.sem[name], n)
        self.v[name] += n
        return instr

    def wait(self, eng, name, val=None):
        eng.wait_ge(self.sem[name], self.v[name] if val is None else val)

    def snap(self):
        return dict(self.v)


def build_program(stop_after=None, dbg=None, single_core_debug=False, sparse=False):
    nc = bass.Bass("TRN2", target_bir_lowering=False, debug=False, dynamic_dma_scratch_size=16384)

    inp = {}
    def ein(name, shape):
        inp[name] = nc.dram_tensor(name, list(shape), F32, kind="ExternalInput").ap()
        return inp[name]

    x0 = ein("x", (T, D))
    Wq0 = ein("L0_attn_q_heads", (D, D))
    Wk0 = ein("L0_attn_k_heads", (D, D))
    Wv0 = ein("L0_attn_v_heads", (D, D))
    Wo0 = ein("L0_attn_output_proj", (D, D))
    Wg0 = ein("L0_ffn_Wg", (D, DCD))
    bg0 = ein("L0_ffn_bg", (DCD,))
    Wu0 = ein("L0_ffn_Wu", (D, DCD))
    bu0 = ein("L0_ffn_bu", (DCD,))
    Wd0 = ein("L0_ffn_Wd", (DCD, D))
    bd0 = ein("L0_ffn_bd", (D,))
    Wqp = ein("L0_ffn_Wq", (D, 512))
    keys0 = ein("L0_ffn_keys", (4, 64, 2, 64))
    dem0 = ein("L0_ffn_down_embed", (NEXP, D))
    uem0 = ein("L0_ffn_up_embed", (NEXP, D))
    Wqkv1 = ein("L1_attn_W_qkv", (D, 3 * D))
    Wout1 = ein("L1_attn_W_out", (D, D))
    Wg1 = ein("L1_ffn_Wg", (E1, D))
    W11 = ein("L1_ffn_W1", (E1, D, F1))
    W21 = ein("L1_ffn_W2", (E1, F1, D))

    out = nc.dram_tensor("out", [T, D], F32, kind="ExternalOutput").ap()

    def scratch(name, shape, dtype):
        return nc.dram_tensor(name, list(shape), dtype, kind="Internal").ap()

    xn_d = scratch("xn_d", (T, D), BF16)
    wcast_d = scratch("wcast_d", (DCD, D), BF16)
    WTq_d = scratch("WTq_d", (D, D), BF16)
    WTk_d = scratch("WTk_d", (D, D), BF16)
    WTv_d = scratch("WTv_d", (D, D), BF16)
    WTo_d = scratch("WTo_d", (D, D), BF16)
    deT_d = scratch("deT_d", (D, NEXP), BF16)
    wgb_d = scratch("wgb_d", (D, DCD), BF16)
    wub_d = scratch("wub_d", (D, DCD), BF16)
    wdb_d = scratch("wdb_d", (DCD, D), BF16)
    wall_d = scratch("wall_d", (T, NEXP), BF16)
    acc_d = scratch("acc_d", (T, D), F32)
    x1_d = scratch("x1_d", (T, D), F32)
    aoT_d = scratch("aoT_d", (D, T), BF16)
    qfT_d = scratch("qfT_d", (D, T), BF16)
    w2b_d = scratch("w2b_d", (E1, F1, D), BF16)
    CAP = 1536
    cx_d = scratch("cx_d", (E1 * CAP, D), BF16)
    cmeta_d = scratch("cmeta_d", (E1 * CAP, 8), F32)
    pos_d = scratch("pos_d", (8, T), F32)
    ysc_d = scratch("ysc_d", (2 * T, D), F32)

    dbg_map = {"xn_d": xn_d, "acc_d": acc_d, "x1_d": x1_d, "aoT_d": aoT_d,
               "wall_d": wall_d, "qfT_d": qfT_d, "WTq_d": WTq_d, "deT_d": deT_d,
               "WTo_d": WTo_d, "ysc_d": ysc_d}
    dbg_out = None
    if dbg is not None:
        dbg_out = nc.dram_tensor("dbg", list(dbg_map[dbg].shape), F32,
                                 kind="ExternalOutput").ap()

    cos_np, sin_np = _rope_tables()
    cosb = nc.inline_tensor(cos_np.astype(ml_dtypes.bfloat16), "cosb").ap()
    sinb = nc.inline_tensor(sin_np.astype(ml_dtypes.bfloat16), "sinb").ap()
    msinb = nc.inline_tensor((-sin_np).astype(ml_dtypes.bfloat16), "msinb").ap()
    idb_c = nc.inline_tensor(np.eye(128, dtype=np.float32).astype(ml_dtypes.bfloat16), "idb").ap()
    idf_c = nc.inline_tensor(np.eye(128, dtype=np.float32), "idf").ap()
    onesb_c = nc.inline_tensor(np.ones((128, 2), np.float32).astype(ml_dtypes.bfloat16), "onesb").ap()
    onesrb_c = nc.inline_tensor(np.ones((1, 128), np.float32).astype(ml_dtypes.bfloat16), "onesrb").ap()
    onesrf_c = nc.inline_tensor(np.ones((1, 128), np.float32), "onesrf").ap()
    dmask_c = nc.inline_tensor(_diag_masks(), "dmask").ap()
    epsc_c = nc.inline_tensor(np.tile(np.array([[1e-5, 1e-6]], np.float32), (128, 1)), "epsc").ap()
    tid_np = (np.arange(NT)[None, :] * 128 + np.arange(128)[:, None]).astype(np.float32)
    tid_c = nc.inline_tensor(tid_np, "tid_c").ap()

    ctx = ExitStack()
    with ctx:
        c = Ctrs(ctx, nc, ["ld", "ldA", "ldB", "ldC", "ldD", "ldE", "ldF", "g", "gA", "gB", "gC", "gD", "gE", "gF", "gG", "gH", "gI", "gJ", "gK", "gL", "gM", "gN", "g1", "g2", "pe", "dve", "act", "st", "stA", "stB", "stC", "stD"])
        sem = c.sem

        def sb(name, shape, dtype):
            return ctx.enter_context(nc.sbuf_tensor(name, list(shape), dtype))

        _uid = [0]
        def usb(pc, name, shape, dtype):
            _uid[0] += 1
            return pc.enter_context(nc.sbuf_tensor(f"{name}_{_uid[0]}", list(shape), dtype))

        keysT_sb = sb("keysT_sb", (128, 8, 64), BF16)
        wgT_sb = sb("wgT_sb", (128, ND, 8), BF16)
        id_sb = sb("id_sb", (128, 128), BF16)
        idf_sb = sb("idf_sb", (128, 128), F32)
        ones_sb = sb("ones_sb", (128, 2), BF16)
        onesr_sb = sb("onesr_sb", (1, 128), BF16)
        onesrf_sb = sb("onesrf_sb", (1, 128), F32)
        cos_sb = sb("cos_sb", (128, NT, 32), BF16)
        sin_sb = sb("sin_sb", (128, NT, 32), BF16)
        msin_sb = sb("msin_sb", (128, NT, 32), BF16)
        dm_sb = sb("dm_sb", (128, 5, 512), F32)
        bdb_sb = sb("bdb_sb", (128, ND, 128), F32)
        bg_sb = sb("bg_sb", (128, 32), F32)
        eps_sb = sb("eps_sb", (128, 2), F32)
        tid_sb = sb("tid_sb", (128, NT), F32)
        bu_sb = sb("bu_sb", (128, 32), F32)

        pq = [ctx.enter_context(nc.psum_tensor(f"pq{i}", [128, 1024], F32)) for i in range(2)]
        pb45 = [ctx.enter_context(nc.psum_tensor(f"pb{i}", [128, 512], F32)) for i in range(2)]
        pbt = [ctx.enter_context(nc.psum_tensor(f"pbt{i}", [128, 1024], BF16)) for i in range(2)]
        pb = [pq[0].ap()[:, 0:512], pq[0].ap()[:, 512:1024],
              pq[1].ap()[:, 0:512], pq[1].ap()[:, 512:1024],
              pb45[0].ap(), pb45[1].ap()]

        block = ctx.enter_context(nc.Block())

        _last_bar = [None]

        def barrier():
            cur = c.snap()
            prev = _last_bar[0]
            changed = [n for n in cur
                       if (prev is None or cur[n] != prev[n]) and n != "g1"]
            def mk(e, changed=changed, cur=cur):
                for n in changed:
                    e.wait_ge(sem[n], cur[n])
            block.tensor(mk)
            block.vector(mk)
            block.scalar(mk)
            block.sync(mk)
            block.gpsimd(mk)
            _last_bar[0] = cur

        with ExitStack() as s0:
            tstage = s0.enter_context(nc.sbuf_tensor("tstage", [128, DCD], BF16))
            kload = s0.enter_context(nc.sbuf_tensor("kload", [64, 8, 64], BF16))
            wgld = s0.enter_context(nc.sbuf_tensor("wgld", [8, D], BF16))
            bsm = s0.enter_context(nc.sbuf_tensor("bsm", [32, 128], F32))
            brow = s0.enter_context(nc.sbuf_tensor("brow", [1, D], F32))

            @block.sync
            def _(sy):
                for dst, src in [(id_sb[:], idb_c), (idf_sb[:], idf_c),
                                 (ones_sb[:], onesb_c), (onesr_sb[:], onesrb_c),
                                 (onesrf_sb[:], onesrf_c)]:
                    c.inc(sy.dma_start(out=dst, in_=src), "ld", 16)
                c.inc(sy.dma_start(out=cos_sb[:], in_=cosb.rearrange("(a p) b -> p a b", p=128)), "ld", 16)
                c.inc(sy.dma_start(out=sin_sb[:], in_=sinb.rearrange("(a p) b -> p a b", p=128)), "ld", 16)
                c.inc(sy.dma_start(out=msin_sb[:], in_=msinb.rearrange("(a p) b -> p a b", p=128)), "ld", 16)
                c.inc(sy.dma_start(out=dm_sb[:], in_=dmask_c.rearrange("a p b -> p a b")), "ld", 16)
                c.inc(sy.dma_start(out=bsm[:], in_=bg0.rearrange("(a b) -> a b", b=128)), "ld", 16)
                c.inc(sy.dma_start(out=brow[:], in_=bd0.rearrange("(a b) -> a b", a=1)), "ld", 16)
                c.inc(sy.dma_start(out=eps_sb[:], in_=epsc_c), "ld", 16)
                c.inc(sy.dma_start(out=tid_sb[:], in_=tid_c), "ld", 16)

            for wi, (w_ap, wt_ap, R) in enumerate(
                    [(Wq0, WTq_d, D), (Wk0, WTk_d, D), (Wv0, WTv_d, D),
                     (Wo0, WTo_d, D), (dem0, deT_d, NEXP)]):
                @block.gpsimd
                def _(g, w_ap=w_ap, R=R):
                    c.wait(g, "ld")
                    c.inc(g.dma_start(out=wcast_d[0:R, :].rearrange("(a p) b -> p a b", p=128),
                                      in_=w_ap.rearrange("(a p) b -> p a b", p=128)), "g", 16)
                @block.sync
                def _(sy, wt_ap=wt_ap, R=R):
                    c.wait(sy, "g")
                    for ch in range(ND):
                        c.inc(sy.dma_start(out=tstage[:, 0:R], in_=wcast_d[0:R, ch * 128:(ch + 1) * 128],
                                           transpose=True), "ld", 16)
                        c.wait(sy, "ld")
                        c.inc(sy.dma_start(out=wt_ap[ch * 128:(ch + 1) * 128, :], in_=tstage[:, 0:R]), "st", 16)
                        c.wait(sy, "st")

            @block.gpsimd
            def _(g):
                for h in range(4):
                    for p in range(2):
                        c.inc(g.dma_start(out=kload[:, h * 2 + p, :], in_=keys0[h, :, p, :]), "g2", 16)
                c.inc(g.dma_start(out=wgld[:], in_=Wg1), "g2", 16)

            b = c.snap()
            @block.tensor
            def _(t, b=b):
                c.wait(t, "g2")
                c.wait(t, "ld")
                for i in range(8):
                    c.inc(t.transpose(pbt[0][0:64, i * 64:(i + 1) * 64], kload[:, i, :],
                                      id_sb[0:64, 0:64]), "pe")
                for dc in range(ND):
                    c.inc(t.transpose(pbt[1][0:128, dc * 8:(dc + 1) * 8],
                                      wgld[0:8, dc * 128:(dc + 1) * 128], id_sb[0:8, 0:8]), "pe")
                c.inc(t.transpose(pb[2][0:128, 0:32], bsm[:], idf_sb[0:32, 0:32]), "pe")
                for half in range(2):
                    if half == 1:
                        t.wait_ge(sem["dve"], b["dve"] + 29)
                    c.inc(t.matmul(pb[3][:, 0:512], onesrf_sb[:],
                                   brow[:, half * 512:(half + 1) * 512], start=True, stop=True), "pe")

            @block.vector
            def _(v, b=b):
                v.wait_ge(sem["pe"], b["pe"] + 8)
                for i in range(8):
                    c.inc(v.tensor_copy(keysT_sb[0:64, i, :], pbt[0][0:64, i * 64:(i + 1) * 64]), "dve")
                    c.inc(v.tensor_copy(keysT_sb[64:128, i, :], pbt[0][0:64, i * 64:(i + 1) * 64]), "dve")
                v.wait_ge(sem["pe"], b["pe"] + 16)
                for dc in range(ND):
                    c.inc(v.tensor_copy(wgT_sb[:, dc, :], pbt[1][0:128, dc * 8:(dc + 1) * 8]), "dve")
                v.wait_ge(sem["pe"], b["pe"] + 17)
                c.inc(v.tensor_copy(bg_sb[:], pb[2][0:128, 0:32]), "dve")
                v.wait_ge(sem["pe"], b["pe"] + 18)
                for j in range(4):
                    c.inc(v.tensor_copy(bdb_sb[:, j, :], pb[3][:, j * 128:(j + 1) * 128]), "dve")
                v.wait_ge(sem["pe"], b["pe"] + 19)
                for j in range(4):
                    c.inc(v.tensor_copy(bdb_sb[:, 4 + j, :], pb[3][:, j * 128:(j + 1) * 128]), "dve")
            barrier()

            b = c.snap()
            @block.sync
            def _(sy, b=b):
                c.inc(sy.dma_start(out=bsm[:], in_=bu0.rearrange("(a b) -> a b", b=128)), "ld", 16)
            @block.tensor
            def _(t, b=b):
                t.wait_ge(sem["ld"], b["ld"] + 16)
                c.inc(t.transpose(pb[2][0:128, 0:32], bsm[:], idf_sb[0:32, 0:32]), "pe")
            @block.vector
            def _(v, b=b):
                v.wait_ge(sem["pe"], b["pe"] + 1)
                c.inc(v.tensor_copy(bu_sb[:], pb[2][0:128, 0:32]), "dve")
            barrier()

        if stop_after == "S0":
            _finish(nc, block, c, dbg, dbg_map, dbg_out)
            return nc

        @block.gpsimd
        def _(g):
            for e in range(E1):
                c.inc(g.dma_start(out=w2b_d[e].rearrange("(a p) b -> p a b", p=128),
                                  in_=W21[e].rearrange("(a p) b -> p a b", p=128)), "g1", 16)
            for dst, src in [(wgb_d, Wg0), (wub_d, Wu0), (wdb_d, Wd0)]:
                c.inc(g.dma_start(out=dst.rearrange("(a p) b -> p a b", p=128),
                                  in_=src.rearrange("(a p) b -> p a b", p=128)), "g1", 16)
        g1_ready = c.v["g1"]

        def emit_ln(x_ap, xn_dst):
            with ExitStack() as pc:
                xt = [usb(pc, f"ln_xt{i}", [128, D], F32) for i in range(2)]
                xc = [usb(pc, f"ln_xc{i}", [128, D], F32) for i in range(2)]
                xnb = [usb(pc, f"ln_xn{i}", [128, D], BF16) for i in range(2)]
                sq = usb(pc, "ln_sq", [128, D], F32)
                stt = usb(pc, "ln_st", [128, 8], F32)
                b = c.snap()
                NDV = 5

                @block.sync
                def _(sy, b=b):
                    for u in range(NT):
                        ldp = "ldA" if u % 2 == 0 else "ldB"
                        if u >= 2:
                            sy.wait_ge(sem["dve"], b["dve"] + NDV * (u - 2) + 3)
                        c.inc(sy.dma_start(out=xt[u % 2][:], in_=x_ap[u * 128:(u + 1) * 128, :]), ldp, 16)
                        if u >= 1:
                            stp = "stA" if (u - 1) % 2 == 0 else "stB"
                            sy.wait_ge(sem["dve"], b["dve"] + NDV * u)
                            c.inc(sy.dma_start(out=xn_dst[(u - 1) * 128:u * 128, :], in_=xnb[(u - 1) % 2][:]), stp, 16)
                    stp = "stA" if (NT - 1) % 2 == 0 else "stB"
                    sy.wait_ge(sem["dve"], b["dve"] + NDV * NT)
                    c.inc(sy.dma_start(out=xn_dst[(NT - 1) * 128:NT * 128, :], in_=xnb[(NT - 1) % 2][:]), stp, 16)
                    c.wait(sy, "stA")
                    c.wait(sy, "stB")

                @block.vector
                def _(v, b=b):
                    for u in range(NT):
                        ldp = "ldA" if u % 2 == 0 else "ldB"
                        v.wait_ge(sem[ldp], b[ldp] + 16 * (u // 2 + 1))
                        c.inc(v.reduce_sum(stt[:, 0:1], xt[u % 2][:], axis=AX.X), "dve")
                        c.inc(v.tensor_scalar(stt[:, 1:2], stt[:, 0:1], -1.0 / D, None, ALU.mult), "dve")
                        c.inc(v.tensor_scalar(xc[u % 2][:], xt[u % 2][:], stt[:, 1:2], None, ALU.add), "dve")
                        v.wait_ge(sem["act"], b["act"] + 2 * (u + 1))
                        c.inc(v.reciprocal(stt[:, 4:5], stt[:, 3:4]), "dve")
                        if u >= 2:
                            stp = "stA" if u % 2 == 0 else "stB"
                            v.wait_ge(sem[stp], b[stp] + 16 * (u // 2))
                        c.inc(v.tensor_scalar(xnb[u % 2][:], xc[u % 2][:], stt[:, 4:5], None, ALU.mult), "dve")

                @block.scalar
                def _(s, b=b):
                    for u in range(NT):
                        s.wait_ge(sem["dve"], b["dve"] + NDV * u + 3)
                        c.inc(s.activation(out=sq[:], in_=xc[u % 2][:], func=AF.Square,
                                           accum_out=stt[:, 2:3]), "act")
                        c.inc(s.activation(out=stt[:, 3:4], in_=stt[:, 2:3], func=AF.Sqrt,
                                           bias=eps_sb[:, 0:1], scale=1.0 / D), "act")
                barrier()

        def emit_xnT(xn_src, xnT_sb):
            b = c.snap()
            @block.sync
            def _(sy, b=b):
                for ch in range(ND):
                    c.inc(sy.dma_start(out=xnT_sb[:, ch, :], in_=xn_src[:, ch * 128:(ch + 1) * 128],
                                       transpose=True), "ld", 16)
                c.wait(sy, "ld")
            barrier()

        def emit_outproj(w_src_ap, w_pretransposed, dst_ap):
            with ExitStack() as pc:
                wsl = usb(pc, "op_w", [128, ND, D], BF16)
                at = [usb(pc, f"op_a{i}", [128, ND, 128], BF16) for i in range(2)]
                ot = [usb(pc, f"op_o{i}", [128, 512], F32) for i in range(4)]
                b = c.snap()

                if w_pretransposed:
                    @block.sync
                    def _(sy, b=b):
                        c.inc(sy.dma_start(out=wsl[:], in_=w_src_ap.rearrange("(a p) b -> p a b", p=128)), "ld", 16)
                else:
                    @block.gpsimd
                    def _(g, b=b):
                        c.inc(g.dma_start(out=wsl[:], in_=w_src_ap.rearrange("(a p) b -> p a b", p=128)), "g", 16)

                wofs = 16 if w_pretransposed else 0

                @block.sync
                def _(sy, b=b):
                    for u in range(NT):
                        ldp = "ldA" if u % 2 == 0 else "ldB"
                        if u >= 2:
                            sy.wait_ge(sem["pe"], b["pe"] + 16 * (u - 1))
                        c.inc(sy.dma_start(
                            out=at[u % 2][:],
                            in_=aoT_d[:, u * 128:(u + 1) * 128].rearrange("(a p) t -> p a t", p=128)),
                            ldp, 16)

                @block.tensor
                def _(t, b=b):
                    if not w_pretransposed:
                        t.wait_ge(sem["g"], b["g"] + 16)
                    else:
                        t.wait_ge(sem["ld"], b["ld"] + 16)
                    for u in range(NT):
                        ldp = "ldA" if u % 2 == 0 else "ldB"
                        t.wait_ge(sem[ldp], b[ldp] + 16 * (u // 2 + 1))
                        for fc in range(2):
                            k = 2 * u + fc
                            if k >= 4:
                                t.wait_ge(sem["dve"], b["dve"] + k - 3)
                            for vc in range(ND):
                                c.inc(t.matmul(pb[k % 4][:, 0:512], at[u % 2][:, vc, :],
                                               wsl[:, vc, fc * 512:(fc + 1) * 512],
                                               start=(vc == 0), stop=(vc == ND - 1)), "pe")

                STP = ["gA", "gB", "gC", "gD"]

                @block.vector
                def _(v, b=b):
                    for u in range(NT):
                        for fc in range(2):
                            k = 2 * u + fc
                            v.wait_ge(sem["pe"], b["pe"] + 8 * (k + 1))
                            if k >= 4:
                                v.wait_ge(sem[STP[k % 4]], b[STP[k % 4]] + 16 * (k // 4))
                            c.inc(v.tensor_copy(ot[k % 4][:], pb[k % 4][:, 0:512]), "dve")

                @block.gpsimd
                def _(g, b=b):
                    for u in range(NT):
                        for fc in range(2):
                            k = 2 * u + fc
                            g.wait_ge(sem["dve"], b["dve"] + k + 1)
                            c.inc(g.dma_start(out=dst_ap[u * 128:(u + 1) * 128, fc * 512:(fc + 1) * 512],
                                              in_=ot[k % 4][:]), STP[k % 4], 16)
                    for n in STP:
                        c.wait(g, n)
                barrier()

        def emit_l0_attn(xnT_sb):
            with ExitStack() as pc:
                wt_sb = usb(pc, "a0_wt", [128, ND, 128], BF16)
                qtA = usb(pc, "a0_qA", [65, T], BF16)
                qtB = usb(pc, "a0_qB", [65, T], BF16)
                ktA = usb(pc, "a0_kA", [65, T], BF16)
                ktB = usb(pc, "a0_kB", [65, T], BF16)
                vtA = usb(pc, "a0_vA", [128, NT, 65], BF16)
                vtB = usb(pc, "a0_vB", [128, NT, 65], BF16)
                ptil = [usb(pc, f"a0_p{i}", [128, 1024], BF16) for i in range(2)]
                sqs = usb(pc, "a0_sq", [64, 512], BF16)
                kmax = usb(pc, "a0_km", [1, 4], F32)
                qnr = usb(pc, "a0_qn", [1, 512], F32)
                rr = usb(pc, "a0_rr", [1, 512], F32)
                rrb = usb(pc, "a0_rrb", [1, 512], BF16)
                rsb = usb(pc, "a0_rsb", [64, 512], F32)
                aop = usb(pc, "a0_ao", [128, T], BF16)

                for pair in range(8):
                    b = c.snap()
                    @block.sync
                    def _(sy, pair=pair, b=b):
                        for wi, wt_d in enumerate([WTq_d, WTk_d, WTv_d]):
                            if wi > 0:
                                sy.wait_ge(sem["pe"], b["pe"] + 64 * wi)
                            c.inc(sy.dma_start(
                                out=wt_sb[:],
                                in_=wt_d[:, pair * 128:(pair + 1) * 128].rearrange("(a p) f -> p a f", p=128)),
                                "ld", 16)

                    @block.tensor
                    def _(t, pair=pair, b=b):
                        for u in range(16):
                            wi, sp = divmod(u, 8)
                            t.wait_ge(sem["ld"], b["ld"] + 16 * (wi + 1))
                            if u >= 2:
                                t.wait_ge(sem["dve"], b["dve"] + 2 * (u - 1))
                            for dc in range(ND):
                                c.inc(t.matmul(pb[u % 2][:, 0:512], wt_sb[:, dc, :],
                                               xnT_sb[:, dc, sp * 512:(sp + 1) * 512],
                                               start=(dc == 0), stop=(dc == ND - 1)), "pe")
                        for tt in range(NT):
                            u = 16 + tt
                            t.wait_ge(sem["ld"], b["ld"] + 48)
                            t.wait_ge(sem["dve"], b["dve"] + 2 * (u - 1))
                            for dc in range(ND):
                                c.inc(t.matmul(pb[u % 2][:, 0:128], xnT_sb[:, dc, tt * 128:(tt + 1) * 128],
                                               wt_sb[:, dc, :],
                                               start=(dc == 0), stop=(dc == ND - 1)), "pe")

                    @block.vector
                    def _(v, pair=pair, b=b):
                        for u in range(16):
                            wi, sp = divmod(u, 8)
                            dA = qtA if wi == 0 else ktA
                            dB = qtB if wi == 0 else ktB
                            v.wait_ge(sem["pe"], b["pe"] + 8 * (u + 1))
                            c.inc(v.tensor_copy(dA[0:64, sp * 512:(sp + 1) * 512], pb[u % 2][0:64, 0:512]), "dve")
                            c.inc(v.tensor_copy(dB[0:64, sp * 512:(sp + 1) * 512], pb[u % 2][64:128, 0:512]), "dve")
                        for tt in range(NT):
                            u = 16 + tt
                            v.wait_ge(sem["pe"], b["pe"] + 8 * (u + 1))
                            c.inc(v.tensor_copy(vtA[:, tt, 0:64], pb[u % 2][:, 0:64]), "dve")
                            c.inc(v.tensor_copy(vtB[:, tt, 0:64], pb[u % 2][:, 64:128]), "dve")
                        c.inc(v.memset(vtA[:, :, 64:65], 1.0), "dve")
                        c.inc(v.memset(vtB[:, :, 64:65], 1.0), "dve")
                        c.inc(v.memset(ktA[64:65, :], 1.0), "dve")
                        c.inc(v.memset(ktB[64:65, :], 1.0), "dve")
                    barrier()

                    for hh in range(2):
                        head = pair * 2 + hh
                        QT = qtA if hh == 0 else qtB
                        KT = ktA if hh == 0 else ktB
                        VT = vtA if hh == 0 else vtB

                        b = c.snap()
                        @block.scalar
                        def _(s, KT=KT, b=b):
                            for sp in range(8):
                                if sp >= 1:
                                    s.wait_ge(sem["pe"], b["pe"] + sp)
                                c.inc(s.activation(out=sqs[:], in_=KT[0:64, sp * 512:(sp + 1) * 512],
                                                   func=AF.Square), "act")
                        @block.tensor
                        def _(t, b=b):
                            for sp in range(8):
                                t.wait_ge(sem["act"], b["act"] + sp + 1)
                                if sp >= 1:
                                    t.wait_ge(sem["dve"], b["dve"] + 3 * (sp - 1) + 1)
                                c.inc(t.matmul(pb[5][0:1, 0:512], ones_sb[0:64, 0:1], sqs[:],
                                               start=True, stop=True), "pe")
                        @block.vector
                        def _(v, b=b):
                            for sp in range(8):
                                v.wait_ge(sem["pe"], b["pe"] + sp + 1)
                                c.inc(v.tensor_copy(qnr[:], pb[5][0:1, 0:512]), "dve")
                                c.inc(v.reduce_max(kmax[:, 2:3], qnr[:], axis=AX.X), "dve")
                                if sp == 0:
                                    c.inc(v.tensor_copy(kmax[:, 0:1], kmax[:, 2:3]), "dve")
                                else:
                                    c.inc(v.tensor_tensor(kmax[:, 0:1], kmax[:, 0:1], kmax[:, 2:3],
                                                          ALU.max), "dve")
                        @block.scalar
                        def _(s, b=b):
                            s.wait_ge(sem["dve"], b["dve"] + 24)
                            c.inc(s.activation(out=kmax[:, 1:2], in_=kmax[:, 0:1], func=AF.Sqrt), "act")
                        barrier()

                        b = c.snap()
                        @block.scalar
                        def _(s, QT=QT, b=b):
                            for sp in range(8):
                                if sp >= 1:
                                    s.wait_ge(sem["pe"], b["pe"] + sp)
                                c.inc(s.activation(out=sqs[:], in_=QT[0:64, sp * 512:(sp + 1) * 512],
                                                   func=AF.Square), "act")
                                s.wait_ge(sem["pe"], b["pe"] + sp + 1)
                                if sp >= 1:
                                    s.wait_ge(sem["dve"], b["dve"] + sp)
                                c.inc(s.activation(out=qnr[:], in_=pb[5][0:1, 0:512], func=AF.Sqrt), "act")
                        @block.tensor
                        def _(t, b=b):
                            for sp in range(8):
                                t.wait_ge(sem["act"], b["act"] + 2 * sp + 1)
                                c.inc(t.matmul(pb[5][0:1, 0:512], ones_sb[0:64, 0:1], sqs[:],
                                               start=True, stop=True), "pe")
                        @block.vector
                        def _(v, QT=QT, b=b):
                            for sp in range(8):
                                v.wait_ge(sem["act"], b["act"] + 2 * (sp + 1))
                                c.inc(v.tensor_scalar(QT[64:65, sp * 512:(sp + 1) * 512], qnr[:],
                                                      kmax[:, 1:2], -0.5, ALU.mult, ALU.mult), "dve")
                        barrier()

                        b = c.snap()
                        units = []
                        for qs2 in range(4):
                            for kb in range(8 * qs2 + 8):
                                units.append((qs2, kb))
                        U = len(units)
                        is_last = [kb == 8 * qs2 + 7 for (qs2, kb) in units]
                        odiag = [kb - 8 * qs2 for (qs2, kb) in units]
                        span_last_idx = {}
                        for i, (qs2, kb) in enumerate(units):
                            if is_last[i]:
                                span_last_idx[qs2] = i
                        pe_ops = []
                        for i in range(U):
                            pe_ops.append(("za", i))
                            pe_ops.append(("zb", i))
                            if i >= 1:
                                pe_ops.append(("ava", i - 1))
                                pe_ops.append(("avb", i - 1))
                                if is_last[i - 1]:
                                    pe_ops.append(("r0", units[i - 1][0]))
                                    pe_ops.append(("r1", units[i - 1][0]))
                        pe_ops += [("ava", U - 1), ("avb", U - 1),
                                   ("r0", units[U - 1][0]), ("r1", units[U - 1][0])]
                        pe_pos = {op: idx + 1 for idx, op in enumerate(pe_ops)}
                        dve_mask_end = {}
                        dve_evac_base = {}
                        d = 0
                        for i, (qs2, kb) in enumerate(units):
                            o = odiag[i]
                            nm = 0 if o < 0 else (1 if o <= 3 else 2)
                            d += nm
                            dve_mask_end[i] = d
                            if is_last[i]:
                                dve_evac_base[qs2] = d
                                d += 8

                        @block.tensor
                        def _(t, QT=QT, KT=KT, VT=VT, b=b, units=units, pe_pos=pe_pos,
                              dve_mask_end=dve_mask_end, dve_evac_base=dve_evac_base,
                              is_last=is_last, span_last_idx=span_last_idx):
                            rev_last = {span_last_idx[s_]: s_ for s_ in span_last_idx}
                            for op in pe_ops:
                                kind, j = op
                                if kind in ("za", "zb"):
                                    i = j
                                    qs2, kb = units[i]
                                    if kind == "za":
                                        if i >= 2:
                                            t.wait_ge(sem["act"], b["act"] + (i - 1))
                                            if (i - 2) in rev_last:
                                                t.wait_ge(sem["dve"], b["dve"] +
                                                          dve_evac_base[rev_last[i - 2]] + 7)
                                        half = pq[i % 2].ap()[:, 0:512]
                                        qsl = QT[:, qs2 * 1024:qs2 * 1024 + 512]
                                    else:
                                        half = pq[i % 2].ap()[:, 512:1024]
                                        qsl = QT[:, qs2 * 1024 + 512:qs2 * 1024 + 1024]
                                    c.inc(t.matmul(half, KT[:, kb * 128:(kb + 1) * 128], qsl,
                                                   start=True, stop=True), "pe")
                                elif kind in ("ava", "avb"):
                                    i = j
                                    qs2, kb = units[i]
                                    t.wait_ge(sem["act"], b["act"] + i + 1)
                                    if kb == 0 and qs2 >= 1 and kind == "ava":
                                        t.wait_ge(sem["dve"], b["dve"] + dve_evac_base[qs2 - 1] + 8)
                                    dst = pb45[0] if kind == "ava" else pb45[1]
                                    rhs = ptil[i % 2][:, 0:512] if kind == "ava" else ptil[i % 2][:, 512:1024]
                                    c.inc(t.matmul(dst.ap()[0:65, 0:512], VT[:, kb, :], rhs,
                                                   start=(kb == 0), stop=is_last[i]), "pe")
                                else:
                                    qs2 = j
                                    il = span_last_idx[qs2]
                                    off = 2 if kind == "r0" else 6
                                    t.wait_ge(sem["dve"], b["dve"] + dve_evac_base[qs2] + off)
                                    c.inc(t.matmul(pq[il % 2].ap()[0:64, 0:512],
                                                   onesr_sb[0:1, 0:64], rrb[:],
                                                   start=True, stop=True), "pe")

                        @block.vector
                        def _(v, b=b, units=units, pe_pos=pe_pos, dve_evac_base=dve_evac_base,
                              is_last=is_last, span_last_idx=span_last_idx, head=head):
                            for i, (qs2, kb) in enumerate(units):
                                o = odiag[i]
                                if o >= 0:
                                    v.wait_ge(sem["pe"], b["pe"] + pe_pos[("zb", i)])
                                    if o <= 3:
                                        c.inc(v.tensor_tensor(pq[i % 2].ap()[:, 0:512],
                                                              pq[i % 2].ap()[:, 0:512],
                                                              dm_sb[:, o, :], ALU.add), "dve")
                                    else:
                                        c.inc(v.tensor_tensor(pq[i % 2].ap()[:, 0:512],
                                                              pq[i % 2].ap()[:, 0:512],
                                                              dm_sb[:, 4, :], ALU.add), "dve")
                                        c.inc(v.tensor_tensor(pq[i % 2].ap()[:, 512:1024],
                                                              pq[i % 2].ap()[:, 512:1024],
                                                              dm_sb[:, o - 4, :], ALU.add), "dve")
                                if is_last[i]:
                                    il = i
                                    r0 = (head % 2) * 64
                                    v.wait_ge(sem["pe"], b["pe"] + pe_pos[("ava", i)])
                                    c.inc(v.reciprocal(rr[:], pb45[0].ap()[64:65, 0:512]), "dve")
                                    c.inc(v.tensor_copy(rrb[:], rr[:]), "dve")
                                    v.wait_ge(sem["pe"], b["pe"] + pe_pos[("r0", qs2)])
                                    c.inc(v.tensor_copy(rsb[:], pq[il % 2].ap()[0:64, 0:512]), "dve")
                                    c.inc(v.tensor_tensor(aop[r0:r0 + 64, qs2 * 1024:qs2 * 1024 + 512],
                                                          pb45[0].ap()[0:64, 0:512], rsb[:], ALU.mult), "dve")
                                    v.wait_ge(sem["pe"], b["pe"] + pe_pos[("avb", i)])
                                    c.inc(v.reciprocal(rr[:], pb45[1].ap()[64:65, 0:512]), "dve")
                                    c.inc(v.tensor_copy(rrb[:], rr[:]), "dve")
                                    v.wait_ge(sem["pe"], b["pe"] + pe_pos[("r1", qs2)])
                                    c.inc(v.tensor_copy(rsb[:], pq[il % 2].ap()[0:64, 0:512]), "dve")
                                    c.inc(v.tensor_tensor(aop[r0:r0 + 64,
                                                              qs2 * 1024 + 512:qs2 * 1024 + 1024],
                                                          pb45[1].ap()[0:64, 0:512], rsb[:], ALU.mult), "dve")

                        @block.scalar
                        def _(s, b=b, units=units, pe_pos=pe_pos, dve_mask_end=dve_mask_end):
                            for i, (qs2, kb) in enumerate(units):
                                s.wait_ge(sem["pe"], b["pe"] + pe_pos[("zb", i)])
                                if odiag[i] >= 0:
                                    s.wait_ge(sem["dve"], b["dve"] + dve_mask_end[i])
                                c.inc(s.activation(out=ptil[i % 2][:], in_=pq[i % 2].ap()[:],
                                                   func=AF.Exp, scale=0.125), "act")
                        barrier()

                    b = c.snap()
                    @block.sync
                    def _(sy, pair=pair, b=b):
                        c.inc(sy.dma_start(out=aoT_d[pair * 128:(pair + 1) * 128, :], in_=aop[:]), "st", 16)
                        c.wait(sy, "st")
                    barrier()

        def emit_l0_swiglu(xnT_sb):
            with ExitStack() as pc:
                wg2 = [usb(pc, f"s2_wg{i}", [128, ND, 256], BF16) for i in range(2)]
                wu2 = [usb(pc, f"s2_wu{i}", [128, ND, 256], BF16) for i in range(2)]
                wd2 = [usb(pc, f"s2_wd{i}", [128, 2, 512], BF16) for i in range(2)]
                gsil = usb(pc, "s2_g", [128, 512], BF16)
                upb = usb(pc, "s2_u", [128, 512], F32)
                prodT = usb(pc, "s2_p", [128, 32, 512], BF16)
                ot = [usb(pc, f"s2_o{i}", [128, 512], F32) for i in range(2)]

                peB_end_prev = [0]
                prev_evac_dve = [0]
                prevA = [None]
                wg_consumed = [0, 0]
                wu_consumed = [0, 0]
                wd_consumed = [0, 0]

                for sp in range(8):
                    bA = c.snap()
                    @block.sync
                    def _(sy, sp=sp, bA=bA):
                        if sp == 0:
                            sy.wait_ge(sem["g1"], g1_ready)
                        for fp in range(16):
                            k = fp % 2
                            gp = "gA" if k == 0 else "gB"
                            up = "gC" if k == 0 else "gD"
                            if fp < 2:
                                sy.wait_ge(sem["pe"], wg_consumed[k])
                            else:
                                sy.wait_ge(sem["pe"], bA["pe"] + 32 * (fp - 1))
                            c.inc(sy.dma_start(out=wg2[k][:],
                                               in_=wgb_d[:, fp * 256:(fp + 1) * 256].rearrange("(a p) f -> p a f", p=128)), gp, 16)
                            c.inc(sy.dma_start(out=wu2[k][:],
                                               in_=wub_d[:, fp * 256:(fp + 1) * 256].rearrange("(a p) f -> p a f", p=128)), up, 16)

                    @block.tensor
                    def _(t, sp=sp, bA=bA):
                        for fc in range(32):
                            fp = fc // 2
                            half = fc % 2
                            k = fp % 2
                            gp = "gA" if k == 0 else "gB"
                            up = "gC" if k == 0 else "gD"
                            if half == 0:
                                t.wait_ge(sem[gp], bA[gp] + 16 * (fp // 2 + 1))
                                t.wait_ge(sem[up], bA[up] + 16 * (fp // 2 + 1))
                            if fc == 0:
                                if prevA[0] is not None:
                                    t.wait_ge(sem["act"], prevA[0][0] + 32)
                                    t.wait_ge(sem["dve"], prevA[0][1] + 64)
                            else:
                                t.wait_ge(sem["act"], bA["act"] + fc)
                                t.wait_ge(sem["dve"], bA["dve"] + 2 * fc - 1)
                            for dc in range(ND):
                                c.inc(t.matmul(pb[0][:, 0:512], wg2[k][:, dc, half * 128:(half + 1) * 128],
                                               xnT_sb[:, dc, sp * 512:(sp + 1) * 512],
                                               start=(dc == 0), stop=(dc == ND - 1)), "pe")
                            for dc in range(ND):
                                c.inc(t.matmul(pb[1][:, 0:512], wu2[k][:, dc, half * 128:(half + 1) * 128],
                                               xnT_sb[:, dc, sp * 512:(sp + 1) * 512],
                                               start=(dc == 0), stop=(dc == ND - 1)), "pe")

                    wg_consumed[0] = bA["pe"] + 16 * (2 * 14 + 2)
                    wg_consumed[1] = bA["pe"] + 16 * (2 * 15 + 2)
                    wu_consumed[0] = wg_consumed[0]
                    wu_consumed[1] = wg_consumed[1]

                    @block.scalar
                    def _(s, sp=sp, bA=bA):
                        for fc in range(32):
                            if fc == 0:
                                s.wait_ge(sem["pe"], peB_end_prev[0])
                            s.wait_ge(sem["pe"], bA["pe"] + 16 * fc + 8)
                            if fc >= 1:
                                s.wait_ge(sem["dve"], bA["dve"] + 2 * fc)
                            c.inc(s.activation(out=gsil[:], in_=pb[0][:, 0:512], func=AF.Silu,
                                               bias=bg_sb[:, fc:fc + 1]), "act")

                    @block.vector
                    def _(v, sp=sp, bA=bA):
                        for fc in range(32):
                            if fc == 0:
                                v.wait_ge(sem["pe"], peB_end_prev[0])
                            v.wait_ge(sem["pe"], bA["pe"] + 16 * (fc + 1))
                            v.wait_ge(sem["act"], bA["act"] + fc + 1)
                            c.inc(v.tensor_scalar(upb[:], pb[1][:, 0:512], bu_sb[:, fc:fc + 1],
                                                  None, ALU.add), "dve")
                            c.inc(v.tensor_tensor(prodT[:, fc, :], upb[:], gsil[:], ALU.mult), "dve")
                    prevA[0] = (bA["act"], bA["dve"])

                    for dc in range(2):
                        bB = c.snap()
                        @block.sync
                        def _(sy, sp=sp, dc=dc, bB=bB):
                            for fp in range(16):
                                k = fp % 2
                                gp = "gE" if k == 0 else "gF"
                                if fp < 2:
                                    sy.wait_ge(sem["pe"], wd_consumed[k])
                                else:
                                    sy.wait_ge(sem["pe"], bB["pe"] + 8 * (fp - 1))
                                c.inc(sy.dma_start(out=wd2[k][:],
                                                   in_=wdb_d[fp * 256:(fp + 1) * 256, dc * 512:(dc + 1) * 512]
                                                       .rearrange("(a p) d -> p a d", p=128)), gp, 16)

                        @block.tensor
                        def _(t, sp=sp, dc=dc, bB=bB, bA=bA):
                            for fp in range(16):
                                k = fp % 2
                                gp = "gE" if k == 0 else "gF"
                                t.wait_ge(sem[gp], bB[gp] + 16 * (fp // 2 + 1))
                                if fp == 0:
                                    t.wait_ge(sem["dve"], prev_evac_dve[0])
                                    t.wait_ge(sem["dve"], bA["dve"] + 64)
                                for half in range(2):
                                    fc = 2 * fp + half
                                    for tt4 in range(4):
                                        c.inc(t.matmul(pb[2 + tt4][:, 0:512],
                                                       prodT[:, fc, tt4 * 128:(tt4 + 1) * 128],
                                                       wd2[k][:, half, :],
                                                       start=(fc == 0), stop=(fc == 31)), "pe")
                        wd_consumed[0] = bB["pe"] + 8 * 15
                        wd_consumed[1] = bB["pe"] + 8 * 16
                        peB_end_prev[0] = c.v["pe"]

                        @block.vector
                        def _(v, sp=sp, dc=dc, bB=bB):
                            v.wait_ge(sem["pe"], bB["pe"] + 128)
                            for tt4 in range(4):
                                stp = "gG" if tt4 % 2 == 0 else "gH"
                                v.wait_ge(sem[stp], bB[stp] + 16 * (tt4 // 2))
                                c.inc(v.tensor_tensor(ot[tt4 % 2][:], pb[2 + tt4][:, 0:512],
                                                      bdb_sb[:, dc * 4:(dc + 1) * 4, :].rearrange("p a x -> p (a x)"),
                                                      ALU.add), "dve")
                        prev_evac_dve[0] = c.v["dve"]

                        @block.gpsimd
                        def _(g, sp=sp, dc=dc, bB=bB):
                            for tt4 in range(4):
                                stp = "gG" if tt4 % 2 == 0 else "gH"
                                g.wait_ge(sem["dve"], bB["dve"] + tt4 + 1)
                                tt = sp * 4 + tt4
                                c.inc(g.dma_start(out=acc_d[tt * 128:(tt + 1) * 128, dc * 512:(dc + 1) * 512],
                                                  in_=ot[tt4 % 2][:], accum_op=ALU.add), stp, 16)
                @block.gpsimd
                def _(g):
                    c.wait(g, "gG")
                    c.wait(g, "gH")
                barrier()

        def emit_l0_peer_a(xnT_sb, qT_sb):
            with ExitStack() as pc:
                det = [usb(pc, f"pa_de{i}", [128, ND, 512], BF16) for i in range(2)]
                wqt = usb(pc, "pa_wq", [128, ND, 512], BF16)
                wst = [usb(pc, f"pa_ws{i}", [128, 512], BF16) for i in range(2)]

                b = c.snap()
                @block.gpsimd
                def _(g, b=b):
                    c.inc(g.dma_start(out=wqt[:], in_=Wqp.rearrange("(a p) f -> p a f", p=128)), "g", 16)
                @block.tensor
                def _(t, b=b):
                    t.wait_ge(sem["g"], b["g"] + 16)
                    for u in range(32):
                        fc, sp = divmod(u, 8)
                        if u >= 2:
                            t.wait_ge(sem["dve"], b["dve"] + u - 1)
                        for dc in range(ND):
                            c.inc(t.matmul(pb[u % 2][:, 0:512], wqt[:, dc, fc * 128:(fc + 1) * 128],
                                           xnT_sb[:, dc, sp * 512:(sp + 1) * 512],
                                           start=(dc == 0), stop=(dc == ND - 1)), "pe")
                @block.vector
                def _(v, b=b):
                    for u in range(32):
                        fc, sp = divmod(u, 8)
                        v.wait_ge(sem["pe"], b["pe"] + 8 * (u + 1))
                        c.inc(v.tensor_copy(qT_sb[:, fc, sp * 512:(sp + 1) * 512], pb[u % 2][:, 0:512]), "dve")
                barrier()

                for ic in range(8):
                    b = c.snap()
                    @block.sync
                    def _(sy, ic=ic, b=b):
                        c.inc(sy.dma_start(out=det[ic % 2][:],
                                           in_=deT_d[:, ic * 512:(ic + 1) * 512].rearrange("(a p) i -> p a i", p=128)),
                              "ld", 16)
                        for tt in range(NT):
                            stp = "stA" if tt % 2 == 0 else "stB"
                            sy.wait_ge(sem["dve"], b["dve"] + tt + 1)
                            c.inc(sy.dma_start(out=wall_d[tt * 128:(tt + 1) * 128, ic * 512:(ic + 1) * 512],
                                               in_=wst[tt % 2][:]), stp, 16)
                        c.wait(sy, "stA")
                        c.wait(sy, "stB")

                    @block.tensor
                    def _(t, ic=ic, b=b):
                        t.wait_ge(sem["ld"], b["ld"] + 16)
                        for tt in range(NT):
                            if tt >= 2:
                                t.wait_ge(sem["dve"], b["dve"] + tt - 1)
                            for dc in range(ND):
                                c.inc(t.matmul(pb[tt % 2][:, 0:512], xnT_sb[:, dc, tt * 128:(tt + 1) * 128],
                                               det[ic % 2][:, dc, :],
                                               start=(dc == 0), stop=(dc == ND - 1)), "pe")

                    @block.vector
                    def _(v, ic=ic, b=b):
                        for tt in range(NT):
                            stp = "stA" if tt % 2 == 0 else "stB"
                            v.wait_ge(sem["pe"], b["pe"] + 8 * (tt + 1))
                            if tt >= 2:
                                v.wait_ge(sem[stp], b[stp] + 16 * (tt // 2)), 
                            c.inc(v.tensor_copy(wst[tt % 2][:], pb[tt % 2][:, 0:512]), "dve")
                    barrier()

        def emit_l0_peer_b(qT_sb):
            with ExitStack() as pc:
                ue = usb(pc, "pb_ue", [128, 32, D], BF16)
                wallt = [usb(pc, f"pb_w{i}", [128, NEXP], BF16) for i in range(2)]
                silw = usb(pc, "pb_sw", [128, NEXP], BF16)
                sfull = usb(pc, "pb_sf", [128, NEXP], BF16)
                fsc = usb(pc, "pb_fs", [128, NEXP], BF16)
                pexp = usb(pc, "pb_pe", [128, NEXP], BF16)
                mskt = usb(pc, "pb_mk", [128, NEXP], BF16)
                sT = usb(pc, "pb_sT", [128, 32, 128], BF16)
                simb = usb(pc, "pb_sb", [128, 2, 64], BF16)
                simf = usb(pc, "pb_sfl", [128, 2, 64], F32)
                sx8 = usb(pc, "pb_sx", [128, 8], F32)
                sy8 = usb(pc, "pb_sy", [128, 8], F32)
                cnd = usb(pc, "pb_cn", [128, 64], BF16)
                cndf = usb(pc, "pb_cf", [128, 64], F32)
                c8 = usb(pc, "pb_c8", [128, 8], F32)
                e8 = usb(pc, "pb_e8", [128, 8], F32)
                cls = usb(pc, "pb_cl", [128, 8], F32)
                est = [usb(pc, f"pb_es{i}", [128, 512], F32) for i in range(2)]

                b0 = c.snap()
                @block.gpsimd
                def _(g):
                    c.inc(g.dma_start(out=ue[:], in_=uem0.rearrange("(a p) d -> p a d", p=128)), "g", 16)

                PE_T = 8 + 32 + 64
                DVE_T = 4 * 13 + 1 + 32 + 2
                ACT_T = 1 + 4 * 3

                for tt in range(NT):
                    b = c.snap()
                    ldp = "ldA" if tt % 2 == 0 else "ldB"
                    @block.sync
                    def _(sy, tt=tt, b=b, ldp=ldp):
                        if tt >= 2:
                            sy.wait_ge(sem["act"], b["act"] - (ACT_T - 1))
                        c.inc(sy.dma_start(out=wallt[tt % 2][:], in_=wall_d[tt * 128:(tt + 1) * 128, :]), ldp, 16)

                    @block.scalar
                    def _(s, tt=tt, b=b, ldp=ldp):
                        s.wait_ge(sem[ldp], b[ldp] + 16)
                        if tt >= 1:
                            s.wait_ge(sem["dve"], b["dve"] - DVE_T + 53)
                        c.inc(s.activation(out=silw[:], in_=wallt[tt % 2][:], func=AF.Silu), "act")
                        for h in range(PEER_H):
                            hb_dve = b["dve"] + 13 * h
                            hb_act = b["act"] + 1 + 3 * h
                            s.wait_ge(sem["dve"], hb_dve + 8)
                            c.inc(s.activation(out=e8[:], in_=c8[:], func=AF.Exp,
                                               bias=cls[:, 0:1], accum_out=cls[:, 1:2]), "act")
                            c.inc(s.activation(out=cls[:, 2:3], in_=cls[:, 1:2], func=AF.Ln), "act")
                            s.wait_ge(sem["dve"], hb_dve + 10)
                            c.inc(s.activation(out=pexp[:], in_=fsc[:], func=AF.Exp,
                                               bias=cls[:, 3:4]), "act")

                    @block.tensor
                    def _(t, tt=tt, b=b):
                        if tt == 0:
                            t.wait_ge(sem["g"], b0["g"] + 16)
                        for h in range(PEER_H):
                            if h >= 1:
                                t.wait_ge(sem["dve"], b["dve"] + 13 * (h - 1) + 2)
                            elif tt >= 1:
                                t.wait_ge(sem["dve"], b["dve"] - DVE_T + 13 * 3 + 2)
                            for p in range(2):
                                fc = 2 * p + h // 2
                                off = (h % 2) * 64
                                c.inc(t.matmul(pb[5][0:128, p * 64:(p + 1) * 64],
                                               qT_sb[off:off + 64, fc, tt * 128:(tt + 1) * 128],
                                               keysT_sb[off:off + 64, h * 2 + p, :],
                                               start=True, stop=True), "pe")
                        t.wait_ge(sem["dve"], b["dve"] + 53)
                        for icn in range(32):
                            if icn >= 2:
                                t.wait_ge(sem["dve"], b["dve"] + 53 + icn - 1)
                            c.inc(t.transpose(pbt[icn % 2][0:128, 0:128],
                                              fsc[:, icn * 128:(icn + 1) * 128], id_sb[:]), "pe")
                        t.wait_ge(sem["dve"], b["dve"] + 85)
                        for dc in range(2):
                            if tt >= 1 and dc == 0:
                                t.wait_ge(sem["dve"], b["dve"] + 85)
                            if tt >= 1:
                                t.wait_ge(sem["st"], b["st"] - 16)
                            for icn in range(32):
                                c.inc(t.matmul(pb[2 + dc][:, 0:512], sT[:, icn, :],
                                               ue[:, icn, dc * 512:(dc + 1) * 512],
                                               start=(icn == 0), stop=(icn == 31)), "pe")

                    @block.vector
                    def _(v, tt=tt, b=b):
                        for h in range(PEER_H):
                            hb_act = b["act"] + 1 + 3 * h
                            v.wait_ge(sem["pe"], b["pe"] + 2 * (h + 1))
                            c.inc(v.tensor_copy(simb[:], pb[5][0:128, 0:128].rearrange("p (a x) -> p a x", a=2)), "dve")
                            c.inc(v.tensor_copy(simf[:], pb[5][0:128, 0:128].rearrange("p (a x) -> p a x", a=2)), "dve")
                            c.inc(v.max(out=sx8[:], in_=simf[:, 0, :]), "dve")
                            c.inc(v.max(out=sy8[:], in_=simf[:, 1, :]), "dve")
                            c.inc(v.tensor_tensor(
                                cnd[:].rearrange("p (a x) -> p a x", a=8),
                                sx8[:].rearrange("p (a o) -> p a o", o=1).to_broadcast([128, 8, 8]),
                                sy8[:].rearrange("p (o x) -> p o x", o=1).to_broadcast([128, 8, 8]),
                                ALU.add), "dve")
                            c.inc(v.tensor_copy(cndf[:], cnd[:]), "dve")
                            c.inc(v.max(out=c8[:], in_=cndf[:]), "dve")
                            c.inc(v.tensor_scalar(cls[:, 0:1], c8[:, 0:1], -1.0, None, ALU.mult), "dve")
                            v.wait_ge(sem["act"], hb_act + 2)
                            c.inc(v.tensor_scalar(cls[:, 3:4], cls[:, 0:1], cls[:, 2:3], None,
                                                  ALU.subtract), "dve")
                            c.inc(v.tensor_tensor(
                                fsc[:].rearrange("p (a x) -> p a x", a=64),
                                simb[:, 0, :].rearrange("p (a o) -> p a o", o=1).to_broadcast([128, 64, 64]),
                                simb[:, 1, :].rearrange("p (o x) -> p o x", o=1).to_broadcast([128, 64, 64]),
                                ALU.add), "dve")
                            c.inc(v.tensor_scalar(mskt[:], fsc[:], c8[:, 7:8], None, ALU.is_ge), "dve")
                            v.wait_ge(sem["act"], hb_act + 3)
                            c.inc(v.tensor_tensor(pexp[:], pexp[:], mskt[:], ALU.mult), "dve")
                            if h == 0:
                                c.inc(v.tensor_copy(sfull[:], pexp[:]), "dve")
                            else:
                                c.inc(v.tensor_tensor(sfull[:], sfull[:], pexp[:], ALU.add), "dve")
                        v.wait_ge(sem["act"], b["act"] + 1)
                        c.inc(v.tensor_tensor(fsc[:], sfull[:], silw[:], ALU.mult), "dve")
                        for icn in range(32):
                            v.wait_ge(sem["pe"], b["pe"] + 8 + icn + 1)
                            c.inc(v.tensor_copy(sT[:, icn, :], pbt[icn % 2][0:128, 0:128]), "dve")
                        for dc in range(2):
                            gp = "gA" if dc == 0 else "gB"
                            v.wait_ge(sem["pe"], b["pe"] + 8 + 32 + 32 * (dc + 1))
                            if tt >= 1:
                                v.wait_ge(sem[gp], b[gp])
                            c.inc(v.tensor_copy(est[dc][:], pb[2 + dc][:, 0:512]), "dve")

                    @block.gpsimd
                    def _(g, tt=tt, b=b):
                        for dc in range(2):
                            gp = "gA" if dc == 0 else "gB"
                            g.wait_ge(sem["dve"], b["dve"] + 85 + dc + 1)
                            c.inc(g.dma_start(out=acc_d[tt * 128:(tt + 1) * 128, dc * 512:(dc + 1) * 512],
                                              in_=est[dc][:], accum_op=ALU.add), gp, 16)
                @block.gpsimd
                def _(g):
                    c.wait(g, "gA")
                    c.wait(g, "gB")
                barrier()

        def emit_residual(x_src, acc_src, dst_ap):
            with ExitStack() as pc:
                xt = [usb(pc, f"r_x{i}", [128, D], F32) for i in range(2)]
                at = [usb(pc, f"r_a{i}", [128, D], F32) for i in range(2)]
                so = [usb(pc, f"r_s{i}", [128, D], F32) for i in range(2)]
                b = c.snap()

                @block.sync
                def _(sy, b=b):
                    for u in range(NT):
                        xp = "ldA" if u % 2 == 0 else "ldB"
                        ap_ = "ldC" if u % 2 == 0 else "ldD"
                        if u >= 2:
                            sy.wait_ge(sem["dve"], b["dve"] + (u - 1))
                        c.inc(sy.dma_start(out=xt[u % 2][:], in_=x_src[u * 128:(u + 1) * 128, :]), xp, 16)
                        c.inc(sy.dma_start(out=at[u % 2][:], in_=acc_src[u * 128:(u + 1) * 128, :]), ap_, 16)
                        if u >= 1:
                            stp = "stA" if (u - 1) % 2 == 0 else "stB"
                            sy.wait_ge(sem["dve"], b["dve"] + u)
                            c.inc(sy.dma_start(out=dst_ap[(u - 1) * 128:u * 128, :], in_=so[(u - 1) % 2][:]), stp, 16)
                    stp = "stA" if (NT - 1) % 2 == 0 else "stB"
                    sy.wait_ge(sem["dve"], b["dve"] + NT)
                    c.inc(sy.dma_start(out=dst_ap[(NT - 1) * 128:NT * 128, :], in_=so[(NT - 1) % 2][:]), stp, 16)
                    c.wait(sy, "stA")
                    c.wait(sy, "stB")

                @block.vector
                def _(v, b=b):
                    for u in range(NT):
                        xp = "ldA" if u % 2 == 0 else "ldB"
                        ap_ = "ldC" if u % 2 == 0 else "ldD"
                        v.wait_ge(sem[xp], b[xp] + 16 * (u // 2 + 1))
                        v.wait_ge(sem[ap_], b[ap_] + 16 * (u // 2 + 1))
                        if u >= 2:
                            stp = "stA" if u % 2 == 0 else "stB"
                            v.wait_ge(sem[stp], b[stp] + 16 * (u // 2))
                        c.inc(v.tensor_tensor(so[u % 2][:], xt[u % 2][:], at[u % 2][:], ALU.add), "dve")
                barrier()

        def emit_l1_attn_a(xnT_sb):
            with ExitStack() as pc:
                wqkv = usb(pc, "l1_wqkv", [128, ND, 3 * D], BF16)
                qkvb = [usb(pc, f"l1_qkv{i}", [128, 3 * D], BF16) for i in range(2)]
                ropc = usb(pc, "l1_rc", [128, 2 * D], BF16)
                ropt = usb(pc, "l1_rt", [128, 2 * D], BF16)
                roped = usb(pc, "l1_rp", [128, 2 * D], BF16)
                tmin = usb(pc, "l1_tm", [128, 2 * D], BF16)
                texp = usb(pc, "l1_te", [128, 2 * D], BF16)
                feat = [usb(pc, f"l1_f{i}", [128, 2 * D], BF16) for i in range(2)]
                onec = usb(pc, "l1_on", [128, 1], BF16)
                qfs = [usb(pc, f"l1_qt{i}", [128, ND, 128], BF16) for i in range(2)]

                b0 = c.snap()
                @block.gpsimd
                def _(g):
                    c.inc(g.dma_start(out=wqkv[:], in_=Wqkv1.rearrange("(a p) f -> p a f", p=128)), "g", 16)
                @block.vector
                def _(v):
                    c.inc(v.memset(onec[:], 1.0), "dve")
                barrier()

                DV = 23
                PE_T2 = 48 + 32 + 8
                b = c.snap()

                def kv_bank(h):
                    return 3 + h // 7, (h % 7) * 65

                @block.tensor
                def _(t, b=b):
                    for tt in range(NT):
                        pbase = b["pe"] + PE_T2 * tt
                        dbase = b["dve"] + DV * tt
                        for fcq in range(6):
                            if tt > 0 or fcq >= 2:
                                if fcq >= 2:
                                    t.wait_ge(sem["dve"], dbase + fcq - 1)
                                else:
                                    t.wait_ge(sem["dve"], dbase - DV + fcq + 5)
                            for dc in range(ND):
                                c.inc(t.matmul(pb[fcq % 2][:, 0:512], xnT_sb[:, dc, tt * 128:(tt + 1) * 128],
                                               wqkv[:, dc, fcq * 512:(fcq + 1) * 512],
                                               start=(dc == 0), stop=(dc == ND - 1)), "pe")
                        t.wait_ge(sem["dve"], dbase + 15)
                        for h in range(16):
                            bk, col = kv_bank(h)
                            first_h = 0 if bk == 3 else (7 if bk == 4 else 14)
                            last_h = 6 if bk == 3 else (13 if bk == 4 else 15)
                            c.inc(t.matmul(pb[bk][0:64, col:col + 64],
                                           feat[tt % 2][:, D + h * 64:D + h * 64 + 64],
                                           qkvb[tt % 2][:, 2 * D + h * 64:2 * D + h * 64 + 64],
                                           start=(tt == 0 and h == first_h), stop=False), "pe")
                            c.inc(t.matmul(pb[bk][0:64, col + 64:col + 65],
                                           feat[tt % 2][:, D + h * 64:D + h * 64 + 64],
                                           onec[:], start=False,
                                           stop=(tt == NT - 1 and h == last_h)), "pe")
                        for dc in range(ND):
                            if dc >= 2:
                                t.wait_ge(sem["dve"], dbase + 15 + dc - 1)
                            c.inc(t.transpose(pbt[dc % 2][0:128, 0:128],
                                              feat[tt % 2][:, dc * 128:(dc + 1) * 128], id_sb[:]), "pe")

                @block.vector
                def _(v, b=b):
                    for tt in range(NT):
                        pbase = b["pe"] + PE_T2 * tt
                        dbase = b["dve"] + DV * tt
                        abase = b["act"] + 2 * tt
                        for fcq in range(6):
                            v.wait_ge(sem["pe"], pbase + 8 * (fcq + 1))
                            c.inc(v.tensor_copy(qkvb[tt % 2][:, fcq * 512:(fcq + 1) * 512],
                                                pb[fcq % 2][:, 0:512]), "dve")
                        qk = qkvb[tt % 2][:, 0:2 * D]
                        qk4 = qk.rearrange("p (h x n) -> p h x n", h=32, x=2)
                        cosv = cos_sb[:, tt, :].rearrange("p (a x n) -> p a x n", a=1, x=1).to_broadcast([128, 32, 2, 32])
                        sinv = sin_sb[:, tt, :].rearrange("p (a n) -> p a n", a=1).to_broadcast([128, 32, 32])
                        msinv = msin_sb[:, tt, :].rearrange("p (a n) -> p a n", a=1).to_broadcast([128, 32, 32])
                        rc4 = ropc[:].rearrange("p (h x n) -> p h x n", h=32, x=2)
                        rt4 = ropt[:].rearrange("p (h x n) -> p h x n", h=32, x=2)
                        c.inc(v.tensor_tensor(rc4, qk4, cosv, ALU.mult), "dve")
                        c.inc(v.tensor_tensor(rt4[:, :, 0, :], qk4[:, :, 1, :], msinv, ALU.mult), "dve")
                        c.inc(v.tensor_tensor(rt4[:, :, 1, :], qk4[:, :, 0, :], sinv, ALU.mult), "dve")
                        c.inc(v.tensor_tensor(roped[:], ropc[:], ropt[:], ALU.add), "dve")
                        c.inc(v.tensor_scalar(tmin[:, 0:D], roped[:, 0:D], 0.125, 0.0, ALU.mult, ALU.min), "dve")
                        c.inc(v.tensor_scalar(tmin[:, D:2 * D], roped[:, D:2 * D], 1.0, 0.0, ALU.mult, ALU.min), "dve")
                        if tt >= 2:
                            v.wait_ge(sem["pe"], b["pe"] + PE_T2 * (tt - 1))
                        c.inc(v.tensor_scalar(feat[tt % 2][:, 0:D], roped[:, 0:D], 0.125, 0.0,
                                              ALU.mult, ALU.max), "dve")
                        c.inc(v.tensor_scalar(feat[tt % 2][:, D:2 * D], roped[:, D:2 * D], 1.0, 0.0,
                                              ALU.mult, ALU.max), "dve")
                        v.wait_ge(sem["act"], abase + 2)
                        c.inc(v.tensor_tensor(feat[tt % 2][:], feat[tt % 2][:], texp[:], ALU.add), "dve")
                        for dc in range(ND):
                            v.wait_ge(sem["pe"], pbase + 48 + 32 + dc + 1)
                            if tt >= 2 and dc == 0:
                                stp = "stA" if tt % 2 == 0 else "stB"
                                v.wait_ge(sem[stp], b[stp] + 16 * (tt // 2))
                            c.inc(v.tensor_copy(qfs[tt % 2][:, dc, :], pbt[dc % 2][0:128, 0:128]), "dve")

                @block.scalar
                def _(s, b=b):
                    for tt in range(NT):
                        dbase = b["dve"] + DV * tt
                        s.wait_ge(sem["dve"], dbase + 12)
                        c.inc(s.activation(out=texp[:, 0:D], in_=tmin[:, 0:D], func=AF.Exp), "act")
                        c.inc(s.activation(out=texp[:, D:2 * D], in_=tmin[:, D:2 * D], func=AF.Exp), "act")

                @block.sync
                def _(sy, b=b):
                    for tt in range(NT):
                        stp = "stA" if tt % 2 == 0 else "stB"
                        sy.wait_ge(sem["dve"], b["dve"] + DV * (tt + 1))
                        c.inc(sy.dma_start(out=qfT_d[:, tt * 128:(tt + 1) * 128].rearrange("(a p) t -> p a t", p=128),
                                           in_=qfs[tt % 2][:]), stp, 16)
                    c.wait(sy, "stA")
                    c.wait(sy, "stB")
                barrier()

        def emit_l1_attn_b():
            with ExitStack() as pc:
                kvb = usb(pc, "l1b_kv", [128, 8, 65], BF16)
                qfp = [usb(pc, f"l1b_q{i}", [128, T], BF16) for i in range(2)]
                dmx = usb(pc, "l1b_dm", [1, 512], F32)
                rr = usb(pc, "l1b_rr", [1, 512], F32)
                rrb = usb(pc, "l1b_rb", [1, 512], BF16)
                rsb = usb(pc, "l1b_rs", [64, 512], F32)
                aop = usb(pc, "l1b_ao", [128, T], BF16)

                b = c.snap()
                @block.vector
                def _(v, b=b):
                    kvf = kvb[:].rearrange("p a x -> p (a x)")
                    for h in range(16):
                        bk, col = (3 + h // 7, (h % 7) * 65)
                        dst_col = (h // 2) * 65
                        r0 = (h % 2) * 64
                        c.inc(v.tensor_copy(kvf[r0:r0 + 64, dst_col:dst_col + 65],
                                            pb[bk][0:64, col:col + 65]), "dve")
                barrier()

                for pair in range(8):
                    b = c.snap()
                    @block.sync
                    def _(sy, pair=pair, b=b):
                        if pair >= 2:
                            sy.wait_ge(sem["pe"], b["pe"] - 32)
                        c.inc(sy.dma_start(out=qfp[pair % 2][:], in_=qfT_d[pair * 128:(pair + 1) * 128, :]), "ld", 16)

                    @block.tensor
                    def _(t, pair=pair, b=b):
                        t.wait_ge(sem["ld"], b["ld"] + 16)
                        rowA = kvb[0:64, pair, :]
                        rowB = kvb[64:128, pair, :]
                        for tc in range(8):
                            for hh in range(2):
                                u = tc * 2 + hh
                                row = rowA if hh == 0 else rowB
                                qap = qfp[pair % 2][(hh * 64):(hh * 64) + 64, tc * 512:(tc + 1) * 512]
                                if u >= 2 or pair >= 1:
                                    t.wait_ge(sem["dve"], b["dve"] + 5 * (u - 2) + 5)
                                c.inc(t.matmul(pb[hh][0:65, 0:512], row, qap, start=True, stop=True), "pe")
                                t.wait_ge(sem["dve"], b["dve"] + 5 * u + 3)
                                c.inc(t.matmul(pb[2][0:64, 0:512], onesr_sb[0:1, 0:64], rrb[:],
                                               start=True, stop=True), "pe")

                    @block.vector
                    def _(v, pair=pair, b=b):
                        for tc in range(8):
                            for hh in range(2):
                                u = tc * 2 + hh
                                if u == 0 and pair >= 1:
                                    v.wait_ge(sem["st"], b["st"])
                                v.wait_ge(sem["pe"], b["pe"] + 2 * u + 1)
                                c.inc(v.tensor_scalar(dmx[:], pb[hh][64:65, 0:512], eps_sb[0:1, 1:2], None, ALU.max), "dve")
                                c.inc(v.reciprocal(rr[:], dmx[:]), "dve")
                                c.inc(v.tensor_copy(rrb[:], rr[:]), "dve")
                                v.wait_ge(sem["pe"], b["pe"] + 2 * u + 2)
                                c.inc(v.tensor_copy(rsb[:], pb[2][0:64, 0:512]), "dve")
                                c.inc(v.tensor_tensor(aop[hh * 64:hh * 64 + 64, tc * 512:(tc + 1) * 512],
                                                      pb[hh][0:64, 0:512], rsb[:], ALU.mult), "dve")

                    @block.sync
                    def _(sy, pair=pair, b=b):
                        sy.wait_ge(sem["dve"], b["dve"] + 80)
                        c.inc(sy.dma_start(out=aoT_d[pair * 128:(pair + 1) * 128, :], in_=aop[:]), "st", 16)
                barrier()

        def emit_l1_router(xnT_sb, g_sb, s_sb):
            with ExitStack() as pc:
                lg = usb(pc, "rt_lg", [128, 8], F32)
                t2 = usb(pc, "rt_t2", [128, 8], F32)
                w01 = usb(pc, "rt_w", [128, 4], F32)
                eqs = usb(pc, "rt_e", [128, 2], F32)
                b = c.snap()
                DVR = 4 + 7 * E1

                @block.tensor
                def _(t, b=b):
                    for tt in range(NT):
                        if tt >= 1:
                            t.wait_ge(sem["dve"], b["dve"] + DVR * (tt - 1) + 1)
                        for dc in range(ND):
                            c.inc(t.matmul(pb[5][0:128, 0:8], xnT_sb[:, dc, tt * 128:(tt + 1) * 128],
                                           wgT_sb[:, dc, :], start=(dc == 0), stop=(dc == ND - 1)), "pe")

                @block.vector
                def _(v, b=b):
                    for tt in range(NT):
                        v.wait_ge(sem["pe"], b["pe"] + 8 * (tt + 1))
                        c.inc(v.tensor_copy(lg[:], pb[5][0:128, 0:8]), "dve")
                        c.inc(v.max(out=t2[:], in_=lg[:]), "dve")
                        c.inc(v.tensor_scalar(w01[:, 0:1], t2[:, 0:1], t2[:, 1:2], None, ALU.subtract), "dve")
                        v.wait_ge(sem["act"], b["act"] + tt + 1)
                        c.inc(v.tensor_scalar(w01[:, 2:3], w01[:, 1:2], -1.0, 1.0, ALU.mult, ALU.add), "dve")
                        for e in range(E1):
                            c.inc(v.tensor_scalar(eqs[:, 0:1], lg[:, e:e + 1], t2[:, 0:1], None, ALU.is_equal), "dve")
                            c.inc(v.tensor_scalar(eqs[:, 1:2], lg[:, e:e + 1], t2[:, 1:2], None, ALU.is_equal), "dve")
                            c.inc(v.tensor_scalar(s_sb[:, tt, e:e + 1], eqs[:, 0:1], -1.0, 1.0, ALU.mult, ALU.add), "dve")
                            c.inc(v.tensor_tensor(eqs[:, 1:2], eqs[:, 1:2], s_sb[:, tt, e:e + 1], ALU.mult), "dve")
                            c.inc(v.tensor_tensor(eqs[:, 0:1], eqs[:, 0:1], w01[:, 1:2], ALU.mult), "dve")
                            c.inc(v.tensor_tensor(eqs[:, 1:2], eqs[:, 1:2], w01[:, 2:3], ALU.mult), "dve")
                            c.inc(v.tensor_tensor(g_sb[:, tt, e:e + 1], eqs[:, 0:1], eqs[:, 1:2], ALU.add), "dve")

                @block.scalar
                def _(s, b=b):
                    for tt in range(NT):
                        s.wait_ge(sem["dve"], b["dve"] + DVR * tt + 3)
                        c.inc(s.activation(out=w01[:, 1:2], in_=w01[:, 0:1], func=AF.Sigmoid), "act")
                barrier()

        def emit_l1_moe(xnT_sb, g_sb):
            with ExitStack() as pc:
                w1s = usb(pc, "moe_w1", [128, ND, F1], BF16)
                w2t = [usb(pc, f"moe_w2{i}", [128, 2, 512], BF16) for i in range(4)]
                actT = usb(pc, "moe_a", [128, 32, 512], BF16)
                ot = [usb(pc, f"moe_o{i}", [128, 512], F32) for i in range(2)]

                LDP = ["ldA", "ldB", "ldC", "ldD"]
                w2_consumed_pe = [0, 0, 0, 0]
                peB_end_prev = [0]
                prev_evac_dve = [0]
                prevA = [None]
                g_after_w1 = [0]

                @block.gpsimd
                def _(g):
                    c.inc(g.dma_start(out=w1s[:], in_=W11[0].rearrange("(a p) f -> p a f", p=128)), "g", 16)
                g_after_w1[0] = c.v["g"]

                for e in range(E1):
                    for sp in range(8):
                        bA = c.snap()
                        @block.tensor
                        def _(t, e=e, sp=sp, bA=bA):
                            if sp == 0:
                                t.wait_ge(sem["g"], g_after_w1[0])
                            for fc in range(32):
                                if fc < 2:
                                    if prevA[0] is not None:
                                        t.wait_ge(sem["act"], prevA[0][0] + 32)
                                else:
                                    t.wait_ge(sem["act"], bA["act"] + fc - 1)
                                for dc in range(ND):
                                    c.inc(t.matmul(pb[fc % 2][:, 0:512], w1s[:, dc, fc * 128:(fc + 1) * 128],
                                                   xnT_sb[:, dc, sp * 512:(sp + 1) * 512],
                                                   start=(dc == 0), stop=(dc == ND - 1)), "pe")

                        @block.scalar
                        def _(s, e=e, sp=sp, bA=bA):
                            for fc in range(32):
                                if fc == 0:
                                    s.wait_ge(sem["pe"], peB_end_prev[0])
                                s.wait_ge(sem["pe"], bA["pe"] + 8 * (fc + 1))
                                c.inc(s.activation(out=actT[:, fc, :], in_=pb[fc % 2][:, 0:512],
                                                   func=AF.Silu), "act")
                        prevA[0] = (bA["act"], bA["dve"])

                        if sp == 7 and e < E1 - 1:
                            gate_pe = c.v["pe"]
                            @block.gpsimd
                            def _(g, e=e, gate_pe=gate_pe):
                                g.wait_ge(sem["pe"], gate_pe)
                                c.inc(g.dma_start(out=w1s[:],
                                                  in_=W11[e + 1].rearrange("(a p) f -> p a f", p=128)), "g", 16)
                            g_after_w1[0] = c.v["g"]

                        for dc in range(2):
                            bB = c.snap()
                            @block.sync
                            def _(sy, e=e, sp=sp, dc=dc, bB=bB):
                                for fp in range(16):
                                    k = fp % 4
                                    if fp < 4:
                                        sy.wait_ge(sem["pe"], w2_consumed_pe[k])
                                    else:
                                        sy.wait_ge(sem["pe"], bB["pe"] + 8 * (fp - 3))
                                    c.inc(sy.dma_start(
                                        out=w2t[k][:],
                                        in_=w2b_d[e][fp * 256:(fp + 1) * 256, dc * 512:(dc + 1) * 512]
                                            .rearrange("(a p) d -> p a d", p=128)), LDP[k], 16)

                            @block.tensor
                            def _(t, e=e, sp=sp, dc=dc, bB=bB, bA=bA):
                                for fp in range(16):
                                    k = fp % 4
                                    t.wait_ge(sem[LDP[k]], bB[LDP[k]] + 16 * (fp // 4 + 1))
                                    if fp == 0:
                                        t.wait_ge(sem["dve"], prev_evac_dve[0])
                                        t.wait_ge(sem["act"], bA["act"] + 32)
                                    for half in range(2):
                                        fc = 2 * fp + half
                                        for tt4 in range(4):
                                            c.inc(t.matmul(pb[2 + tt4][:, 0:512],
                                                           actT[:, fc, tt4 * 128:(tt4 + 1) * 128],
                                                           w2t[k][:, half, :],
                                                           start=(fc == 0), stop=(fc == 31)), "pe")
                                    w2_consumed_pe[k] = c.v["pe"]
                            peB_end_prev[0] = c.v["pe"]

                            @block.vector
                            def _(v, e=e, sp=sp, dc=dc, bB=bB):
                                v.wait_ge(sem["pe"], bB["pe"] + 128)
                                for tt4 in range(4):
                                    gp = "gA" if tt4 % 2 == 0 else "gB"
                                    v.wait_ge(sem[gp], bB[gp] + 16 * (tt4 // 2))
                                    tt = sp * 4 + tt4
                                    c.inc(v.tensor_scalar(ot[tt4 % 2][:], pb[2 + tt4][:, 0:512],
                                                          g_sb[:, tt, e:e + 1], None, ALU.mult), "dve")
                            prev_evac_dve[0] = c.v["dve"]

                            @block.gpsimd
                            def _(g, e=e, sp=sp, dc=dc, bB=bB):
                                for tt4 in range(4):
                                    gp = "gA" if tt4 % 2 == 0 else "gB"
                                    g.wait_ge(sem["dve"], bB["dve"] + tt4 + 1)
                                    tt = sp * 4 + tt4
                                    c.inc(g.dma_start(out=acc_d[tt * 128:(tt + 1) * 128,
                                                                dc * 512:(dc + 1) * 512],
                                                      in_=ot[tt4 % 2][:], accum_op=ALU.add), gp, 16)
                @block.gpsimd
                def _(g):
                    c.wait(g, "gA")
                    c.wait(g, "gB")
                barrier()

        def emit_l1_moe_sparse(g_sb, s_sb):
            from concourse.bass_core_rust import IndirectOffsetOnAxis
            I32 = mybir.dt.int32

            with ExitStack() as ps_:
                gT = usb(ps_, "ms_gT", [8, T], F32)
                rA = usb(ps_, "ms_rA", [8, T], F32)
                rB = usb(ps_, "ms_rB", [8, T], F32)
                zst = usb(ps_, "ms_z", [128, 4096], F32)
                mst = usb(ps_, "ms_m", [128, CAP // 128 * 8], F32)
                b = c.snap()

                @block.tensor
                def _(t, b=b):
                    for tt in range(NT):
                        if tt >= 1:
                            t.wait_ge(sem["dve"], b["dve"] + tt)
                        c.inc(t.transpose(pb45[0].ap()[0:8, 0:128], g_sb[:, tt, :],
                                          idf_sb[:]), "pe")

                @block.vector
                def _(v, b=b):
                    for tt in range(NT):
                        v.wait_ge(sem["pe"], b["pe"] + tt + 1)
                        c.inc(v.tensor_copy(gT[0:8, tt * 128:(tt + 1) * 128],
                                            pb45[0].ap()[0:8, 0:128]), "dve")
                    c.inc(v.tensor_scalar(rA[:], gT[:], 0.0, None, ALU.is_gt), "dve")
                    c.inc(v.tensor_tensor_scan(rB[:], rA[:], rA[:], 0.0, ALU.add, ALU.bypass), "dve")
                    c.inc(v.tensor_tensor(rB[:], rB[:], rA[:], ALU.subtract), "dve")
                    c.inc(v.tensor_tensor(rB[:], rB[:], rA[:], ALU.mult), "dve")
                    c.inc(v.tensor_scalar(rA[:], rA[:], -4000000.0, 4000000.0, ALU.mult, ALU.add), "dve")
                    c.inc(v.tensor_tensor(rB[:], rB[:], rA[:], ALU.add), "dve")
                    c.inc(v.memset(zst[:], 0.0), "dve")
                    c.inc(v.memset(mst[:], 0.0), "dve")
                    c.inc(v.memset(mst[:].rearrange("p (a x) -> p a x", x=8)[:, :, 1:2], 5000000.0), "dve")

                @block.sync
                def _(sy, b=b):
                    c.wait(sy, "dve")
                    c.inc(sy.dma_start(out=pos_d, in_=rB[:]), "ld", 16)
                    for k in range(16):
                        c.inc(sy.dma_start(
                            out=ysc_d[k * 512:(k + 1) * 512, :].rearrange("(a p) x -> p a x", p=128),
                            in_=zst[:].rearrange("p (a x) -> p a x", a=4)), "st", 16)
                    for e in range(E1):
                        c.inc(sy.dma_start(
                            out=cmeta_d[e * CAP:(e + 1) * CAP, :].rearrange("(a p) x -> p a x", p=128),
                            in_=mst[:].rearrange("p (a x) -> p a x", x=8)), "st", 16)
                    c.wait(sy, "st")
                    c.wait(sy, "ld")
                barrier()

            with ExitStack() as pc:
                w1s = usb(pc, "ms_w1", [128, ND, F1], BF16)
                w2t = [usb(pc, f"ms_w2{i}", [128, 2, 512], BF16) for i in range(4)]
                cxT = [usb(pc, f"ms_cxT{i}", [128, ND, CAP], BF16) for i in range(2)]
                actT = usb(pc, "ms_a", [128, 32, 512], BF16)
                ot = [usb(pc, f"ms_o{i}", [128, 512], F32) for i in range(2)]
                xnb = [usb(pc, f"ms_x{i}", [128, D], BF16) for i in range(2)]
                posf = [usb(pc, f"ms_p{i}", [128, 1], F32) for i in range(2)]
                posi = [usb(pc, f"ms_pi{i}", [128, 1], I32) for i in range(2)]
                meta = [usb(pc, f"ms_mt{i}", [128, 8], F32) for i in range(2)]
                cme = [usb(pc, f"ms_cm{i}", [128, CAP // 128, 8], F32) for i in range(2)]
                tids = [usb(pc, f"ms_ti{i}", [128, CAP // 128], I32) for i in range(2)]

                LDP = ["ldA", "ldB", "ldC", "ldD"]
                w2_consumed_pe = [0, 0, 0, 0]
                prev_evac_dve = [0]
                prevA = [None]
                g_after_w1 = [0]
                cxT_free_pe = [0, 0]
                scat_done = [None] * E1
                ysc_done = [None] * E1
                NSP = CAP // 512

                @block.vector
                def _(v):
                    c.inc(v.memset(meta[0][:], 0.0), "dve")
                    c.inc(v.memset(meta[1][:], 0.0), "dve")

                bc_regs = []
                @block.gpsimd
                def _(g):
                    bc_regs.append(g.snap(CAP - 1))
                    bc_regs.append(g.snap(2 * T - 1))
                    c.inc(g.dma_start(out=w1s[:], in_=W11[0].rearrange("(a p) f -> p a f", p=128)), "g", 16)
                g_after_w1[0] = c.v["g"]

                for e in range(E1):
                    bS = c.snap()
                    @block.sync
                    def _(sy, e=e, bS=bS):
                        for tt in range(NT):
                            xp = "ldE" if tt % 2 == 0 else "ldF"
                            pp = "stC" if tt % 2 == 0 else "stD"
                            if tt >= 2:
                                gx = "gI" if tt % 2 == 0 else "gJ"
                                sy.wait_ge(sem[gx], bS[gx] + 16 * (tt // 2))
                                sy.wait_ge(sem["dve"], bS["dve"] + 4 * (tt - 1))
                            c.inc(sy.dma_start(out=xnb[tt % 2][:],
                                               in_=xn_d[tt * 128:(tt + 1) * 128, :]), xp, 16)
                            c.inc(sy.dma_start(
                                out=posf[tt % 2][:],
                                in_=pos_d[e:e + 1, tt * 128:(tt + 1) * 128]
                                    .rearrange("o (p x) -> (o p) x", x=1)), pp, 16)

                    @block.vector
                    def _(v, e=e, bS=bS):
                        for tt in range(NT):
                            pp = "stC" if tt % 2 == 0 else "stD"
                            gm = "gK" if tt % 2 == 0 else "gL"
                            v.wait_ge(sem[pp], bS[pp] + 16 * (tt // 2 + 1))
                            if tt >= 2:
                                gx2 = "gI" if tt % 2 == 0 else "gJ"
                                v.wait_ge(sem[gm], bS[gm] + 16 * (tt // 2))
                                v.wait_ge(sem[gx2], bS[gx2] + 16 * (tt // 2))
                            c.inc(v.tensor_copy(posi[tt % 2][:], posf[tt % 2][:]), "dve")
                            c.inc(v.tensor_copy(meta[tt % 2][:, 0:1], g_sb[:, tt, e:e + 1]), "dve")
                            c.inc(v.tensor_scalar(meta[tt % 2][:, 1:2], s_sb[:, tt, e:e + 1],
                                                  4096.0, None, ALU.mult), "dve")
                            c.inc(v.tensor_tensor(meta[tt % 2][:, 1:2], meta[tt % 2][:, 1:2],
                                                  tid_sb[:, tt:tt + 1], ALU.add), "dve")

                    @block.gpsimd
                    def _(g, e=e, bS=bS):
                        if e >= 1:
                            for n in ("gI", "gJ", "gK", "gL"):
                                g.wait_ge(sem[n], scat_done[e - 1][n])
                            for n in ("gM", "gN"):
                                g.wait_ge(sem[n], ysc_done[e - 1][n])
                            g.dma_reset()
                        for tt in range(NT):
                            xp = "ldE" if tt % 2 == 0 else "ldF"
                            gx = "gI" if tt % 2 == 0 else "gJ"
                            gm = "gK" if tt % 2 == 0 else "gL"
                            if tt >= 4:
                                g.wait_ge(sem[gx], bS[gx] + 16 * ((tt - 4) // 2 + 1))
                                g.wait_ge(sem[gm], bS[gm] + 16 * ((tt - 4) // 2 + 1))
                            g.wait_ge(sem[xp], bS[xp] + 16 * (tt // 2 + 1))
                            g.wait_ge(sem["dve"], bS["dve"] + 4 * (tt + 1))
                            c.inc(g.indirect_dma_start(
                                out=cx_d, out_offset=IndirectOffsetOnAxis(posi[tt % 2][:], 0),
                                in_=xnb[tt % 2][:], in_offset=None, element_offset=e * CAP * D,
                                bounds_check=bc_regs[0], oob_is_err=False), gx, 16)
                            c.inc(g.indirect_dma_start(
                                out=cmeta_d, out_offset=IndirectOffsetOnAxis(posi[tt % 2][:], 0),
                                in_=meta[tt % 2][:], in_offset=None, element_offset=e * CAP * 8,
                                bounds_check=bc_regs[0], oob_is_err=False), gm, 16)
                    scat_done[e] = {n: c.v[n] for n in ("gI", "gJ", "gK", "gL")}

                    bT = c.snap()
                    @block.sync
                    def _(sy, e=e, bT=bT):
                        for n in ("gI", "gJ"):
                            sy.wait_ge(sem[n], scat_done[e][n])
                        sy.wait_ge(sem["pe"], cxT_free_pe[e % 2])
                        for ch in range(ND):
                            c.inc(sy.dma_start(out=cxT[e % 2][:, ch, :],
                                               in_=cx_d[e * CAP:(e + 1) * CAP, ch * 128:(ch + 1) * 128],
                                               transpose=True), "ld", 16)
                        for n in ("gK", "gL"):
                            sy.wait_ge(sem[n], scat_done[e][n])
                        c.inc(sy.dma_start(out=cme[e % 2][:],
                                           in_=cmeta_d[e * CAP:(e + 1) * CAP, :].rearrange("(a p) x -> p a x", p=128)), "ld", 16)
                    ld_after_T = c.v["ld"]

                    @block.vector
                    def _(v, e=e, bT=bT, ld_after_T=ld_after_T):
                        v.wait_ge(sem["ld"], ld_after_T)
                        for j in range(CAP // 128):
                            c.inc(v.tensor_copy(tids[e % 2][:, j:j + 1], cme[e % 2][:, j, 1:2]), "dve")
                    tids_dve = c.v["dve"]

                    for sp in range(NSP):
                        bA = c.snap()
                        @block.tensor
                        def _(t, e=e, sp=sp, bA=bA, ld_after_T=ld_after_T):
                            if sp == 0:
                                t.wait_ge(sem["g"], g_after_w1[0])
                                t.wait_ge(sem["ld"], ld_after_T)
                            for fc in range(32):
                                if fc < 2:
                                    if prevA[0] is not None:
                                        t.wait_ge(sem["act"], prevA[0][0] + 32)
                                else:
                                    t.wait_ge(sem["act"], bA["act"] + fc - 1)
                                for dc in range(ND):
                                    c.inc(t.matmul(pb[fc % 2][:, 0:512], w1s[:, dc, fc * 128:(fc + 1) * 128],
                                                   cxT[e % 2][:, dc, sp * 512:(sp + 1) * 512],
                                                   start=(dc == 0), stop=(dc == ND - 1)), "pe")

                        @block.scalar
                        def _(s, e=e, sp=sp, bA=bA):
                            for fc in range(32):
                                if fc == 0:
                                    s.wait_ge(sem["pe"], prev_evac_dve[0] * 0 + peB_end_prev2[0])
                                s.wait_ge(sem["pe"], bA["pe"] + 8 * (fc + 1))
                                c.inc(s.activation(out=actT[:, fc, :], in_=pb[fc % 2][:, 0:512],
                                                   func=AF.Silu), "act")
                        prevA[0] = (bA["act"], bA["dve"])

                        if sp == NSP - 1:
                            cxT_free_pe[e % 2] = c.v["pe"]
                            if e < E1 - 1:
                                gate_pe = c.v["pe"]
                                @block.gpsimd
                                def _(g, e=e, gate_pe=gate_pe):
                                    g.wait_ge(sem["pe"], gate_pe)
                                    c.inc(g.dma_start(out=w1s[:],
                                                      in_=W11[e + 1].rearrange("(a p) f -> p a f", p=128)), "g", 16)
                                g_after_w1[0] = c.v["g"]

                        for dc in range(2):
                            bB = c.snap()
                            @block.sync
                            def _(sy, e=e, sp=sp, dc=dc, bB=bB):
                                for fp in range(16):
                                    k = fp % 4
                                    if fp < 4:
                                        sy.wait_ge(sem["pe"], w2_consumed_pe[k])
                                    else:
                                        sy.wait_ge(sem["pe"], bB["pe"] + 8 * (fp - 3))
                                    c.inc(sy.dma_start(
                                        out=w2t[k][:],
                                        in_=w2b_d[e][fp * 256:(fp + 1) * 256, dc * 512:(dc + 1) * 512]
                                            .rearrange("(a p) d -> p a d", p=128)), LDP[k], 16)

                            @block.tensor
                            def _(t, e=e, sp=sp, dc=dc, bB=bB, bA=bA):
                                for fp in range(16):
                                    k = fp % 4
                                    t.wait_ge(sem[LDP[k]], bB[LDP[k]] + 16 * (fp // 4 + 1))
                                    if fp == 0:
                                        t.wait_ge(sem["dve"], prev_evac_dve[0])
                                        t.wait_ge(sem["act"], bA["act"] + 32)
                                    for half in range(2):
                                        fc = 2 * fp + half
                                        for tt4 in range(4):
                                            c.inc(t.matmul(pb[2 + tt4][:, 0:512],
                                                           actT[:, fc, tt4 * 128:(tt4 + 1) * 128],
                                                           w2t[k][:, half, :],
                                                           start=(fc == 0), stop=(fc == 31)), "pe")
                                    w2_consumed_pe[k] = c.v["pe"]
                            peB_end_prev2[0] = c.v["pe"]

                            @block.vector
                            def _(v, e=e, sp=sp, dc=dc, bB=bB, tids_dve=tids_dve):
                                v.wait_ge(sem["pe"], bB["pe"] + 128)
                                v.wait_ge(sem["dve"], tids_dve)
                                for tt4 in range(4):
                                    gp = "gM" if tt4 % 2 == 0 else "gN"
                                    v.wait_ge(sem[gp], bB[gp] + 16 * (tt4 // 2))
                                    j = sp * 4 + tt4
                                    c.inc(v.tensor_scalar(ot[tt4 % 2][:], pb[2 + tt4][:, 0:512],
                                                          cme[e % 2][:, j, 0:1], None, ALU.mult), "dve")
                            prev_evac_dve[0] = c.v["dve"]

                            @block.gpsimd
                            def _(g, e=e, sp=sp, dc=dc, bB=bB):
                                g.wait_ge(sem["gM"], bB["gM"])
                                g.wait_ge(sem["gN"], bB["gN"])
                                for tt4 in range(4):
                                    gp = "gM" if tt4 % 2 == 0 else "gN"
                                    g.wait_ge(sem["dve"], bB["dve"] + (CAP // 128) * 0 + tt4 + 1 +
                                              (c.v["dve"] - bB["dve"]) * 0)
                                    j = sp * 4 + tt4
                                    c.inc(g.indirect_dma_start(
                                        out=ysc_d,
                                        out_offset=IndirectOffsetOnAxis(tids[e % 2][:, j:j + 1], 0),
                                        in_=ot[tt4 % 2][:], in_offset=None,
                                        element_offset=dc * 512,
                                        bounds_check=bc_regs[1], oob_is_err=False), gp, 16)
                    ysc_done[e] = {n: c.v[n] for n in ("gM", "gN")}

                @block.gpsimd
                def _(g):
                    c.wait(g, "gM")
                    c.wait(g, "gN")
                barrier()

        peB_end_prev2 = [0]

        def emit_residual4(x_src, acc_src, ysc_src, dst_ap):
            with ExitStack() as pc:
                xt = [usb(pc, f"r4_x{i}", [128, D], F32) for i in range(2)]
                at = [usb(pc, f"r4_a{i}", [128, D], F32) for i in range(2)]
                y0t = [usb(pc, f"r4_y0{i}", [128, D], F32) for i in range(2)]
                y1t = [usb(pc, f"r4_y1{i}", [128, D], F32) for i in range(2)]
                so = [usb(pc, f"r4_s{i}", [128, D], F32) for i in range(2)]
                b = c.snap()
                NDV4 = 3

                @block.sync
                def _(sy, b=b):
                    for u in range(NT):
                        xp = "ldA" if u % 2 == 0 else "ldB"
                        ap_ = "ldC" if u % 2 == 0 else "ldD"
                        y0p = "stC" if u % 2 == 0 else "stD"
                        y1p = "ldE" if u % 2 == 0 else "ldF"
                        if u >= 2:
                            sy.wait_ge(sem["dve"], b["dve"] + NDV4 * (u - 1))
                        c.inc(sy.dma_start(out=xt[u % 2][:], in_=x_src[u * 128:(u + 1) * 128, :]), xp, 16)
                        c.inc(sy.dma_start(out=at[u % 2][:], in_=acc_src[u * 128:(u + 1) * 128, :]), ap_, 16)
                        c.inc(sy.dma_start(out=y0t[u % 2][:], in_=ysc_src[u * 128:(u + 1) * 128, :]), y0p, 16)
                        c.inc(sy.dma_start(out=y1t[u % 2][:], in_=ysc_src[T + u * 128:T + (u + 1) * 128, :]), y1p, 16)
                        if u >= 1:
                            stp = "stA" if (u - 1) % 2 == 0 else "stB"
                            sy.wait_ge(sem["dve"], b["dve"] + NDV4 * u)
                            c.inc(sy.dma_start(out=dst_ap[(u - 1) * 128:u * 128, :], in_=so[(u - 1) % 2][:]), stp, 16)
                    stp = "stA" if (NT - 1) % 2 == 0 else "stB"
                    sy.wait_ge(sem["dve"], b["dve"] + NDV4 * NT)
                    c.inc(sy.dma_start(out=dst_ap[(NT - 1) * 128:NT * 128, :], in_=so[(NT - 1) % 2][:]), stp, 16)
                    c.wait(sy, "stA")
                    c.wait(sy, "stB")

                @block.vector
                def _(v, b=b):
                    for u in range(NT):
                        xp = "ldA" if u % 2 == 0 else "ldB"
                        ap_ = "ldC" if u % 2 == 0 else "ldD"
                        y0p = "stC" if u % 2 == 0 else "stD"
                        y1p = "ldE" if u % 2 == 0 else "ldF"
                        for n in (xp, ap_, y0p, y1p):
                            v.wait_ge(sem[n], b[n] + 16 * (u // 2 + 1))
                        if u >= 2:
                            stp = "stA" if u % 2 == 0 else "stB"
                            v.wait_ge(sem[stp], b[stp] + 16 * (u // 2))
                        c.inc(v.tensor_tensor(so[u % 2][:], xt[u % 2][:], at[u % 2][:], ALU.add), "dve")
                        c.inc(v.tensor_tensor(so[u % 2][:], so[u % 2][:], y0t[u % 2][:], ALU.add), "dve")
                        c.inc(v.tensor_tensor(so[u % 2][:], so[u % 2][:], y1t[u % 2][:], ALU.add), "dve")
                barrier()

        done = [False]
        def chk(name):
            if stop_after is not None and name == stop_after:
                done[0] = True
            return done[0]

        emit_ln(x0, xn_d)
        if not chk("ln0"):
            with ExitStack() as l0c:
                qT_sb = usb(l0c, "qT_sb", [128, 4, T], BF16)
                with ExitStack() as l0i:
                    xnT_sb = usb(l0i, "xnT0", [128, ND, T], BF16)
                    emit_xnT(xn_d, xnT_sb)
                    if not chk("xnT0"):
                        emit_l0_attn(xnT_sb)
                    if not chk("attn0"):
                        emit_outproj(WTo_d, True, acc_d)
                    if not chk("proj0"):
                        emit_l0_swiglu(xnT_sb)
                    if not chk("swiglu0"):
                        emit_l0_peer_a(xnT_sb, qT_sb)
                    chk("peera")
                if not done[0]:
                    emit_l0_peer_b(qT_sb)
                chk("peerb")
            if not done[0]:
                emit_residual(x0, acc_d, x1_d)
            chk("res0")

        if not done[0]:
            emit_ln(x1_d, xn_d)
        if not chk("ln1"):
            with ExitStack() as l1c:
                g_sb = usb(l1c, "g_sb", [128, NT, 8], F32)
                s_sb = usb(l1c, "s_sb", [128, NT, 8], F32)
                with ExitStack() as l1i:
                    xnT_sb = usb(l1i, "xnT1", [128, ND, T], BF16)
                    emit_xnT(xn_d, xnT_sb)
                    if not chk("xnT1"):
                        emit_l1_attn_a(xnT_sb)
                    if not chk("attn1a"):
                        emit_l1_attn_b()
                    if not chk("attn1b"):
                        emit_outproj(Wout1, False, acc_d)
                    if not chk("proj1"):
                        emit_l1_router(xnT_sb, g_sb, s_sb)
                    if not chk("router1") and not sparse:
                        emit_l1_moe(xnT_sb, g_sb)
                if not done[0] and sparse:
                    emit_l1_moe_sparse(g_sb, s_sb)
                chk("moe1")
            if not done[0]:
                if sparse:
                    emit_residual4(x1_d, acc_d, ysc_d, out)
                else:
                    emit_residual(x1_d, acc_d, out)

        _finish(nc, block, c, dbg, dbg_map, dbg_out)

    return nc


def _finish(nc, block, c, dbg, dbg_map, dbg_out):
    sem = c.sem
    if dbg is not None:
        src_ap = dbg_map[dbg]
        @block.gpsimd
        def _(g):
            c.wait(g, "st")
            c.inc(g.dma_start(out=dbg_out.rearrange("(a p) b -> p a b", p=128),
                              in_=src_ap.rearrange("(a p) b -> p a b", p=128)), "g", 16)

    @block.sync
    def _(sy):
        c.wait(sy, "st")
        c.wait(sy, "ld")

    @block.gpsimd
    def _(g):
        c.wait(g, "g")
        c.wait(g, "st")


def kernel(**inputs):
    x_full = np.ascontiguousarray(inputs["L0_attn_x"])
    weights = {k: np.ascontiguousarray(v) for k, v in inputs.items() if k != "L0_attn_x"}
    nc = build_program()
    in_maps = []
    for core in range(8):
        m = {"x": np.ascontiguousarray(x_full[core])}
        m.update(weights)
        in_maps.append(m)
    res = run_bass_kernel_spmd(nc, in_maps, core_ids=list(range(8)))
    return np.stack([r["out"] for r in res.results], axis=0)


# version 10 (final)
# speedup vs baseline: 1.0280x; 1.0251x over previous
import numpy as np
import ml_dtypes
from contextlib import ExitStack

import concourse.bass as bass
import concourse.mybir as mybir
from concourse.bass_utils import run_bass_kernel_spmd

F32 = mybir.dt.float32
BF16 = mybir.dt.bfloat16
AF = mybir.ActivationFunctionType
ALU = mybir.AluOpType
AX = mybir.AxisListType

T = 4096
D = 1024
NT = T // 128
ND = D // 128
DCD = 4096
NEXP = 4096
PEER_H = 4
F1 = 4096
E1 = 8
NEG = -1.0e30


def _rope_tables():
    inv_freq = (1.0 / (500000.0 ** (np.arange(0, 64, 2, dtype=np.float32) / 64))).astype(np.float32)
    t = np.arange(T, dtype=np.float32)
    freqs = (t[:, None] * inv_freq[None, :]).astype(np.float32)
    return np.cos(freqs).astype(np.float32), np.sin(freqs).astype(np.float32)


def _diag_masks():
    kk = np.arange(128)[:, None]
    qq = np.arange(512)[None, :]
    ms = [np.where(qq >= o * 128 + kk, 0.0, NEG).astype(np.float32) for o in range(4)]
    ms.append(np.full((128, 512), NEG, np.float32))
    return np.stack(ms)


class Ctrs:
    def __init__(self, ctx, nc, names):
        self.sem = {}
        self.v = {}
        for n in names:
            self.sem[n] = ctx.enter_context(nc.semaphore(n))
            self.v[n] = 0

    def inc(self, instr, name, n=1):
        if name in ("dve", "act"):
            instr._wait_ge(self.sem[name], self.v[name])
        instr.then_inc(self.sem[name], n)
        self.v[name] += n
        return instr

    def wait(self, eng, name, val=None):
        eng.wait_ge(self.sem[name], self.v[name] if val is None else val)

    def snap(self):
        return dict(self.v)


def build_program(stop_after=None, dbg=None, single_core_debug=False, sparse=False):
    nc = bass.Bass("TRN2", target_bir_lowering=False, debug=False, dynamic_dma_scratch_size=16384)

    inp = {}
    def ein(name, shape):
        inp[name] = nc.dram_tensor(name, list(shape), F32, kind="ExternalInput").ap()
        return inp[name]

    x0 = ein("x", (T, D))
    Wq0 = ein("L0_attn_q_heads", (D, D))
    Wk0 = ein("L0_attn_k_heads", (D, D))
    Wv0 = ein("L0_attn_v_heads", (D, D))
    Wo0 = ein("L0_attn_output_proj", (D, D))
    Wg0 = ein("L0_ffn_Wg", (D, DCD))
    bg0 = ein("L0_ffn_bg", (DCD,))
    Wu0 = ein("L0_ffn_Wu", (D, DCD))
    bu0 = ein("L0_ffn_bu", (DCD,))
    Wd0 = ein("L0_ffn_Wd", (DCD, D))
    bd0 = ein("L0_ffn_bd", (D,))
    Wqp = ein("L0_ffn_Wq", (D, 512))
    keys0 = ein("L0_ffn_keys", (4, 64, 2, 64))
    dem0 = ein("L0_ffn_down_embed", (NEXP, D))
    uem0 = ein("L0_ffn_up_embed", (NEXP, D))
    Wqkv1 = ein("L1_attn_W_qkv", (D, 3 * D))
    Wout1 = ein("L1_attn_W_out", (D, D))
    Wg1 = ein("L1_ffn_Wg", (E1, D))
    W11 = ein("L1_ffn_W1", (E1, D, F1))
    W21 = ein("L1_ffn_W2", (E1, F1, D))

    out = nc.dram_tensor("out", [T, D], F32, kind="ExternalOutput").ap()

    def scratch(name, shape, dtype):
        return nc.dram_tensor(name, list(shape), dtype, kind="Internal").ap()

    xn_d = scratch("xn_d", (T, D), BF16)
    wcast_d = scratch("wcast_d", (DCD, D), BF16)
    wcast2_d = scratch("wcast2_d", (DCD, D), BF16)
    WTq_d = scratch("WTq_d", (D, D), BF16)
    WTk_d = scratch("WTk_d", (D, D), BF16)
    WTv_d = scratch("WTv_d", (D, D), BF16)
    WTo_d = scratch("WTo_d", (D, D), BF16)
    deT_d = scratch("deT_d", (D, NEXP), BF16)
    wgb_d = scratch("wgb_d", (D, DCD), BF16)
    wub_d = scratch("wub_d", (D, DCD), BF16)
    wdb_d = scratch("wdb_d", (DCD, D), BF16)
    wall_d = scratch("wall_d", (T, NEXP), BF16)
    acc_d = scratch("acc_d", (T, D), F32)
    x1_d = scratch("x1_d", (T, D), F32)
    aoT_d = scratch("aoT_d", (D, T), BF16)
    qfT_d = scratch("qfT_d", (D, T), BF16)
    w2b_d = scratch("w2b_d", (E1, F1, D), BF16)
    CAP = 1536
    cx_d = scratch("cx_d", (E1 * CAP, D), BF16)
    cmeta_d = scratch("cmeta_d", (E1 * CAP, 8), F32)
    pos_d = scratch("pos_d", (8, T), F32)
    ysc_d = scratch("ysc_d", (2 * T, D), F32)

    dbg_map = {"xn_d": xn_d, "acc_d": acc_d, "x1_d": x1_d, "aoT_d": aoT_d,
               "wall_d": wall_d, "qfT_d": qfT_d, "WTq_d": WTq_d, "deT_d": deT_d,
               "WTo_d": WTo_d, "ysc_d": ysc_d}
    dbg_out = None
    if dbg is not None:
        dbg_out = nc.dram_tensor("dbg", list(dbg_map[dbg].shape), F32,
                                 kind="ExternalOutput").ap()

    cos_np, sin_np = _rope_tables()
    cosb = nc.inline_tensor(cos_np.astype(ml_dtypes.bfloat16), "cosb").ap()
    sinb = nc.inline_tensor(sin_np.astype(ml_dtypes.bfloat16), "sinb").ap()
    msinb = nc.inline_tensor((-sin_np).astype(ml_dtypes.bfloat16), "msinb").ap()
    idb_c = nc.inline_tensor(np.eye(128, dtype=np.float32).astype(ml_dtypes.bfloat16), "idb").ap()
    idf_c = nc.inline_tensor(np.eye(128, dtype=np.float32), "idf").ap()
    onesb_c = nc.inline_tensor(np.ones((128, 2), np.float32).astype(ml_dtypes.bfloat16), "onesb").ap()
    onesrb_c = nc.inline_tensor(np.ones((1, 128), np.float32).astype(ml_dtypes.bfloat16), "onesrb").ap()
    onesrf_c = nc.inline_tensor(np.ones((1, 128), np.float32), "onesrf").ap()
    dmask_c = nc.inline_tensor(_diag_masks(), "dmask").ap()
    epsc_c = nc.inline_tensor(np.tile(np.array([[1e-5, 1e-6]], np.float32), (128, 1)), "epsc").ap()
    tid_np = (np.arange(NT)[None, :] * 128 + np.arange(128)[:, None]).astype(np.float32)
    tid_c = nc.inline_tensor(tid_np, "tid_c").ap()

    ctx = ExitStack()
    with ctx:
        c = Ctrs(ctx, nc, ["ld", "ldA", "ldB", "ldC", "ldD", "ldE", "ldF", "g", "gA", "gB", "gC", "gD", "gE", "gF", "gG", "gH", "gI", "gJ", "gK", "gL", "gM", "gN", "g1", "g2", "pe", "dve", "act", "st", "stA", "stB", "stC", "stD"])
        sem = c.sem

        def sb(name, shape, dtype):
            return ctx.enter_context(nc.sbuf_tensor(name, list(shape), dtype))

        _uid = [0]
        def usb(pc, name, shape, dtype):
            _uid[0] += 1
            return pc.enter_context(nc.sbuf_tensor(f"{name}_{_uid[0]}", list(shape), dtype))

        keysT_sb = sb("keysT_sb", (128, 8, 64), BF16)
        wgT_sb = sb("wgT_sb", (128, ND, 8), BF16)
        id_sb = sb("id_sb", (128, 128), BF16)
        idf_sb = sb("idf_sb", (128, 128), F32)
        ones_sb = sb("ones_sb", (128, 2), BF16)
        onesr_sb = sb("onesr_sb", (1, 128), BF16)
        onesrf_sb = sb("onesrf_sb", (1, 128), F32)
        cos_sb = sb("cos_sb", (128, NT, 32), BF16)
        sin_sb = sb("sin_sb", (128, NT, 32), BF16)
        msin_sb = sb("msin_sb", (128, NT, 32), BF16)
        dm_sb = sb("dm_sb", (128, 5, 512), F32)
        bdb_sb = sb("bdb_sb", (128, ND, 128), F32)
        bg_sb = sb("bg_sb", (128, 32), F32)
        eps_sb = sb("eps_sb", (128, 2), F32)
        tid_sb = sb("tid_sb", (128, NT), F32)
        bu_sb = sb("bu_sb", (128, 32), F32)

        pq = [ctx.enter_context(nc.psum_tensor(f"pq{i}", [128, 1024], F32)) for i in range(2)]
        pb45 = [ctx.enter_context(nc.psum_tensor(f"pb{i}", [128, 512], F32)) for i in range(2)]
        pbt = [ctx.enter_context(nc.psum_tensor(f"pbt{i}", [128, 1024], BF16)) for i in range(2)]
        pb = [pq[0].ap()[:, 0:512], pq[0].ap()[:, 512:1024],
              pq[1].ap()[:, 0:512], pq[1].ap()[:, 512:1024],
              pb45[0].ap(), pb45[1].ap()]

        block = ctx.enter_context(nc.Block())

        _last_bar = [None]

        def barrier():
            cur = c.snap()
            prev = _last_bar[0]
            changed = [n for n in cur
                       if (prev is None or cur[n] != prev[n]) and n != "g1"]
            def mk(e, changed=changed, cur=cur):
                for n in changed:
                    e.wait_ge(sem[n], cur[n])
            block.tensor(mk)
            block.vector(mk)
            block.scalar(mk)
            block.sync(mk)
            block.gpsimd(mk)
            _last_bar[0] = cur

        with ExitStack() as s0:
            tstage = [s0.enter_context(nc.sbuf_tensor(f"tstage{i}", [128, DCD], BF16))
                      for i in range(2)]
            kload = s0.enter_context(nc.sbuf_tensor("kload", [64, 8, 64], BF16))
            wgld = s0.enter_context(nc.sbuf_tensor("wgld", [8, D], BF16))
            bsm = s0.enter_context(nc.sbuf_tensor("bsm", [32, 128], F32))
            brow = s0.enter_context(nc.sbuf_tensor("brow", [1, D], F32))

            @block.sync
            def _(sy):
                for dst, src in [(id_sb[:], idb_c), (idf_sb[:], idf_c),
                                 (ones_sb[:], onesb_c), (onesr_sb[:], onesrb_c),
                                 (onesrf_sb[:], onesrf_c)]:
                    c.inc(sy.dma_start(out=dst, in_=src), "ld", 16)
                c.inc(sy.dma_start(out=cos_sb[:], in_=cosb.rearrange("(a p) b -> p a b", p=128)), "ld", 16)
                c.inc(sy.dma_start(out=sin_sb[:], in_=sinb.rearrange("(a p) b -> p a b", p=128)), "ld", 16)
                c.inc(sy.dma_start(out=msin_sb[:], in_=msinb.rearrange("(a p) b -> p a b", p=128)), "ld", 16)
                c.inc(sy.dma_start(out=dm_sb[:], in_=dmask_c.rearrange("a p b -> p a b")), "ld", 16)
                c.inc(sy.dma_start(out=bsm[:], in_=bg0.rearrange("(a b) -> a b", b=128)), "ld", 16)
                c.inc(sy.dma_start(out=brow[:], in_=bd0.rearrange("(a b) -> a b", a=1)), "ld", 16)
                c.inc(sy.dma_start(out=eps_sb[:], in_=epsc_c), "ld", 16)
                c.inc(sy.dma_start(out=tid_sb[:], in_=tid_c), "ld", 16)

            tr_ld_after = {}
            for wi, (w_ap, wt_ap, R) in enumerate(
                    [(Wq0, WTq_d, D), (Wk0, WTk_d, D), (Wv0, WTv_d, D),
                     (Wo0, WTo_d, D), (dem0, deT_d, NEXP)]):
                wc = wcast_d if wi % 2 == 0 else wcast2_d
                @block.gpsimd
                def _(g, w_ap=w_ap, R=R, wc=wc, wi=wi):
                    if wi >= 2:
                        g.wait_ge(sem["ld"], tr_ld_after[wi - 2])
                    c.inc(g.dma_start(out=wc[0:R, :].rearrange("(a p) b -> p a b", p=128),
                                      in_=w_ap.rearrange("(a p) b -> p a b", p=128)), "g", 16)
                g_after = c.v["g"]
                b0 = c.snap()
                @block.sync
                def _(sy, wt_ap=wt_ap, R=R, wc=wc, b0=b0, g_after=g_after):
                    sy.wait_ge(sem["g"], g_after)
                    for ch in range(ND):
                        if ch >= 2:
                            sy.wait_ge(sem["st"], b0["st"] + 16 * (ch - 1))
                        else:
                            sy.wait_ge(sem["st"], b0["st"])
                        c.inc(sy.dma_start(out=tstage[ch % 2][:, 0:R],
                                           in_=wc[0:R, ch * 128:(ch + 1) * 128],
                                           transpose=True), "ld", 16)
                        sy.wait_ge(sem["ld"], b0["ld"] + 16 * (ch + 1))
                        c.inc(sy.dma_start(out=wt_ap[ch * 128:(ch + 1) * 128, :],
                                           in_=tstage[ch % 2][:, 0:R]), "st", 16)
                tr_ld_after[wi] = c.v["ld"]

            @block.gpsimd
            def _(g):
                for h in range(4):
                    for p in range(2):
                        c.inc(g.dma_start(out=kload[:, h * 2 + p, :], in_=keys0[h, :, p, :]), "g2", 16)
                c.inc(g.dma_start(out=wgld[:], in_=Wg1), "g2", 16)

            b = c.snap()
            @block.tensor
            def _(t, b=b):
                c.wait(t, "g2")
                c.wait(t, "ld")
                for i in range(8):
                    c.inc(t.transpose(pbt[0][0:64, i * 64:(i + 1) * 64], kload[:, i, :],
                                      id_sb[0:64, 0:64]), "pe")
                for dc in range(ND):
                    c.inc(t.transpose(pbt[1][0:128, dc * 8:(dc + 1) * 8],
                                      wgld[0:8, dc * 128:(dc + 1) * 128], id_sb[0:8, 0:8]), "pe")
                c.inc(t.transpose(pb[2][0:128, 0:32], bsm[:], idf_sb[0:32, 0:32]), "pe")
                for half in range(2):
                    if half == 1:
                        t.wait_ge(sem["dve"], b["dve"] + 29)
                    c.inc(t.matmul(pb[3][:, 0:512], onesrf_sb[:],
                                   brow[:, half * 512:(half + 1) * 512], start=True, stop=True), "pe")

            @block.vector
            def _(v, b=b):
                v.wait_ge(sem["pe"], b["pe"] + 8)
                for i in range(8):
                    c.inc(v.tensor_copy(keysT_sb[0:64, i, :], pbt[0][0:64, i * 64:(i + 1) * 64]), "dve")
                    c.inc(v.tensor_copy(keysT_sb[64:128, i, :], pbt[0][0:64, i * 64:(i + 1) * 64]), "dve")
                v.wait_ge(sem["pe"], b["pe"] + 16)
                for dc in range(ND):
                    c.inc(v.tensor_copy(wgT_sb[:, dc, :], pbt[1][0:128, dc * 8:(dc + 1) * 8]), "dve")
                v.wait_ge(sem["pe"], b["pe"] + 17)
                c.inc(v.tensor_copy(bg_sb[:], pb[2][0:128, 0:32]), "dve")
                v.wait_ge(sem["pe"], b["pe"] + 18)
                for j in range(4):
                    c.inc(v.tensor_copy(bdb_sb[:, j, :], pb[3][:, j * 128:(j + 1) * 128]), "dve")
                v.wait_ge(sem["pe"], b["pe"] + 19)
                for j in range(4):
                    c.inc(v.tensor_copy(bdb_sb[:, 4 + j, :], pb[3][:, j * 128:(j + 1) * 128]), "dve")
            barrier()

            b = c.snap()
            @block.sync
            def _(sy, b=b):
                c.inc(sy.dma_start(out=bsm[:], in_=bu0.rearrange("(a b) -> a b", b=128)), "ld", 16)
            @block.tensor
            def _(t, b=b):
                t.wait_ge(sem["ld"], b["ld"] + 16)
                c.inc(t.transpose(pb[2][0:128, 0:32], bsm[:], idf_sb[0:32, 0:32]), "pe")
            @block.vector
            def _(v, b=b):
                v.wait_ge(sem["pe"], b["pe"] + 1)
                c.inc(v.tensor_copy(bu_sb[:], pb[2][0:128, 0:32]), "dve")
            barrier()

        if stop_after == "S0":
            _finish(nc, block, c, dbg, dbg_map, dbg_out)
            return nc

        @block.gpsimd
        def _(g):
            for e in range(E1):
                c.inc(g.dma_start(out=w2b_d[e].rearrange("(a p) b -> p a b", p=128),
                                  in_=W21[e].rearrange("(a p) b -> p a b", p=128)), "g1", 16)
            for dst, src in [(wgb_d, Wg0), (wub_d, Wu0), (wdb_d, Wd0)]:
                c.inc(g.dma_start(out=dst.rearrange("(a p) b -> p a b", p=128),
                                  in_=src.rearrange("(a p) b -> p a b", p=128)), "g1", 16)
        g1_ready = c.v["g1"]

        def emit_ln(x_ap, xn_dst):
            with ExitStack() as pc:
                xt = [usb(pc, f"ln_xt{i}", [128, D], F32) for i in range(2)]
                xc = [usb(pc, f"ln_xc{i}", [128, D], F32) for i in range(2)]
                xnb = [usb(pc, f"ln_xn{i}", [128, D], BF16) for i in range(2)]
                sq = usb(pc, "ln_sq", [128, D], F32)
                stt = usb(pc, "ln_st", [128, 8], F32)
                b = c.snap()
                NDV = 5

                @block.sync
                def _(sy, b=b):
                    for u in range(NT):
                        ldp = "ldA" if u % 2 == 0 else "ldB"
                        if u >= 2:
                            sy.wait_ge(sem["dve"], b["dve"] + NDV * (u - 2) + 3)
                        c.inc(sy.dma_start(out=xt[u % 2][:], in_=x_ap[u * 128:(u + 1) * 128, :]), ldp, 16)
                        if u >= 1:
                            stp = "stA" if (u - 1) % 2 == 0 else "stB"
                            sy.wait_ge(sem["dve"], b["dve"] + NDV * u)
                            c.inc(sy.dma_start(out=xn_dst[(u - 1) * 128:u * 128, :], in_=xnb[(u - 1) % 2][:]), stp, 16)
                    stp = "stA" if (NT - 1) % 2 == 0 else "stB"
                    sy.wait_ge(sem["dve"], b["dve"] + NDV * NT)
                    c.inc(sy.dma_start(out=xn_dst[(NT - 1) * 128:NT * 128, :], in_=xnb[(NT - 1) % 2][:]), stp, 16)
                    c.wait(sy, "stA")
                    c.wait(sy, "stB")

                @block.vector
                def _(v, b=b):
                    for u in range(NT):
                        ldp = "ldA" if u % 2 == 0 else "ldB"
                        v.wait_ge(sem[ldp], b[ldp] + 16 * (u // 2 + 1))
                        c.inc(v.reduce_sum(stt[:, 0:1], xt[u % 2][:], axis=AX.X), "dve")
                        c.inc(v.tensor_scalar(stt[:, 1:2], stt[:, 0:1], -1.0 / D, None, ALU.mult), "dve")
                        c.inc(v.tensor_scalar(xc[u % 2][:], xt[u % 2][:], stt[:, 1:2], None, ALU.add), "dve")
                        v.wait_ge(sem["act"], b["act"] + 2 * (u + 1))
                        c.inc(v.reciprocal(stt[:, 4:5], stt[:, 3:4]), "dve")
                        if u >= 2:
                            stp = "stA" if u % 2 == 0 else "stB"
                            v.wait_ge(sem[stp], b[stp] + 16 * (u // 2))
                        c.inc(v.tensor_scalar(xnb[u % 2][:], xc[u % 2][:], stt[:, 4:5], None, ALU.mult), "dve")

                @block.scalar
                def _(s, b=b):
                    for u in range(NT):
                        s.wait_ge(sem["dve"], b["dve"] + NDV * u + 3)
                        c.inc(s.activation(out=sq[:], in_=xc[u % 2][:], func=AF.Square,
                                           accum_out=stt[:, 2:3]), "act")
                        c.inc(s.activation(out=stt[:, 3:4], in_=stt[:, 2:3], func=AF.Sqrt,
                                           bias=eps_sb[:, 0:1], scale=1.0 / D), "act")
                barrier()

        def emit_xnT(xn_src, xnT_sb):
            b = c.snap()
            @block.sync
            def _(sy, b=b):
                for ch in range(ND):
                    c.inc(sy.dma_start(out=xnT_sb[:, ch, :], in_=xn_src[:, ch * 128:(ch + 1) * 128],
                                       transpose=True), "ld", 16)
                c.wait(sy, "ld")
            barrier()

        def emit_outproj(w_src_ap, w_pretransposed, dst_ap):
            with ExitStack() as pc:
                wsl = usb(pc, "op_w", [128, ND, D], BF16)
                at = [usb(pc, f"op_a{i}", [128, ND, 128], BF16) for i in range(2)]
                ot = [usb(pc, f"op_o{i}", [128, 512], F32) for i in range(4)]
                b = c.snap()

                if w_pretransposed:
                    @block.sync
                    def _(sy, b=b):
                        c.inc(sy.dma_start(out=wsl[:], in_=w_src_ap.rearrange("(a p) b -> p a b", p=128)), "ld", 16)
                else:
                    @block.gpsimd
                    def _(g, b=b):
                        c.inc(g.dma_start(out=wsl[:], in_=w_src_ap.rearrange("(a p) b -> p a b", p=128)), "g", 16)

                wofs = 16 if w_pretransposed else 0

                @block.sync
                def _(sy, b=b):
                    for u in range(NT):
                        ldp = "ldA" if u % 2 == 0 else "ldB"
                        if u >= 2:
                            sy.wait_ge(sem["pe"], b["pe"] + 16 * (u - 1))
                        c.inc(sy.dma_start(
                            out=at[u % 2][:],
                            in_=aoT_d[:, u * 128:(u + 1) * 128].rearrange("(a p) t -> p a t", p=128)),
                            ldp, 16)

                @block.tensor
                def _(t, b=b):
                    if not w_pretransposed:
                        t.wait_ge(sem["g"], b["g"] + 16)
                    else:
                        t.wait_ge(sem["ld"], b["ld"] + 16)
                    for u in range(NT):
                        ldp = "ldA" if u % 2 == 0 else "ldB"
                        t.wait_ge(sem[ldp], b[ldp] + 16 * (u // 2 + 1))
                        for fc in range(2):
                            k = 2 * u + fc
                            if k >= 4:
                                t.wait_ge(sem["dve"], b["dve"] + k - 3)
                            for vc in range(ND):
                                c.inc(t.matmul(pb[k % 4][:, 0:512], at[u % 2][:, vc, :],
                                               wsl[:, vc, fc * 512:(fc + 1) * 512],
                                               start=(vc == 0), stop=(vc == ND - 1)), "pe")

                STP = ["gA", "gB", "gC", "gD"]

                @block.vector
                def _(v, b=b):
                    for u in range(NT):
                        for fc in range(2):
                            k = 2 * u + fc
                            v.wait_ge(sem["pe"], b["pe"] + 8 * (k + 1))
                            if k >= 4:
                                v.wait_ge(sem[STP[k % 4]], b[STP[k % 4]] + 16 * (k // 4))
                            c.inc(v.tensor_copy(ot[k % 4][:], pb[k % 4][:, 0:512]), "dve")

                @block.gpsimd
                def _(g, b=b):
                    for u in range(NT):
                        for fc in range(2):
                            k = 2 * u + fc
                            g.wait_ge(sem["dve"], b["dve"] + k + 1)
                            c.inc(g.dma_start(out=dst_ap[u * 128:(u + 1) * 128, fc * 512:(fc + 1) * 512],
                                              in_=ot[k % 4][:]), STP[k % 4], 16)
                    for n in STP:
                        c.wait(g, n)
                barrier()

        def emit_l0_attn(xnT_sb):
            with ExitStack() as pc:
                wt_sb = usb(pc, "a0_wt", [128, ND, 128], BF16)
                qtA = usb(pc, "a0_qA", [65, T], BF16)
                qtB = usb(pc, "a0_qB", [65, T], BF16)
                ktA = usb(pc, "a0_kA", [65, T], BF16)
                ktB = usb(pc, "a0_kB", [65, T], BF16)
                vtA = usb(pc, "a0_vA", [128, NT, 65], BF16)
                vtB = usb(pc, "a0_vB", [128, NT, 65], BF16)
                ptil = [usb(pc, f"a0_p{i}", [128, 1024], BF16) for i in range(2)]
                sqs = usb(pc, "a0_sq", [64, 512], BF16)
                kmax = usb(pc, "a0_km", [1, 4], F32)
                qnr = usb(pc, "a0_qn", [1, 512], F32)
                rr = usb(pc, "a0_rr", [1, 512], F32)
                rrb = usb(pc, "a0_rrb", [1, 512], BF16)
                rsb = usb(pc, "a0_rsb", [64, 512], F32)
                aop = usb(pc, "a0_ao", [128, T], BF16)

                for pair in range(8):
                    b = c.snap()
                    @block.sync
                    def _(sy, pair=pair, b=b):
                        for wi, wt_d in enumerate([WTq_d, WTk_d, WTv_d]):
                            if wi > 0:
                                sy.wait_ge(sem["pe"], b["pe"] + 64 * wi)
                            c.inc(sy.dma_start(
                                out=wt_sb[:],
                                in_=wt_d[:, pair * 128:(pair + 1) * 128].rearrange("(a p) f -> p a f", p=128)),
                                "ld", 16)

                    @block.tensor
                    def _(t, pair=pair, b=b):
                        for u in range(16):
                            wi, sp = divmod(u, 8)
                            t.wait_ge(sem["ld"], b["ld"] + 16 * (wi + 1))
                            if u >= 2:
                                t.wait_ge(sem["dve"], b["dve"] + 2 * (u - 1))
                            for dc in range(ND):
                                c.inc(t.matmul(pb[u % 2][:, 0:512], wt_sb[:, dc, :],
                                               xnT_sb[:, dc, sp * 512:(sp + 1) * 512],
                                               start=(dc == 0), stop=(dc == ND - 1)), "pe")
                        for tt in range(NT):
                            u = 16 + tt
                            t.wait_ge(sem["ld"], b["ld"] + 48)
                            t.wait_ge(sem["dve"], b["dve"] + 2 * (u - 1))
                            for dc in range(ND):
                                c.inc(t.matmul(pb[u % 2][:, 0:128], xnT_sb[:, dc, tt * 128:(tt + 1) * 128],
                                               wt_sb[:, dc, :],
                                               start=(dc == 0), stop=(dc == ND - 1)), "pe")

                    @block.vector
                    def _(v, pair=pair, b=b):
                        for u in range(16):
                            wi, sp = divmod(u, 8)
                            dA = qtA if wi == 0 else ktA
                            dB = qtB if wi == 0 else ktB
                            v.wait_ge(sem["pe"], b["pe"] + 8 * (u + 1))
                            c.inc(v.tensor_copy(dA[0:64, sp * 512:(sp + 1) * 512], pb[u % 2][0:64, 0:512]), "dve")
                            c.inc(v.tensor_copy(dB[0:64, sp * 512:(sp + 1) * 512], pb[u % 2][64:128, 0:512]), "dve")
                        for tt in range(NT):
                            u = 16 + tt
                            v.wait_ge(sem["pe"], b["pe"] + 8 * (u + 1))
                            c.inc(v.tensor_copy(vtA[:, tt, 0:64], pb[u % 2][:, 0:64]), "dve")
                            c.inc(v.tensor_copy(vtB[:, tt, 0:64], pb[u % 2][:, 64:128]), "dve")
                        c.inc(v.memset(vtA[:, :, 64:65], 1.0), "dve")
                        c.inc(v.memset(vtB[:, :, 64:65], 1.0), "dve")
                        c.inc(v.memset(ktA[64:65, :], 1.0), "dve")
                        c.inc(v.memset(ktB[64:65, :], 1.0), "dve")
                    barrier()

                    for hh in range(2):
                        head = pair * 2 + hh
                        QT = qtA if hh == 0 else qtB
                        KT = ktA if hh == 0 else ktB
                        VT = vtA if hh == 0 else vtB

                        b = c.snap()
                        @block.scalar
                        def _(s, KT=KT, b=b):
                            for sp in range(8):
                                if sp >= 1:
                                    s.wait_ge(sem["pe"], b["pe"] + sp)
                                c.inc(s.activation(out=sqs[:], in_=KT[0:64, sp * 512:(sp + 1) * 512],
                                                   func=AF.Square), "act")
                        @block.tensor
                        def _(t, b=b):
                            for sp in range(8):
                                t.wait_ge(sem["act"], b["act"] + sp + 1)
                                if sp >= 1:
                                    t.wait_ge(sem["dve"], b["dve"] + 3 * (sp - 1) + 1)
                                c.inc(t.matmul(pb[5][0:1, 0:512], ones_sb[0:64, 0:1], sqs[:],
                                               start=True, stop=True), "pe")
                        @block.vector
                        def _(v, b=b):
                            for sp in range(8):
                                v.wait_ge(sem["pe"], b["pe"] + sp + 1)
                                c.inc(v.tensor_copy(qnr[:], pb[5][0:1, 0:512]), "dve")
                                c.inc(v.reduce_max(kmax[:, 2:3], qnr[:], axis=AX.X), "dve")
                                if sp == 0:
                                    c.inc(v.tensor_copy(kmax[:, 0:1], kmax[:, 2:3]), "dve")
                                else:
                                    c.inc(v.tensor_tensor(kmax[:, 0:1], kmax[:, 0:1], kmax[:, 2:3],
                                                          ALU.max), "dve")
                        @block.scalar
                        def _(s, b=b):
                            s.wait_ge(sem["dve"], b["dve"] + 24)
                            c.inc(s.activation(out=kmax[:, 1:2], in_=kmax[:, 0:1], func=AF.Sqrt), "act")
                        barrier()

                        b = c.snap()
                        @block.scalar
                        def _(s, QT=QT, b=b):
                            for sp in range(8):
                                if sp >= 1:
                                    s.wait_ge(sem["pe"], b["pe"] + sp)
                                c.inc(s.activation(out=sqs[:], in_=QT[0:64, sp * 512:(sp + 1) * 512],
                                                   func=AF.Square), "act")
                                s.wait_ge(sem["pe"], b["pe"] + sp + 1)
                                if sp >= 1:
                                    s.wait_ge(sem["dve"], b["dve"] + sp)
                                c.inc(s.activation(out=qnr[:], in_=pb[5][0:1, 0:512], func=AF.Sqrt), "act")
                        @block.tensor
                        def _(t, b=b):
                            for sp in range(8):
                                t.wait_ge(sem["act"], b["act"] + 2 * sp + 1)
                                c.inc(t.matmul(pb[5][0:1, 0:512], ones_sb[0:64, 0:1], sqs[:],
                                               start=True, stop=True), "pe")
                        @block.vector
                        def _(v, QT=QT, b=b):
                            for sp in range(8):
                                v.wait_ge(sem["act"], b["act"] + 2 * (sp + 1))
                                c.inc(v.tensor_scalar(QT[64:65, sp * 512:(sp + 1) * 512], qnr[:],
                                                      kmax[:, 1:2], -0.5, ALU.mult, ALU.mult), "dve")
                        barrier()

                        b = c.snap()
                        units = []
                        for qs2 in range(4):
                            for kb in range(8 * qs2 + 8):
                                units.append((qs2, kb))
                        U = len(units)
                        is_last = [kb == 8 * qs2 + 7 for (qs2, kb) in units]
                        odiag = [kb - 8 * qs2 for (qs2, kb) in units]
                        span_last_idx = {}
                        for i, (qs2, kb) in enumerate(units):
                            if is_last[i]:
                                span_last_idx[qs2] = i
                        pe_ops = []
                        for i in range(U):
                            pe_ops.append(("za", i))
                            pe_ops.append(("zb", i))
                            if i >= 1:
                                pe_ops.append(("ava", i - 1))
                                pe_ops.append(("avb", i - 1))
                                if is_last[i - 1]:
                                    pe_ops.append(("r0", units[i - 1][0]))
                                    pe_ops.append(("r1", units[i - 1][0]))
                        pe_ops += [("ava", U - 1), ("avb", U - 1),
                                   ("r0", units[U - 1][0]), ("r1", units[U - 1][0])]
                        pe_pos = {op: idx + 1 for idx, op in enumerate(pe_ops)}
                        dve_mask_end = {}
                        dve_evac_base = {}
                        d = 0
                        for i, (qs2, kb) in enumerate(units):
                            o = odiag[i]
                            nm = 0 if o < 0 else (1 if o <= 3 else 2)
                            d += nm
                            dve_mask_end[i] = d
                            if is_last[i]:
                                dve_evac_base[qs2] = d
                                d += 8

                        @block.tensor
                        def _(t, QT=QT, KT=KT, VT=VT, b=b, units=units, pe_pos=pe_pos,
                              dve_mask_end=dve_mask_end, dve_evac_base=dve_evac_base,
                              is_last=is_last, span_last_idx=span_last_idx):
                            rev_last = {span_last_idx[s_]: s_ for s_ in span_last_idx}
                            for op in pe_ops:
                                kind, j = op
                                if kind in ("za", "zb"):
                                    i = j
                                    qs2, kb = units[i]
                                    if kind == "za":
                                        if i >= 2:
                                            t.wait_ge(sem["act"], b["act"] + (i - 1))
                                            if (i - 2) in rev_last:
                                                t.wait_ge(sem["dve"], b["dve"] +
                                                          dve_evac_base[rev_last[i - 2]] + 7)
                                        half = pq[i % 2].ap()[:, 0:512]
                                        qsl = QT[:, qs2 * 1024:qs2 * 1024 + 512]
                                    else:
                                        half = pq[i % 2].ap()[:, 512:1024]
                                        qsl = QT[:, qs2 * 1024 + 512:qs2 * 1024 + 1024]
                                    c.inc(t.matmul(half, KT[:, kb * 128:(kb + 1) * 128], qsl,
                                                   start=True, stop=True), "pe")
                                elif kind in ("ava", "avb"):
                                    i = j
                                    qs2, kb = units[i]
                                    t.wait_ge(sem["act"], b["act"] + i + 1)
                                    if kb == 0 and qs2 >= 1 and kind == "ava":
                                        t.wait_ge(sem["dve"], b["dve"] + dve_evac_base[qs2 - 1] + 8)
                                    dst = pb45[0] if kind == "ava" else pb45[1]
                                    rhs = ptil[i % 2][:, 0:512] if kind == "ava" else ptil[i % 2][:, 512:1024]
                                    c.inc(t.matmul(dst.ap()[0:65, 0:512], VT[:, kb, :], rhs,
                                                   start=(kb == 0), stop=is_last[i]), "pe")
                                else:
                                    qs2 = j
                                    il = span_last_idx[qs2]
                                    off = 2 if kind == "r0" else 6
                                    t.wait_ge(sem["dve"], b["dve"] + dve_evac_base[qs2] + off)
                                    c.inc(t.matmul(pq[il % 2].ap()[0:64, 0:512],
                                                   onesr_sb[0:1, 0:64], rrb[:],
                                                   start=True, stop=True), "pe")

                        @block.vector
                        def _(v, b=b, units=units, pe_pos=pe_pos, dve_evac_base=dve_evac_base,
                              is_last=is_last, span_last_idx=span_last_idx, head=head):
                            for i, (qs2, kb) in enumerate(units):
                                o = odiag[i]
                                if o >= 0:
                                    v.wait_ge(sem["pe"], b["pe"] + pe_pos[("zb", i)])
                                    if o <= 3:
                                        c.inc(v.tensor_tensor(pq[i % 2].ap()[:, 0:512],
                                                              pq[i % 2].ap()[:, 0:512],
                                                              dm_sb[:, o, :], ALU.add), "dve")
                                    else:
                                        c.inc(v.tensor_tensor(pq[i % 2].ap()[:, 0:512],
                                                              pq[i % 2].ap()[:, 0:512],
                                                              dm_sb[:, 4, :], ALU.add), "dve")
                                        c.inc(v.tensor_tensor(pq[i % 2].ap()[:, 512:1024],
                                                              pq[i % 2].ap()[:, 512:1024],
                                                              dm_sb[:, o - 4, :], ALU.add), "dve")
                                if is_last[i]:
                                    il = i
                                    r0 = (head % 2) * 64
                                    v.wait_ge(sem["pe"], b["pe"] + pe_pos[("ava", i)])
                                    c.inc(v.reciprocal(rr[:], pb45[0].ap()[64:65, 0:512]), "dve")
                                    c.inc(v.tensor_copy(rrb[:], rr[:]), "dve")
                                    v.wait_ge(sem["pe"], b["pe"] + pe_pos[("r0", qs2)])
                                    c.inc(v.tensor_copy(rsb[:], pq[il % 2].ap()[0:64, 0:512]), "dve")
                                    c.inc(v.tensor_tensor(aop[r0:r0 + 64, qs2 * 1024:qs2 * 1024 + 512],
                                                          pb45[0].ap()[0:64, 0:512], rsb[:], ALU.mult), "dve")
                                    v.wait_ge(sem["pe"], b["pe"] + pe_pos[("avb", i)])
                                    c.inc(v.reciprocal(rr[:], pb45[1].ap()[64:65, 0:512]), "dve")
                                    c.inc(v.tensor_copy(rrb[:], rr[:]), "dve")
                                    v.wait_ge(sem["pe"], b["pe"] + pe_pos[("r1", qs2)])
                                    c.inc(v.tensor_copy(rsb[:], pq[il % 2].ap()[0:64, 0:512]), "dve")
                                    c.inc(v.tensor_tensor(aop[r0:r0 + 64,
                                                              qs2 * 1024 + 512:qs2 * 1024 + 1024],
                                                          pb45[1].ap()[0:64, 0:512], rsb[:], ALU.mult), "dve")

                        @block.scalar
                        def _(s, b=b, units=units, pe_pos=pe_pos, dve_mask_end=dve_mask_end):
                            for i, (qs2, kb) in enumerate(units):
                                s.wait_ge(sem["pe"], b["pe"] + pe_pos[("zb", i)])
                                if odiag[i] >= 0:
                                    s.wait_ge(sem["dve"], b["dve"] + dve_mask_end[i])
                                c.inc(s.activation(out=ptil[i % 2][:], in_=pq[i % 2].ap()[:],
                                                   func=AF.Exp, scale=0.125), "act")
                        barrier()

                    b = c.snap()
                    @block.sync
                    def _(sy, pair=pair, b=b):
                        c.inc(sy.dma_start(out=aoT_d[pair * 128:(pair + 1) * 128, :], in_=aop[:]), "st", 16)
                        c.wait(sy, "st")
                    barrier()

        def emit_l0_swiglu(xnT_sb):
            with ExitStack() as pc:
                wg2 = [usb(pc, f"s2_wg{i}", [128, ND, 256], BF16) for i in range(2)]
                wu2 = [usb(pc, f"s2_wu{i}", [128, ND, 256], BF16) for i in range(2)]
                wd2 = [usb(pc, f"s2_wd{i}", [128, 2, 512], BF16) for i in range(2)]
                gsil = usb(pc, "s2_g", [128, 512], BF16)
                upb = usb(pc, "s2_u", [128, 512], F32)
                prodT = usb(pc, "s2_p", [128, 32, 512], BF16)
                ot = [usb(pc, f"s2_o{i}", [128, 512], F32) for i in range(2)]

                peB_end_prev = [0]
                prev_evac_dve = [0]
                prevA = [None]
                wg_consumed = [0, 0]
                wu_consumed = [0, 0]
                wd_consumed = [0, 0]

                for sp in range(8):
                    bA = c.snap()
                    @block.sync
                    def _(sy, sp=sp, bA=bA):
                        if sp == 0:
                            sy.wait_ge(sem["g1"], g1_ready)
                        for fp in range(16):
                            k = fp % 2
                            gp = "gA" if k == 0 else "gB"
                            up = "gC" if k == 0 else "gD"
                            if fp < 2:
                                sy.wait_ge(sem["pe"], wg_consumed[k])
                            else:
                                sy.wait_ge(sem["pe"], bA["pe"] + 32 * (fp - 1))
                            c.inc(sy.dma_start(out=wg2[k][:],
                                               in_=wgb_d[:, fp * 256:(fp + 1) * 256].rearrange("(a p) f -> p a f", p=128)), gp, 16)
                            c.inc(sy.dma_start(out=wu2[k][:],
                                               in_=wub_d[:, fp * 256:(fp + 1) * 256].rearrange("(a p) f -> p a f", p=128)), up, 16)

                    @block.tensor
                    def _(t, sp=sp, bA=bA):
                        for fc in range(32):
                            fp = fc // 2
                            half = fc % 2
                            k = fp % 2
                            gp = "gA" if k == 0 else "gB"
                            up = "gC" if k == 0 else "gD"
                            if half == 0:
                                t.wait_ge(sem[gp], bA[gp] + 16 * (fp // 2 + 1))
                                t.wait_ge(sem[up], bA[up] + 16 * (fp // 2 + 1))
                            if fc == 0:
                                if prevA[0] is not None:
                                    t.wait_ge(sem["act"], prevA[0][0] + 32)
                                    t.wait_ge(sem["dve"], prevA[0][1] + 64)
                            else:
                                t.wait_ge(sem["act"], bA["act"] + fc)
                                t.wait_ge(sem["dve"], bA["dve"] + 2 * fc - 1)
                            for dc in range(ND):
                                c.inc(t.matmul(pb[0][:, 0:512], wg2[k][:, dc, half * 128:(half + 1) * 128],
                                               xnT_sb[:, dc, sp * 512:(sp + 1) * 512],
                                               start=(dc == 0), stop=(dc == ND - 1)), "pe")
                            for dc in range(ND):
                                c.inc(t.matmul(pb[1][:, 0:512], wu2[k][:, dc, half * 128:(half + 1) * 128],
                                               xnT_sb[:, dc, sp * 512:(sp + 1) * 512],
                                               start=(dc == 0), stop=(dc == ND - 1)), "pe")

                    wg_consumed[0] = bA["pe"] + 16 * (2 * 14 + 2)
                    wg_consumed[1] = bA["pe"] + 16 * (2 * 15 + 2)
                    wu_consumed[0] = wg_consumed[0]
                    wu_consumed[1] = wg_consumed[1]

                    @block.scalar
                    def _(s, sp=sp, bA=bA):
                        for fc in range(32):
                            if fc == 0:
                                s.wait_ge(sem["pe"], peB_end_prev[0])
                            s.wait_ge(sem["pe"], bA["pe"] + 16 * fc + 8)
                            if fc >= 1:
                                s.wait_ge(sem["dve"], bA["dve"] + 2 * fc)
                            c.inc(s.activation(out=gsil[:], in_=pb[0][:, 0:512], func=AF.Silu,
                                               bias=bg_sb[:, fc:fc + 1]), "act")

                    @block.vector
                    def _(v, sp=sp, bA=bA):
                        for fc in range(32):
                            if fc == 0:
                                v.wait_ge(sem["pe"], peB_end_prev[0])
                            v.wait_ge(sem["pe"], bA["pe"] + 16 * (fc + 1))
                            v.wait_ge(sem["act"], bA["act"] + fc + 1)
                            c.inc(v.tensor_scalar(upb[:], pb[1][:, 0:512], bu_sb[:, fc:fc + 1],
                                                  None, ALU.add), "dve")
                            c.inc(v.tensor_tensor(prodT[:, fc, :], upb[:], gsil[:], ALU.mult), "dve")
                    prevA[0] = (bA["act"], bA["dve"])

                    for dc in range(2):
                        bB = c.snap()
                        @block.sync
                        def _(sy, sp=sp, dc=dc, bB=bB):
                            for fp in range(16):
                                k = fp % 2
                                gp = "gE" if k == 0 else "gF"
                                if fp < 2:
                                    sy.wait_ge(sem["pe"], wd_consumed[k])
                                else:
                                    sy.wait_ge(sem["pe"], bB["pe"] + 8 * (fp - 1))
                                c.inc(sy.dma_start(out=wd2[k][:],
                                                   in_=wdb_d[fp * 256:(fp + 1) * 256, dc * 512:(dc + 1) * 512]
                                                       .rearrange("(a p) d -> p a d", p=128)), gp, 16)

                        @block.tensor
                        def _(t, sp=sp, dc=dc, bB=bB, bA=bA):
                            for fp in range(16):
                                k = fp % 2
                                gp = "gE" if k == 0 else "gF"
                                t.wait_ge(sem[gp], bB[gp] + 16 * (fp // 2 + 1))
                                if fp == 0:
                                    t.wait_ge(sem["dve"], prev_evac_dve[0])
                                    t.wait_ge(sem["dve"], bA["dve"] + 64)
                                for half in range(2):
                                    fc = 2 * fp + half
                                    for tt4 in range(4):
                                        c.inc(t.matmul(pb[2 + tt4][:, 0:512],
                                                       prodT[:, fc, tt4 * 128:(tt4 + 1) * 128],
                                                       wd2[k][:, half, :],
                                                       start=(fc == 0), stop=(fc == 31)), "pe")
                        wd_consumed[0] = bB["pe"] + 8 * 15
                        wd_consumed[1] = bB["pe"] + 8 * 16
                        peB_end_prev[0] = c.v["pe"]

                        @block.vector
                        def _(v, sp=sp, dc=dc, bB=bB):
                            v.wait_ge(sem["pe"], bB["pe"] + 128)
                            for tt4 in range(4):
                                stp = "gG" if tt4 % 2 == 0 else "gH"
                                v.wait_ge(sem[stp], bB[stp] + 16 * (tt4 // 2))
                                c.inc(v.tensor_tensor(ot[tt4 % 2][:], pb[2 + tt4][:, 0:512],
                                                      bdb_sb[:, dc * 4:(dc + 1) * 4, :].rearrange("p a x -> p (a x)"),
                                                      ALU.add), "dve")
                        prev_evac_dve[0] = c.v["dve"]

                        @block.gpsimd
                        def _(g, sp=sp, dc=dc, bB=bB):
                            for tt4 in range(4):
                                stp = "gG" if tt4 % 2 == 0 else "gH"
                                g.wait_ge(sem["dve"], bB["dve"] + tt4 + 1)
                                tt = sp * 4 + tt4
                                c.inc(g.dma_start(out=acc_d[tt * 128:(tt + 1) * 128, dc * 512:(dc + 1) * 512],
                                                  in_=ot[tt4 % 2][:], accum_op=ALU.add), stp, 16)
                @block.gpsimd
                def _(g):
                    c.wait(g, "gG")
                    c.wait(g, "gH")
                barrier()

        def emit_l0_peer_a(xnT_sb, qT_sb):
            with ExitStack() as pc:
                det = [usb(pc, f"pa_de{i}", [128, ND, 512], BF16) for i in range(2)]
                wqt = usb(pc, "pa_wq", [128, ND, 512], BF16)
                wst = [usb(pc, f"pa_ws{i}", [128, 512], BF16) for i in range(2)]

                b = c.snap()
                @block.gpsimd
                def _(g, b=b):
                    c.inc(g.dma_start(out=wqt[:], in_=Wqp.rearrange("(a p) f -> p a f", p=128)), "g", 16)
                @block.tensor
                def _(t, b=b):
                    t.wait_ge(sem["g"], b["g"] + 16)
                    for u in range(32):
                        fc, sp = divmod(u, 8)
                        if u >= 2:
                            t.wait_ge(sem["dve"], b["dve"] + u - 1)
                        for dc in range(ND):
                            c.inc(t.matmul(pb[u % 2][:, 0:512], wqt[:, dc, fc * 128:(fc + 1) * 128],
                                           xnT_sb[:, dc, sp * 512:(sp + 1) * 512],
                                           start=(dc == 0), stop=(dc == ND - 1)), "pe")
                @block.vector
                def _(v, b=b):
                    for u in range(32):
                        fc, sp = divmod(u, 8)
                        v.wait_ge(sem["pe"], b["pe"] + 8 * (u + 1))
                        c.inc(v.tensor_copy(qT_sb[:, fc, sp * 512:(sp + 1) * 512], pb[u % 2][:, 0:512]), "dve")
                barrier()

                for ic in range(8):
                    b = c.snap()
                    @block.sync
                    def _(sy, ic=ic, b=b):
                        c.inc(sy.dma_start(out=det[ic % 2][:],
                                           in_=deT_d[:, ic * 512:(ic + 1) * 512].rearrange("(a p) i -> p a i", p=128)),
                              "ld", 16)
                        for tt in range(NT):
                            stp = "stA" if tt % 2 == 0 else "stB"
                            sy.wait_ge(sem["dve"], b["dve"] + tt + 1)
                            c.inc(sy.dma_start(out=wall_d[tt * 128:(tt + 1) * 128, ic * 512:(ic + 1) * 512],
                                               in_=wst[tt % 2][:]), stp, 16)
                        c.wait(sy, "stA")
                        c.wait(sy, "stB")

                    @block.tensor
                    def _(t, ic=ic, b=b):
                        t.wait_ge(sem["ld"], b["ld"] + 16)
                        for tt in range(NT):
                            if tt >= 2:
                                t.wait_ge(sem["dve"], b["dve"] + tt - 1)
                            for dc in range(ND):
                                c.inc(t.matmul(pb[tt % 2][:, 0:512], xnT_sb[:, dc, tt * 128:(tt + 1) * 128],
                                               det[ic % 2][:, dc, :],
                                               start=(dc == 0), stop=(dc == ND - 1)), "pe")

                    @block.vector
                    def _(v, ic=ic, b=b):
                        for tt in range(NT):
                            stp = "stA" if tt % 2 == 0 else "stB"
                            v.wait_ge(sem["pe"], b["pe"] + 8 * (tt + 1))
                            if tt >= 2:
                                v.wait_ge(sem[stp], b[stp] + 16 * (tt // 2)), 
                            c.inc(v.tensor_copy(wst[tt % 2][:], pb[tt % 2][:, 0:512]), "dve")
                    barrier()

        def emit_l0_peer_b(qT_sb):
            with ExitStack() as pc:
                ue = usb(pc, "pb_ue", [128, 32, D], BF16)
                wallt = [usb(pc, f"pb_w{i}", [128, NEXP], BF16) for i in range(2)]
                silw = usb(pc, "pb_sw", [128, NEXP], BF16)
                sfull = usb(pc, "pb_sf", [128, NEXP], BF16)
                fsc = usb(pc, "pb_fs", [128, NEXP], BF16)
                pexp = usb(pc, "pb_pe", [128, NEXP], BF16)
                mskt = usb(pc, "pb_mk", [128, NEXP], BF16)
                sT = usb(pc, "pb_sT", [128, 32, 128], BF16)
                simb = usb(pc, "pb_sb", [128, 2, 64], BF16)
                simf = usb(pc, "pb_sfl", [128, 2, 64], F32)
                sx8 = usb(pc, "pb_sx", [128, 8], F32)
                sy8 = usb(pc, "pb_sy", [128, 8], F32)
                cnd = usb(pc, "pb_cn", [128, 64], BF16)
                cndf = usb(pc, "pb_cf", [128, 64], F32)
                c8 = usb(pc, "pb_c8", [128, 8], F32)
                e8 = usb(pc, "pb_e8", [128, 8], F32)
                cls = usb(pc, "pb_cl", [128, 8], F32)
                est = [usb(pc, f"pb_es{i}", [128, 512], F32) for i in range(2)]

                b0 = c.snap()
                @block.gpsimd
                def _(g):
                    c.inc(g.dma_start(out=ue[:], in_=uem0.rearrange("(a p) d -> p a d", p=128)), "g", 16)

                PE_T = 8 + 32 + 64
                DVE_T = 4 * 13 + 1 + 32 + 2
                ACT_T = 1 + 4 * 3

                for tt in range(NT):
                    b = c.snap()
                    ldp = "ldA" if tt % 2 == 0 else "ldB"
                    @block.sync
                    def _(sy, tt=tt, b=b, ldp=ldp):
                        if tt >= 2:
                            sy.wait_ge(sem["act"], b["act"] - (ACT_T - 1))
                        c.inc(sy.dma_start(out=wallt[tt % 2][:], in_=wall_d[tt * 128:(tt + 1) * 128, :]), ldp, 16)

                    @block.scalar
                    def _(s, tt=tt, b=b, ldp=ldp):
                        s.wait_ge(sem[ldp], b[ldp] + 16)
                        if tt >= 1:
                            s.wait_ge(sem["dve"], b["dve"] - DVE_T + 53)
                        c.inc(s.activation(out=silw[:], in_=wallt[tt % 2][:], func=AF.Silu), "act")
                        for h in range(PEER_H):
                            hb_dve = b["dve"] + 13 * h
                            hb_act = b["act"] + 1 + 3 * h
                            s.wait_ge(sem["dve"], hb_dve + 8)
                            c.inc(s.activation(out=e8[:], in_=c8[:], func=AF.Exp,
                                               bias=cls[:, 0:1], accum_out=cls[:, 1:2]), "act")
                            c.inc(s.activation(out=cls[:, 2:3], in_=cls[:, 1:2], func=AF.Ln), "act")
                            s.wait_ge(sem["dve"], hb_dve + 10)
                            c.inc(s.activation(out=pexp[:], in_=fsc[:], func=AF.Exp,
                                               bias=cls[:, 3:4]), "act")

                    @block.tensor
                    def _(t, tt=tt, b=b):
                        if tt == 0:
                            t.wait_ge(sem["g"], b0["g"] + 16)
                        for h in range(PEER_H):
                            if h >= 1:
                                t.wait_ge(sem["dve"], b["dve"] + 13 * (h - 1) + 2)
                            elif tt >= 1:
                                t.wait_ge(sem["dve"], b["dve"] - DVE_T + 13 * 3 + 2)
                            for p in range(2):
                                fc = 2 * p + h // 2
                                off = (h % 2) * 64
                                c.inc(t.matmul(pb[5][0:128, p * 64:(p + 1) * 64],
                                               qT_sb[off:off + 64, fc, tt * 128:(tt + 1) * 128],
                                               keysT_sb[off:off + 64, h * 2 + p, :],
                                               start=True, stop=True), "pe")
                        t.wait_ge(sem["dve"], b["dve"] + 53)
                        for icn in range(32):
                            if icn >= 2:
                                t.wait_ge(sem["dve"], b["dve"] + 53 + icn - 1)
                            c.inc(t.transpose(pbt[icn % 2][0:128, 0:128],
                                              fsc[:, icn * 128:(icn + 1) * 128], id_sb[:]), "pe")
                        t.wait_ge(sem["dve"], b["dve"] + 85)
                        for dc in range(2):
                            if tt >= 1 and dc == 0:
                                t.wait_ge(sem["dve"], b["dve"] + 85)
                            if tt >= 1:
                                t.wait_ge(sem["st"], b["st"] - 16)
                            for icn in range(32):
                                c.inc(t.matmul(pb[2 + dc][:, 0:512], sT[:, icn, :],
                                               ue[:, icn, dc * 512:(dc + 1) * 512],
                                               start=(icn == 0), stop=(icn == 31)), "pe")

                    @block.vector
                    def _(v, tt=tt, b=b):
                        for h in range(PEER_H):
                            hb_act = b["act"] + 1 + 3 * h
                            v.wait_ge(sem["pe"], b["pe"] + 2 * (h + 1))
                            c.inc(v.tensor_copy(simb[:], pb[5][0:128, 0:128].rearrange("p (a x) -> p a x", a=2)), "dve")
                            c.inc(v.tensor_copy(simf[:], pb[5][0:128, 0:128].rearrange("p (a x) -> p a x", a=2)), "dve")
                            c.inc(v.max(out=sx8[:], in_=simf[:, 0, :]), "dve")
                            c.inc(v.max(out=sy8[:], in_=simf[:, 1, :]), "dve")
                            c.inc(v.tensor_tensor(
                                cnd[:].rearrange("p (a x) -> p a x", a=8),
                                sx8[:].rearrange("p (a o) -> p a o", o=1).to_broadcast([128, 8, 8]),
                                sy8[:].rearrange("p (o x) -> p o x", o=1).to_broadcast([128, 8, 8]),
                                ALU.add), "dve")
                            c.inc(v.tensor_copy(cndf[:], cnd[:]), "dve")
                            c.inc(v.max(out=c8[:], in_=cndf[:]), "dve")
                            c.inc(v.tensor_scalar(cls[:, 0:1], c8[:, 0:1], -1.0, None, ALU.mult), "dve")
                            v.wait_ge(sem["act"], hb_act + 2)
                            c.inc(v.tensor_scalar(cls[:, 3:4], cls[:, 0:1], cls[:, 2:3], None,
                                                  ALU.subtract), "dve")
                            c.inc(v.tensor_tensor(
                                fsc[:].rearrange("p (a x) -> p a x", a=64),
                                simb[:, 0, :].rearrange("p (a o) -> p a o", o=1).to_broadcast([128, 64, 64]),
                                simb[:, 1, :].rearrange("p (o x) -> p o x", o=1).to_broadcast([128, 64, 64]),
                                ALU.add), "dve")
                            c.inc(v.tensor_scalar(mskt[:], fsc[:], c8[:, 7:8], None, ALU.is_ge), "dve")
                            v.wait_ge(sem["act"], hb_act + 3)
                            c.inc(v.tensor_tensor(pexp[:], pexp[:], mskt[:], ALU.mult), "dve")
                            if h == 0:
                                c.inc(v.tensor_copy(sfull[:], pexp[:]), "dve")
                            else:
                                c.inc(v.tensor_tensor(sfull[:], sfull[:], pexp[:], ALU.add), "dve")
                        v.wait_ge(sem["act"], b["act"] + 1)
                        c.inc(v.tensor_tensor(fsc[:], sfull[:], silw[:], ALU.mult), "dve")
                        for icn in range(32):
                            v.wait_ge(sem["pe"], b["pe"] + 8 + icn + 1)
                            c.inc(v.tensor_copy(sT[:, icn, :], pbt[icn % 2][0:128, 0:128]), "dve")
                        for dc in range(2):
                            gp = "gA" if dc == 0 else "gB"
                            v.wait_ge(sem["pe"], b["pe"] + 8 + 32 + 32 * (dc + 1))
                            if tt >= 1:
                                v.wait_ge(sem[gp], b[gp])
                            c.inc(v.tensor_copy(est[dc][:], pb[2 + dc][:, 0:512]), "dve")

                    @block.gpsimd
                    def _(g, tt=tt, b=b):
                        for dc in range(2):
                            gp = "gA" if dc == 0 else "gB"
                            g.wait_ge(sem["dve"], b["dve"] + 85 + dc + 1)
                            c.inc(g.dma_start(out=acc_d[tt * 128:(tt + 1) * 128, dc * 512:(dc + 1) * 512],
                                              in_=est[dc][:], accum_op=ALU.add), gp, 16)
                @block.gpsimd
                def _(g):
                    c.wait(g, "gA")
                    c.wait(g, "gB")
                barrier()

        def emit_residual(x_src, acc_src, dst_ap):
            with ExitStack() as pc:
                xt = [usb(pc, f"r_x{i}", [128, D], F32) for i in range(2)]
                at = [usb(pc, f"r_a{i}", [128, D], F32) for i in range(2)]
                so = [usb(pc, f"r_s{i}", [128, D], F32) for i in range(2)]
                b = c.snap()

                @block.sync
                def _(sy, b=b):
                    for u in range(NT):
                        xp = "ldA" if u % 2 == 0 else "ldB"
                        ap_ = "ldC" if u % 2 == 0 else "ldD"
                        if u >= 2:
                            sy.wait_ge(sem["dve"], b["dve"] + (u - 1))
                        c.inc(sy.dma_start(out=xt[u % 2][:], in_=x_src[u * 128:(u + 1) * 128, :]), xp, 16)
                        c.inc(sy.dma_start(out=at[u % 2][:], in_=acc_src[u * 128:(u + 1) * 128, :]), ap_, 16)
                        if u >= 1:
                            stp = "stA" if (u - 1) % 2 == 0 else "stB"
                            sy.wait_ge(sem["dve"], b["dve"] + u)
                            c.inc(sy.dma_start(out=dst_ap[(u - 1) * 128:u * 128, :], in_=so[(u - 1) % 2][:]), stp, 16)
                    stp = "stA" if (NT - 1) % 2 == 0 else "stB"
                    sy.wait_ge(sem["dve"], b["dve"] + NT)
                    c.inc(sy.dma_start(out=dst_ap[(NT - 1) * 128:NT * 128, :], in_=so[(NT - 1) % 2][:]), stp, 16)
                    c.wait(sy, "stA")
                    c.wait(sy, "stB")

                @block.vector
                def _(v, b=b):
                    for u in range(NT):
                        xp = "ldA" if u % 2 == 0 else "ldB"
                        ap_ = "ldC" if u % 2 == 0 else "ldD"
                        v.wait_ge(sem[xp], b[xp] + 16 * (u // 2 + 1))
                        v.wait_ge(sem[ap_], b[ap_] + 16 * (u // 2 + 1))
                        if u >= 2:
                            stp = "stA" if u % 2 == 0 else "stB"
                            v.wait_ge(sem[stp], b[stp] + 16 * (u // 2))
                        c.inc(v.tensor_tensor(so[u % 2][:], xt[u % 2][:], at[u % 2][:], ALU.add), "dve")
                barrier()

        def emit_l1_attn_a(xnT_sb):
            with ExitStack() as pc:
                wqkv = usb(pc, "l1_wqkv", [128, ND, 3 * D], BF16)
                qkvb = [usb(pc, f"l1_qkv{i}", [128, 3 * D], BF16) for i in range(2)]
                ropc = usb(pc, "l1_rc", [128, 2 * D], BF16)
                ropt = usb(pc, "l1_rt", [128, 2 * D], BF16)
                roped = usb(pc, "l1_rp", [128, 2 * D], BF16)
                tmin = usb(pc, "l1_tm", [128, 2 * D], BF16)
                texp = usb(pc, "l1_te", [128, 2 * D], BF16)
                feat = [usb(pc, f"l1_f{i}", [128, 2 * D], BF16) for i in range(2)]
                onec = usb(pc, "l1_on", [128, 1], BF16)
                qfs = [usb(pc, f"l1_qt{i}", [128, ND, 128], BF16) for i in range(2)]

                b0 = c.snap()
                @block.gpsimd
                def _(g):
                    c.inc(g.dma_start(out=wqkv[:], in_=Wqkv1.rearrange("(a p) f -> p a f", p=128)), "g", 16)
                @block.vector
                def _(v):
                    c.inc(v.memset(onec[:], 1.0), "dve")
                barrier()

                DV = 23
                PE_T2 = 48 + 32 + 8
                b = c.snap()

                def kv_bank(h):
                    return 3 + h // 7, (h % 7) * 65

                @block.tensor
                def _(t, b=b):
                    for tt in range(NT):
                        pbase = b["pe"] + PE_T2 * tt
                        dbase = b["dve"] + DV * tt
                        for fcq in range(6):
                            if tt > 0 or fcq >= 2:
                                if fcq >= 2:
                                    t.wait_ge(sem["dve"], dbase + fcq - 1)
                                else:
                                    t.wait_ge(sem["dve"], dbase - DV + fcq + 5)
                            for dc in range(ND):
                                c.inc(t.matmul(pb[fcq % 2][:, 0:512], xnT_sb[:, dc, tt * 128:(tt + 1) * 128],
                                               wqkv[:, dc, fcq * 512:(fcq + 1) * 512],
                                               start=(dc == 0), stop=(dc == ND - 1)), "pe")
                        t.wait_ge(sem["dve"], dbase + 15)
                        for h in range(16):
                            bk, col = kv_bank(h)
                            first_h = 0 if bk == 3 else (7 if bk == 4 else 14)
                            last_h = 6 if bk == 3 else (13 if bk == 4 else 15)
                            c.inc(t.matmul(pb[bk][0:64, col:col + 64],
                                           feat[tt % 2][:, D + h * 64:D + h * 64 + 64],
                                           qkvb[tt % 2][:, 2 * D + h * 64:2 * D + h * 64 + 64],
                                           start=(tt == 0 and h == first_h), stop=False), "pe")
                            c.inc(t.matmul(pb[bk][0:64, col + 64:col + 65],
                                           feat[tt % 2][:, D + h * 64:D + h * 64 + 64],
                                           onec[:], start=False,
                                           stop=(tt == NT - 1 and h == last_h)), "pe")
                        for dc in range(ND):
                            if dc >= 2:
                                t.wait_ge(sem["dve"], dbase + 15 + dc - 1)
                            c.inc(t.transpose(pbt[dc % 2][0:128, 0:128],
                                              feat[tt % 2][:, dc * 128:(dc + 1) * 128], id_sb[:]), "pe")

                @block.vector
                def _(v, b=b):
                    for tt in range(NT):
                        pbase = b["pe"] + PE_T2 * tt
                        dbase = b["dve"] + DV * tt
                        abase = b["act"] + 2 * tt
                        for fcq in range(6):
                            v.wait_ge(sem["pe"], pbase + 8 * (fcq + 1))
                            c.inc(v.tensor_copy(qkvb[tt % 2][:, fcq * 512:(fcq + 1) * 512],
                                                pb[fcq % 2][:, 0:512]), "dve")
                        qk = qkvb[tt % 2][:, 0:2 * D]
                        qk4 = qk.rearrange("p (h x n) -> p h x n", h=32, x=2)
                        cosv = cos_sb[:, tt, :].rearrange("p (a x n) -> p a x n", a=1, x=1).to_broadcast([128, 32, 2, 32])
                        sinv = sin_sb[:, tt, :].rearrange("p (a n) -> p a n", a=1).to_broadcast([128, 32, 32])
                        msinv = msin_sb[:, tt, :].rearrange("p (a n) -> p a n", a=1).to_broadcast([128, 32, 32])
                        rc4 = ropc[:].rearrange("p (h x n) -> p h x n", h=32, x=2)
                        rt4 = ropt[:].rearrange("p (h x n) -> p h x n", h=32, x=2)
                        c.inc(v.tensor_tensor(rc4, qk4, cosv, ALU.mult), "dve")
                        c.inc(v.tensor_tensor(rt4[:, :, 0, :], qk4[:, :, 1, :], msinv, ALU.mult), "dve")
                        c.inc(v.tensor_tensor(rt4[:, :, 1, :], qk4[:, :, 0, :], sinv, ALU.mult), "dve")
                        c.inc(v.tensor_tensor(roped[:], ropc[:], ropt[:], ALU.add), "dve")
                        c.inc(v.tensor_scalar(tmin[:, 0:D], roped[:, 0:D], 0.125, 0.0, ALU.mult, ALU.min), "dve")
                        c.inc(v.tensor_scalar(tmin[:, D:2 * D], roped[:, D:2 * D], 1.0, 0.0, ALU.mult, ALU.min), "dve")
                        if tt >= 2:
                            v.wait_ge(sem["pe"], b["pe"] + PE_T2 * (tt - 1))
                        c.inc(v.tensor_scalar(feat[tt % 2][:, 0:D], roped[:, 0:D], 0.125, 0.0,
                                              ALU.mult, ALU.max), "dve")
                        c.inc(v.tensor_scalar(feat[tt % 2][:, D:2 * D], roped[:, D:2 * D], 1.0, 0.0,
                                              ALU.mult, ALU.max), "dve")
                        v.wait_ge(sem["act"], abase + 2)
                        c.inc(v.tensor_tensor(feat[tt % 2][:], feat[tt % 2][:], texp[:], ALU.add), "dve")
                        for dc in range(ND):
                            v.wait_ge(sem["pe"], pbase + 48 + 32 + dc + 1)
                            if tt >= 2 and dc == 0:
                                stp = "stA" if tt % 2 == 0 else "stB"
                                v.wait_ge(sem[stp], b[stp] + 16 * (tt // 2))
                            c.inc(v.tensor_copy(qfs[tt % 2][:, dc, :], pbt[dc % 2][0:128, 0:128]), "dve")

                @block.scalar
                def _(s, b=b):
                    for tt in range(NT):
                        dbase = b["dve"] + DV * tt
                        s.wait_ge(sem["dve"], dbase + 12)
                        c.inc(s.activation(out=texp[:, 0:D], in_=tmin[:, 0:D], func=AF.Exp), "act")
                        c.inc(s.activation(out=texp[:, D:2 * D], in_=tmin[:, D:2 * D], func=AF.Exp), "act")

                @block.sync
                def _(sy, b=b):
                    for tt in range(NT):
                        stp = "stA" if tt % 2 == 0 else "stB"
                        sy.wait_ge(sem["dve"], b["dve"] + DV * (tt + 1))
                        c.inc(sy.dma_start(out=qfT_d[:, tt * 128:(tt + 1) * 128].rearrange("(a p) t -> p a t", p=128),
                                           in_=qfs[tt % 2][:]), stp, 16)
                    c.wait(sy, "stA")
                    c.wait(sy, "stB")
                barrier()

        def emit_l1_attn_b():
            with ExitStack() as pc:
                kvb = usb(pc, "l1b_kv", [128, 8, 65], BF16)
                qfp = [usb(pc, f"l1b_q{i}", [128, T], BF16) for i in range(2)]
                dmx = usb(pc, "l1b_dm", [1, 512], F32)
                rr = usb(pc, "l1b_rr", [1, 512], F32)
                rrb = usb(pc, "l1b_rb", [1, 512], BF16)
                rsb = usb(pc, "l1b_rs", [64, 512], F32)
                aop = usb(pc, "l1b_ao", [128, T], BF16)

                b = c.snap()
                @block.vector
                def _(v, b=b):
                    kvf = kvb[:].rearrange("p a x -> p (a x)")
                    for h in range(16):
                        bk, col = (3 + h // 7, (h % 7) * 65)
                        dst_col = (h // 2) * 65
                        r0 = (h % 2) * 64
                        c.inc(v.tensor_copy(kvf[r0:r0 + 64, dst_col:dst_col + 65],
                                            pb[bk][0:64, col:col + 65]), "dve")
                barrier()

                for pair in range(8):
                    b = c.snap()
                    @block.sync
                    def _(sy, pair=pair, b=b):
                        if pair >= 2:
                            sy.wait_ge(sem["pe"], b["pe"] - 32)
                        c.inc(sy.dma_start(out=qfp[pair % 2][:], in_=qfT_d[pair * 128:(pair + 1) * 128, :]), "ld", 16)

                    @block.tensor
                    def _(t, pair=pair, b=b):
                        t.wait_ge(sem["ld"], b["ld"] + 16)
                        rowA = kvb[0:64, pair, :]
                        rowB = kvb[64:128, pair, :]
                        for tc in range(8):
                            for hh in range(2):
                                u = tc * 2 + hh
                                row = rowA if hh == 0 else rowB
                                qap = qfp[pair % 2][(hh * 64):(hh * 64) + 64, tc * 512:(tc + 1) * 512]
                                if u >= 2 or pair >= 1:
                                    t.wait_ge(sem["dve"], b["dve"] + 5 * (u - 2) + 5)
                                c.inc(t.matmul(pb[hh][0:65, 0:512], row, qap, start=True, stop=True), "pe")
                                t.wait_ge(sem["dve"], b["dve"] + 5 * u + 3)
                                c.inc(t.matmul(pb[2][0:64, 0:512], onesr_sb[0:1, 0:64], rrb[:],
                                               start=True, stop=True), "pe")

                    @block.vector
                    def _(v, pair=pair, b=b):
                        for tc in range(8):
                            for hh in range(2):
                                u = tc * 2 + hh
                                if u == 0 and pair >= 1:
                                    v.wait_ge(sem["st"], b["st"])
                                v.wait_ge(sem["pe"], b["pe"] + 2 * u + 1)
                                c.inc(v.tensor_scalar(dmx[:], pb[hh][64:65, 0:512], eps_sb[0:1, 1:2], None, ALU.max), "dve")
                                c.inc(v.reciprocal(rr[:], dmx[:]), "dve")
                                c.inc(v.tensor_copy(rrb[:], rr[:]), "dve")
                                v.wait_ge(sem["pe"], b["pe"] + 2 * u + 2)
                                c.inc(v.tensor_copy(rsb[:], pb[2][0:64, 0:512]), "dve")
                                c.inc(v.tensor_tensor(aop[hh * 64:hh * 64 + 64, tc * 512:(tc + 1) * 512],
                                                      pb[hh][0:64, 0:512], rsb[:], ALU.mult), "dve")

                    @block.sync
                    def _(sy, pair=pair, b=b):
                        sy.wait_ge(sem["dve"], b["dve"] + 80)
                        c.inc(sy.dma_start(out=aoT_d[pair * 128:(pair + 1) * 128, :], in_=aop[:]), "st", 16)
                barrier()

        def emit_l1_router(xnT_sb, g_sb, s_sb):
            with ExitStack() as pc:
                lg = usb(pc, "rt_lg", [128, 8], F32)
                t2 = usb(pc, "rt_t2", [128, 8], F32)
                w01 = usb(pc, "rt_w", [128, 4], F32)
                eqs = usb(pc, "rt_e", [128, 2], F32)
                b = c.snap()
                DVR = 4 + 7 * E1

                @block.tensor
                def _(t, b=b):
                    for tt in range(NT):
                        if tt >= 1:
                            t.wait_ge(sem["dve"], b["dve"] + DVR * (tt - 1) + 1)
                        for dc in range(ND):
                            c.inc(t.matmul(pb[5][0:128, 0:8], xnT_sb[:, dc, tt * 128:(tt + 1) * 128],
                                           wgT_sb[:, dc, :], start=(dc == 0), stop=(dc == ND - 1)), "pe")

                @block.vector
                def _(v, b=b):
                    for tt in range(NT):
                        v.wait_ge(sem["pe"], b["pe"] + 8 * (tt + 1))
                        c.inc(v.tensor_copy(lg[:], pb[5][0:128, 0:8]), "dve")
                        c.inc(v.max(out=t2[:], in_=lg[:]), "dve")
                        c.inc(v.tensor_scalar(w01[:, 0:1], t2[:, 0:1], t2[:, 1:2], None, ALU.subtract), "dve")
                        v.wait_ge(sem["act"], b["act"] + tt + 1)
                        c.inc(v.tensor_scalar(w01[:, 2:3], w01[:, 1:2], -1.0, 1.0, ALU.mult, ALU.add), "dve")
                        for e in range(E1):
                            c.inc(v.tensor_scalar(eqs[:, 0:1], lg[:, e:e + 1], t2[:, 0:1], None, ALU.is_equal), "dve")
                            c.inc(v.tensor_scalar(eqs[:, 1:2], lg[:, e:e + 1], t2[:, 1:2], None, ALU.is_equal), "dve")
                            c.inc(v.tensor_scalar(s_sb[:, tt, e:e + 1], eqs[:, 0:1], -1.0, 1.0, ALU.mult, ALU.add), "dve")
                            c.inc(v.tensor_tensor(eqs[:, 1:2], eqs[:, 1:2], s_sb[:, tt, e:e + 1], ALU.mult), "dve")
                            c.inc(v.tensor_tensor(eqs[:, 0:1], eqs[:, 0:1], w01[:, 1:2], ALU.mult), "dve")
                            c.inc(v.tensor_tensor(eqs[:, 1:2], eqs[:, 1:2], w01[:, 2:3], ALU.mult), "dve")
                            c.inc(v.tensor_tensor(g_sb[:, tt, e:e + 1], eqs[:, 0:1], eqs[:, 1:2], ALU.add), "dve")

                @block.scalar
                def _(s, b=b):
                    for tt in range(NT):
                        s.wait_ge(sem["dve"], b["dve"] + DVR * tt + 3)
                        c.inc(s.activation(out=w01[:, 1:2], in_=w01[:, 0:1], func=AF.Sigmoid), "act")
                barrier()

        def emit_l1_moe(xnT_sb, g_sb):
            with ExitStack() as pc:
                w1s = usb(pc, "moe_w1", [128, ND, F1], BF16)
                w2t = [usb(pc, f"moe_w2{i}", [128, 2, 512], BF16) for i in range(4)]
                actT = usb(pc, "moe_a", [128, 32, 512], BF16)
                ot = [usb(pc, f"moe_o{i}", [128, 512], F32) for i in range(2)]

                LDP = ["ldA", "ldB", "ldC", "ldD"]
                w2_consumed_pe = [0, 0, 0, 0]
                peB_end_prev = [0]
                prev_evac_dve = [0]
                prevA = [None]
                g_after_w1 = [0]

                @block.gpsimd
                def _(g):
                    c.inc(g.dma_start(out=w1s[:], in_=W11[0].rearrange("(a p) f -> p a f", p=128)), "g", 16)
                g_after_w1[0] = c.v["g"]

                for e in range(E1):
                    for sp in range(8):
                        bA = c.snap()
                        @block.tensor
                        def _(t, e=e, sp=sp, bA=bA):
                            if sp == 0:
                                t.wait_ge(sem["g"], g_after_w1[0])
                            for fc in range(32):
                                if fc < 2:
                                    if prevA[0] is not None:
                                        t.wait_ge(sem["act"], prevA[0][0] + 32)
                                else:
                                    t.wait_ge(sem["act"], bA["act"] + fc - 1)
                                for dc in range(ND):
                                    c.inc(t.matmul(pb[fc % 2][:, 0:512], w1s[:, dc, fc * 128:(fc + 1) * 128],
                                                   xnT_sb[:, dc, sp * 512:(sp + 1) * 512],
                                                   start=(dc == 0), stop=(dc == ND - 1)), "pe")

                        @block.scalar
                        def _(s, e=e, sp=sp, bA=bA):
                            for fc in range(32):
                                if fc == 0:
                                    s.wait_ge(sem["pe"], peB_end_prev[0])
                                s.wait_ge(sem["pe"], bA["pe"] + 8 * (fc + 1))
                                c.inc(s.activation(out=actT[:, fc, :], in_=pb[fc % 2][:, 0:512],
                                                   func=AF.Silu), "act")
                        prevA[0] = (bA["act"], bA["dve"])

                        if sp == 7 and e < E1 - 1:
                            gate_pe = c.v["pe"]
                            @block.gpsimd
                            def _(g, e=e, gate_pe=gate_pe):
                                g.wait_ge(sem["pe"], gate_pe)
                                c.inc(g.dma_start(out=w1s[:],
                                                  in_=W11[e + 1].rearrange("(a p) f -> p a f", p=128)), "g", 16)
                            g_after_w1[0] = c.v["g"]

                        for dc in range(2):
                            bB = c.snap()
                            @block.sync
                            def _(sy, e=e, sp=sp, dc=dc, bB=bB):
                                for fp in range(16):
                                    k = fp % 4
                                    if fp < 4:
                                        sy.wait_ge(sem["pe"], w2_consumed_pe[k])
                                    else:
                                        sy.wait_ge(sem["pe"], bB["pe"] + 8 * (fp - 3))
                                    c.inc(sy.dma_start(
                                        out=w2t[k][:],
                                        in_=w2b_d[e][fp * 256:(fp + 1) * 256, dc * 512:(dc + 1) * 512]
                                            .rearrange("(a p) d -> p a d", p=128)), LDP[k], 16)

                            @block.tensor
                            def _(t, e=e, sp=sp, dc=dc, bB=bB, bA=bA):
                                for fp in range(16):
                                    k = fp % 4
                                    t.wait_ge(sem[LDP[k]], bB[LDP[k]] + 16 * (fp // 4 + 1))
                                    if fp == 0:
                                        t.wait_ge(sem["dve"], prev_evac_dve[0])
                                        t.wait_ge(sem["act"], bA["act"] + 32)
                                    for half in range(2):
                                        fc = 2 * fp + half
                                        for tt4 in range(4):
                                            c.inc(t.matmul(pb[2 + tt4][:, 0:512],
                                                           actT[:, fc, tt4 * 128:(tt4 + 1) * 128],
                                                           w2t[k][:, half, :],
                                                           start=(fc == 0), stop=(fc == 31)), "pe")
                                    w2_consumed_pe[k] = c.v["pe"]
                            peB_end_prev[0] = c.v["pe"]

                            @block.vector
                            def _(v, e=e, sp=sp, dc=dc, bB=bB):
                                v.wait_ge(sem["pe"], bB["pe"] + 128)
                                for tt4 in range(4):
                                    gp = "gA" if tt4 % 2 == 0 else "gB"
                                    v.wait_ge(sem[gp], bB[gp] + 16 * (tt4 // 2))
                                    tt = sp * 4 + tt4
                                    c.inc(v.tensor_scalar(ot[tt4 % 2][:], pb[2 + tt4][:, 0:512],
                                                          g_sb[:, tt, e:e + 1], None, ALU.mult), "dve")
                            prev_evac_dve[0] = c.v["dve"]

                            @block.gpsimd
                            def _(g, e=e, sp=sp, dc=dc, bB=bB):
                                for tt4 in range(4):
                                    gp = "gA" if tt4 % 2 == 0 else "gB"
                                    g.wait_ge(sem["dve"], bB["dve"] + tt4 + 1)
                                    tt = sp * 4 + tt4
                                    c.inc(g.dma_start(out=acc_d[tt * 128:(tt + 1) * 128,
                                                                dc * 512:(dc + 1) * 512],
                                                      in_=ot[tt4 % 2][:], accum_op=ALU.add), gp, 16)
                @block.gpsimd
                def _(g):
                    c.wait(g, "gA")
                    c.wait(g, "gB")
                barrier()

        def emit_l1_moe_sparse(g_sb, s_sb):
            from concourse.bass_core_rust import IndirectOffsetOnAxis
            I32 = mybir.dt.int32

            with ExitStack() as ps_:
                gT = usb(ps_, "ms_gT", [8, T], F32)
                rA = usb(ps_, "ms_rA", [8, T], F32)
                rB = usb(ps_, "ms_rB", [8, T], F32)
                zst = usb(ps_, "ms_z", [128, 4096], F32)
                mst = usb(ps_, "ms_m", [128, CAP // 128 * 8], F32)
                b = c.snap()

                @block.tensor
                def _(t, b=b):
                    for tt in range(NT):
                        if tt >= 1:
                            t.wait_ge(sem["dve"], b["dve"] + tt)
                        c.inc(t.transpose(pb45[0].ap()[0:8, 0:128], g_sb[:, tt, :],
                                          idf_sb[:]), "pe")

                @block.vector
                def _(v, b=b):
                    for tt in range(NT):
                        v.wait_ge(sem["pe"], b["pe"] + tt + 1)
                        c.inc(v.tensor_copy(gT[0:8, tt * 128:(tt + 1) * 128],
                                            pb45[0].ap()[0:8, 0:128]), "dve")
                    c.inc(v.tensor_scalar(rA[:], gT[:], 0.0, None, ALU.is_gt), "dve")
                    c.inc(v.tensor_tensor_scan(rB[:], rA[:], rA[:], 0.0, ALU.add, ALU.bypass), "dve")
                    c.inc(v.tensor_tensor(rB[:], rB[:], rA[:], ALU.subtract), "dve")
                    c.inc(v.tensor_tensor(rB[:], rB[:], rA[:], ALU.mult), "dve")
                    c.inc(v.tensor_scalar(rA[:], rA[:], -4000000.0, 4000000.0, ALU.mult, ALU.add), "dve")
                    c.inc(v.tensor_tensor(rB[:], rB[:], rA[:], ALU.add), "dve")
                    c.inc(v.memset(zst[:], 0.0), "dve")
                    c.inc(v.memset(mst[:], 0.0), "dve")
                    c.inc(v.memset(mst[:].rearrange("p (a x) -> p a x", x=8)[:, :, 1:2], 5000000.0), "dve")

                @block.sync
                def _(sy, b=b):
                    c.wait(sy, "dve")
                    c.inc(sy.dma_start(out=pos_d, in_=rB[:]), "ld", 16)
                    for k in range(16):
                        c.inc(sy.dma_start(
                            out=ysc_d[k * 512:(k + 1) * 512, :].rearrange("(a p) x -> p a x", p=128),
                            in_=zst[:].rearrange("p (a x) -> p a x", a=4)), "st", 16)
                    for e in range(E1):
                        c.inc(sy.dma_start(
                            out=cmeta_d[e * CAP:(e + 1) * CAP, :].rearrange("(a p) x -> p a x", p=128),
                            in_=mst[:].rearrange("p (a x) -> p a x", x=8)), "st", 16)
                    c.wait(sy, "st")
                    c.wait(sy, "ld")
                barrier()

            with ExitStack() as pc:
                w1s = usb(pc, "ms_w1", [128, ND, F1], BF16)
                w2t = [usb(pc, f"ms_w2{i}", [128, 2, 512], BF16) for i in range(4)]
                cxT = [usb(pc, f"ms_cxT{i}", [128, ND, CAP], BF16) for i in range(2)]
                actT = usb(pc, "ms_a", [128, 32, 512], BF16)
                ot = [usb(pc, f"ms_o{i}", [128, 512], F32) for i in range(2)]
                xnb = [usb(pc, f"ms_x{i}", [128, D], BF16) for i in range(2)]
                posf = [usb(pc, f"ms_p{i}", [128, 1], F32) for i in range(2)]
                posi = [usb(pc, f"ms_pi{i}", [128, 1], I32) for i in range(2)]
                meta = [usb(pc, f"ms_mt{i}", [128, 8], F32) for i in range(2)]
                cme = [usb(pc, f"ms_cm{i}", [128, CAP // 128, 8], F32) for i in range(2)]
                tids = [usb(pc, f"ms_ti{i}", [128, CAP // 128], I32) for i in range(2)]

                LDP = ["ldA", "ldB", "ldC", "ldD"]
                w2_consumed_pe = [0, 0, 0, 0]
                prev_evac_dve = [0]
                prevA = [None]
                g_after_w1 = [0]
                cxT_free_pe = [0, 0]
                scat_done = [None] * E1
                ysc_done = [None] * E1
                NSP = CAP // 512

                @block.vector
                def _(v):
                    c.inc(v.memset(meta[0][:], 0.0), "dve")
                    c.inc(v.memset(meta[1][:], 0.0), "dve")

                bc_regs = []
                @block.gpsimd
                def _(g):
                    bc_regs.append(g.snap(CAP - 1))
                    bc_regs.append(g.snap(2 * T - 1))
                    c.inc(g.dma_start(out=w1s[:], in_=W11[0].rearrange("(a p) f -> p a f", p=128)), "g", 16)
                g_after_w1[0] = c.v["g"]

                for e in range(E1):
                    bS = c.snap()
                    @block.sync
                    def _(sy, e=e, bS=bS):
                        for tt in range(NT):
                            xp = "ldE" if tt % 2 == 0 else "ldF"
                            pp = "stC" if tt % 2 == 0 else "stD"
                            if tt >= 2:
                                gx = "gI" if tt % 2 == 0 else "gJ"
                                sy.wait_ge(sem[gx], bS[gx] + 16 * (tt // 2))
                                sy.wait_ge(sem["dve"], bS["dve"] + 4 * (tt - 1))
                            c.inc(sy.dma_start(out=xnb[tt % 2][:],
                                               in_=xn_d[tt * 128:(tt + 1) * 128, :]), xp, 16)
                            c.inc(sy.dma_start(
                                out=posf[tt % 2][:],
                                in_=pos_d[e:e + 1, tt * 128:(tt + 1) * 128]
                                    .rearrange("o (p x) -> (o p) x", x=1)), pp, 16)

                    @block.vector
                    def _(v, e=e, bS=bS):
                        for tt in range(NT):
                            pp = "stC" if tt % 2 == 0 else "stD"
                            gm = "gK" if tt % 2 == 0 else "gL"
                            v.wait_ge(sem[pp], bS[pp] + 16 * (tt // 2 + 1))
                            if tt >= 2:
                                gx2 = "gI" if tt % 2 == 0 else "gJ"
                                v.wait_ge(sem[gm], bS[gm] + 16 * (tt // 2))
                                v.wait_ge(sem[gx2], bS[gx2] + 16 * (tt // 2))
                            c.inc(v.tensor_copy(posi[tt % 2][:], posf[tt % 2][:]), "dve")
                            c.inc(v.tensor_copy(meta[tt % 2][:, 0:1], g_sb[:, tt, e:e + 1]), "dve")
                            c.inc(v.tensor_scalar(meta[tt % 2][:, 1:2], s_sb[:, tt, e:e + 1],
                                                  4096.0, None, ALU.mult), "dve")
                            c.inc(v.tensor_tensor(meta[tt % 2][:, 1:2], meta[tt % 2][:, 1:2],
                                                  tid_sb[:, tt:tt + 1], ALU.add), "dve")

                    @block.gpsimd
                    def _(g, e=e, bS=bS):
                        if e >= 1:
                            for n in ("gI", "gJ", "gK", "gL"):
                                g.wait_ge(sem[n], scat_done[e - 1][n])
                            for n in ("gM", "gN"):
                                g.wait_ge(sem[n], ysc_done[e - 1][n])
                            g.dma_reset()
                        for tt in range(NT):
                            xp = "ldE" if tt % 2 == 0 else "ldF"
                            gx = "gI" if tt % 2 == 0 else "gJ"
                            gm = "gK" if tt % 2 == 0 else "gL"
                            if tt >= 4:
                                g.wait_ge(sem[gx], bS[gx] + 16 * ((tt - 4) // 2 + 1))
                                g.wait_ge(sem[gm], bS[gm] + 16 * ((tt - 4) // 2 + 1))
                            g.wait_ge(sem[xp], bS[xp] + 16 * (tt // 2 + 1))
                            g.wait_ge(sem["dve"], bS["dve"] + 4 * (tt + 1))
                            c.inc(g.indirect_dma_start(
                                out=cx_d, out_offset=IndirectOffsetOnAxis(posi[tt % 2][:], 0),
                                in_=xnb[tt % 2][:], in_offset=None, element_offset=e * CAP * D,
                                bounds_check=bc_regs[0], oob_is_err=False), gx, 16)
                            c.inc(g.indirect_dma_start(
                                out=cmeta_d, out_offset=IndirectOffsetOnAxis(posi[tt % 2][:], 0),
                                in_=meta[tt % 2][:], in_offset=None, element_offset=e * CAP * 8,
                                bounds_check=bc_regs[0], oob_is_err=False), gm, 16)
                    scat_done[e] = {n: c.v[n] for n in ("gI", "gJ", "gK", "gL")}

                    bT = c.snap()
                    @block.sync
                    def _(sy, e=e, bT=bT):
                        for n in ("gI", "gJ"):
                            sy.wait_ge(sem[n], scat_done[e][n])
                        sy.wait_ge(sem["pe"], cxT_free_pe[e % 2])
                        for ch in range(ND):
                            c.inc(sy.dma_start(out=cxT[e % 2][:, ch, :],
                                               in_=cx_d[e * CAP:(e + 1) * CAP, ch * 128:(ch + 1) * 128],
                                               transpose=True), "ld", 16)
                        for n in ("gK", "gL"):
                            sy.wait_ge(sem[n], scat_done[e][n])
                        c.inc(sy.dma_start(out=cme[e % 2][:],
                                           in_=cmeta_d[e * CAP:(e + 1) * CAP, :].rearrange("(a p) x -> p a x", p=128)), "ld", 16)
                    ld_after_T = c.v["ld"]

                    @block.vector
                    def _(v, e=e, bT=bT, ld_after_T=ld_after_T):
                        v.wait_ge(sem["ld"], ld_after_T)
                        for j in range(CAP // 128):
                            c.inc(v.tensor_copy(tids[e % 2][:, j:j + 1], cme[e % 2][:, j, 1:2]), "dve")
                    tids_dve = c.v["dve"]

                    for sp in range(NSP):
                        bA = c.snap()
                        @block.tensor
                        def _(t, e=e, sp=sp, bA=bA, ld_after_T=ld_after_T):
                            if sp == 0:
                                t.wait_ge(sem["g"], g_after_w1[0])
                                t.wait_ge(sem["ld"], ld_after_T)
                            for fc in range(32):
                                if fc < 2:
                                    if prevA[0] is not None:
                                        t.wait_ge(sem["act"], prevA[0][0] + 32)
                                else:
                                    t.wait_ge(sem["act"], bA["act"] + fc - 1)
                                for dc in range(ND):
                                    c.inc(t.matmul(pb[fc % 2][:, 0:512], w1s[:, dc, fc * 128:(fc + 1) * 128],
                                                   cxT[e % 2][:, dc, sp * 512:(sp + 1) * 512],
                                                   start=(dc == 0), stop=(dc == ND - 1)), "pe")

                        @block.scalar
                        def _(s, e=e, sp=sp, bA=bA):
                            for fc in range(32):
                                if fc == 0:
                                    s.wait_ge(sem["pe"], prev_evac_dve[0] * 0 + peB_end_prev2[0])
                                s.wait_ge(sem["pe"], bA["pe"] + 8 * (fc + 1))
                                c.inc(s.activation(out=actT[:, fc, :], in_=pb[fc % 2][:, 0:512],
                                                   func=AF.Silu), "act")
                        prevA[0] = (bA["act"], bA["dve"])

                        if sp == NSP - 1:
                            cxT_free_pe[e % 2] = c.v["pe"]
                            if e < E1 - 1:
                                gate_pe = c.v["pe"]
                                @block.gpsimd
                                def _(g, e=e, gate_pe=gate_pe):
                                    g.wait_ge(sem["pe"], gate_pe)
                                    c.inc(g.dma_start(out=w1s[:],
                                                      in_=W11[e + 1].rearrange("(a p) f -> p a f", p=128)), "g", 16)
                                g_after_w1[0] = c.v["g"]

                        for dc in range(2):
                            bB = c.snap()
                            @block.sync
                            def _(sy, e=e, sp=sp, dc=dc, bB=bB):
                                for fp in range(16):
                                    k = fp % 4
                                    if fp < 4:
                                        sy.wait_ge(sem["pe"], w2_consumed_pe[k])
                                    else:
                                        sy.wait_ge(sem["pe"], bB["pe"] + 8 * (fp - 3))
                                    c.inc(sy.dma_start(
                                        out=w2t[k][:],
                                        in_=w2b_d[e][fp * 256:(fp + 1) * 256, dc * 512:(dc + 1) * 512]
                                            .rearrange("(a p) d -> p a d", p=128)), LDP[k], 16)

                            @block.tensor
                            def _(t, e=e, sp=sp, dc=dc, bB=bB, bA=bA):
                                for fp in range(16):
                                    k = fp % 4
                                    t.wait_ge(sem[LDP[k]], bB[LDP[k]] + 16 * (fp // 4 + 1))
                                    if fp == 0:
                                        t.wait_ge(sem["dve"], prev_evac_dve[0])
                                        t.wait_ge(sem["act"], bA["act"] + 32)
                                    for half in range(2):
                                        fc = 2 * fp + half
                                        for tt4 in range(4):
                                            c.inc(t.matmul(pb[2 + tt4][:, 0:512],
                                                           actT[:, fc, tt4 * 128:(tt4 + 1) * 128],
                                                           w2t[k][:, half, :],
                                                           start=(fc == 0), stop=(fc == 31)), "pe")
                                    w2_consumed_pe[k] = c.v["pe"]
                            peB_end_prev2[0] = c.v["pe"]

                            @block.vector
                            def _(v, e=e, sp=sp, dc=dc, bB=bB, tids_dve=tids_dve):
                                v.wait_ge(sem["pe"], bB["pe"] + 128)
                                v.wait_ge(sem["dve"], tids_dve)
                                for tt4 in range(4):
                                    gp = "gM" if tt4 % 2 == 0 else "gN"
                                    v.wait_ge(sem[gp], bB[gp] + 16 * (tt4 // 2))
                                    j = sp * 4 + tt4
                                    c.inc(v.tensor_scalar(ot[tt4 % 2][:], pb[2 + tt4][:, 0:512],
                                                          cme[e % 2][:, j, 0:1], None, ALU.mult), "dve")
                            prev_evac_dve[0] = c.v["dve"]

                            @block.gpsimd
                            def _(g, e=e, sp=sp, dc=dc, bB=bB):
                                g.wait_ge(sem["gM"], bB["gM"])
                                g.wait_ge(sem["gN"], bB["gN"])
                                for tt4 in range(4):
                                    gp = "gM" if tt4 % 2 == 0 else "gN"
                                    g.wait_ge(sem["dve"], bB["dve"] + (CAP // 128) * 0 + tt4 + 1 +
                                              (c.v["dve"] - bB["dve"]) * 0)
                                    j = sp * 4 + tt4
                                    c.inc(g.indirect_dma_start(
                                        out=ysc_d,
                                        out_offset=IndirectOffsetOnAxis(tids[e % 2][:, j:j + 1], 0),
                                        in_=ot[tt4 % 2][:], in_offset=None,
                                        element_offset=dc * 512,
                                        bounds_check=bc_regs[1], oob_is_err=False), gp, 16)
                    ysc_done[e] = {n: c.v[n] for n in ("gM", "gN")}

                @block.gpsimd
                def _(g):
                    c.wait(g, "gM")
                    c.wait(g, "gN")
                barrier()

        peB_end_prev2 = [0]

        def emit_residual4(x_src, acc_src, ysc_src, dst_ap):
            with ExitStack() as pc:
                xt = [usb(pc, f"r4_x{i}", [128, D], F32) for i in range(2)]
                at = [usb(pc, f"r4_a{i}", [128, D], F32) for i in range(2)]
                y0t = [usb(pc, f"r4_y0{i}", [128, D], F32) for i in range(2)]
                y1t = [usb(pc, f"r4_y1{i}", [128, D], F32) for i in range(2)]
                so = [usb(pc, f"r4_s{i}", [128, D], F32) for i in range(2)]
                b = c.snap()
                NDV4 = 3

                @block.sync
                def _(sy, b=b):
                    for u in range(NT):
                        xp = "ldA" if u % 2 == 0 else "ldB"
                        ap_ = "ldC" if u % 2 == 0 else "ldD"
                        y0p = "stC" if u % 2 == 0 else "stD"
                        y1p = "ldE" if u % 2 == 0 else "ldF"
                        if u >= 2:
                            sy.wait_ge(sem["dve"], b["dve"] + NDV4 * (u - 1))
                        c.inc(sy.dma_start(out=xt[u % 2][:], in_=x_src[u * 128:(u + 1) * 128, :]), xp, 16)
                        c.inc(sy.dma_start(out=at[u % 2][:], in_=acc_src[u * 128:(u + 1) * 128, :]), ap_, 16)
                        c.inc(sy.dma_start(out=y0t[u % 2][:], in_=ysc_src[u * 128:(u + 1) * 128, :]), y0p, 16)
                        c.inc(sy.dma_start(out=y1t[u % 2][:], in_=ysc_src[T + u * 128:T + (u + 1) * 128, :]), y1p, 16)
                        if u >= 1:
                            stp = "stA" if (u - 1) % 2 == 0 else "stB"
                            sy.wait_ge(sem["dve"], b["dve"] + NDV4 * u)
                            c.inc(sy.dma_start(out=dst_ap[(u - 1) * 128:u * 128, :], in_=so[(u - 1) % 2][:]), stp, 16)
                    stp = "stA" if (NT - 1) % 2 == 0 else "stB"
                    sy.wait_ge(sem["dve"], b["dve"] + NDV4 * NT)
                    c.inc(sy.dma_start(out=dst_ap[(NT - 1) * 128:NT * 128, :], in_=so[(NT - 1) % 2][:]), stp, 16)
                    c.wait(sy, "stA")
                    c.wait(sy, "stB")

                @block.vector
                def _(v, b=b):
                    for u in range(NT):
                        xp = "ldA" if u % 2 == 0 else "ldB"
                        ap_ = "ldC" if u % 2 == 0 else "ldD"
                        y0p = "stC" if u % 2 == 0 else "stD"
                        y1p = "ldE" if u % 2 == 0 else "ldF"
                        for n in (xp, ap_, y0p, y1p):
                            v.wait_ge(sem[n], b[n] + 16 * (u // 2 + 1))
                        if u >= 2:
                            stp = "stA" if u % 2 == 0 else "stB"
                            v.wait_ge(sem[stp], b[stp] + 16 * (u // 2))
                        c.inc(v.tensor_tensor(so[u % 2][:], xt[u % 2][:], at[u % 2][:], ALU.add), "dve")
                        c.inc(v.tensor_tensor(so[u % 2][:], so[u % 2][:], y0t[u % 2][:], ALU.add), "dve")
                        c.inc(v.tensor_tensor(so[u % 2][:], so[u % 2][:], y1t[u % 2][:], ALU.add), "dve")
                barrier()

        done = [False]
        def chk(name):
            if stop_after is not None and name == stop_after:
                done[0] = True
            return done[0]

        emit_ln(x0, xn_d)
        if not chk("ln0"):
            with ExitStack() as l0c:
                qT_sb = usb(l0c, "qT_sb", [128, 4, T], BF16)
                with ExitStack() as l0i:
                    xnT_sb = usb(l0i, "xnT0", [128, ND, T], BF16)
                    emit_xnT(xn_d, xnT_sb)
                    if not chk("xnT0"):
                        emit_l0_attn(xnT_sb)
                    if not chk("attn0"):
                        emit_outproj(WTo_d, True, acc_d)
                    if not chk("proj0"):
                        emit_l0_swiglu(xnT_sb)
                    if not chk("swiglu0"):
                        emit_l0_peer_a(xnT_sb, qT_sb)
                    chk("peera")
                if not done[0]:
                    emit_l0_peer_b(qT_sb)
                chk("peerb")
            if not done[0]:
                emit_residual(x0, acc_d, x1_d)
            chk("res0")

        if not done[0]:
            emit_ln(x1_d, xn_d)
        if not chk("ln1"):
            with ExitStack() as l1c:
                g_sb = usb(l1c, "g_sb", [128, NT, 8], F32)
                s_sb = usb(l1c, "s_sb", [128, NT, 8], F32)
                with ExitStack() as l1i:
                    xnT_sb = usb(l1i, "xnT1", [128, ND, T], BF16)
                    emit_xnT(xn_d, xnT_sb)
                    if not chk("xnT1"):
                        emit_l1_attn_a(xnT_sb)
                    if not chk("attn1a"):
                        emit_l1_attn_b()
                    if not chk("attn1b"):
                        emit_outproj(Wout1, False, acc_d)
                    if not chk("proj1"):
                        emit_l1_router(xnT_sb, g_sb, s_sb)
                    if not chk("router1") and not sparse:
                        emit_l1_moe(xnT_sb, g_sb)
                if not done[0] and sparse:
                    emit_l1_moe_sparse(g_sb, s_sb)
                chk("moe1")
            if not done[0]:
                if sparse:
                    emit_residual4(x1_d, acc_d, ysc_d, out)
                else:
                    emit_residual(x1_d, acc_d, out)

        _finish(nc, block, c, dbg, dbg_map, dbg_out)

    return nc


def _finish(nc, block, c, dbg, dbg_map, dbg_out):
    sem = c.sem
    if dbg is not None:
        src_ap = dbg_map[dbg]
        @block.gpsimd
        def _(g):
            c.wait(g, "st")
            c.inc(g.dma_start(out=dbg_out.rearrange("(a p) b -> p a b", p=128),
                              in_=src_ap.rearrange("(a p) b -> p a b", p=128)), "g", 16)

    @block.sync
    def _(sy):
        c.wait(sy, "st")
        c.wait(sy, "ld")

    @block.gpsimd
    def _(g):
        c.wait(g, "g")
        c.wait(g, "st")


def kernel(**inputs):
    x_full = np.ascontiguousarray(inputs["L0_attn_x"])
    weights = {k: np.ascontiguousarray(v) for k, v in inputs.items() if k != "L0_attn_x"}
    nc = build_program()
    in_maps = []
    for core in range(8):
        m = {"x": np.ascontiguousarray(x_full[core])}
        m.update(weights)
        in_maps.append(m)
    res = run_bass_kernel_spmd(nc, in_maps, core_ids=list(range(8)))
    return np.stack([r["out"] for r in res.results], axis=0)
